# version 27
# baseline (speedup 1.0000x reference)
import os
from contextlib import ExitStack
import numpy as np

import concourse.bass as bass
import concourse.mybir as mybir
from concourse.bass_utils import run_bass_kernel_spmd

F32 = mybir.dt.float32
BF16 = mybir.dt.bfloat16
AF = mybir.ActivationFunctionType
ALU = mybir.AluOpType
AX = mybir.AxisListType

B, S, D = 4, 4096, 1024
INNER = 2048
KCONV = 4
NHEAD = 4
DH = 512
L = 4
EXP, FF = 8, 2048
NCORES = 8
SHALF = S // 2

TBA = 256
NTBA = S // TBA
TBC = 512
NTBC = S // TBC
TBM = 256

NEG_HALF_LOG_DH = -0.5 * float(np.log(DH))

L_EFF = L
MOE_ON = True
LAT_ON = True
DBG_B = False
POOL_STATS = True

ENGS = ("sync", "gpsimd", "scalar", "vector", "tensor")
QDEPTH = 6
SAME_ENGINE_SYNC = True
PREFETCH_PRO = True
MASK_POOL = True


class _Step:
    __slots__ = ("idx", "eng", "fn", "dma", "deps", "sem", "inc", "cum",
                 "small")

    def __init__(self, idx, eng, fn, dma):
        self.idx, self.eng, self.fn, self.dma = idx, eng, fn, dma
        self.small = False
        self.deps = set()
        self.sem = None
        self.inc = 0
        self.cum = 0


class Em:

    def __init__(self, nc):
        self.nc = nc
        self.sems = {}
        self.cnt = {}
        self.steps = []
        self.fstart = 0
        self.lastw = {}
        self.readers = {}
        self.hw = {e: {} for e in ENGS}
        self.qidx = {}
        self.last_event = {}
        self.nflush = 0

    def _sem(self, name):
        if name not in self.sems:
            h = self.nc.ctx.enter_context(
                self.nc.semaphore(name.replace(":", "_")))
            self.sems[name] = h
            self.cnt[name] = 0
        return self.sems[name]

    def step(self, eng, fn, reads=(), writes=(), dma=False, small=False):
        i = len(self.steps)
        s = _Step(i, eng, fn, dma)
        s.small = small
        for t in reads:
            w = self.lastw.get(t)
            if w is not None:
                s.deps.add(w)
        for t in writes:
            w = self.lastw.get(t)
            if w is not None:
                s.deps.add(w)
            for r in self.readers.get(t, ()):
                s.deps.add(r)
        for t in reads:
            self.readers.setdefault(t, []).append(i)
        for t in writes:
            self.lastw[t] = i
            self.readers[t] = []
        if dma:
            q = self.qidx.get(eng, 0)
            self.qidx[eng] = q + 1
            name = f"q:{eng}:{q % QDEPTH}"
            self._sem(name)
            s.sem = name
            s.inc = 16
            self.cnt[name] += 16
            s.cum = self.cnt[name]
        self.steps.append(s)
        return i

    def flush(self):
        new = self.steps[self.fstart:]
        if not new:
            return
        need = set()
        for s in new:
            for d in s.deps:
                if d < self.fstart:
                    continue
                dd = self.steps[d]
                if dd.dma:
                    continue
                if dd.eng == s.eng and not s.dma and (
                        s.eng == "tensor"
                        or not (SAME_ENGINE_SYNC and (s.small or dd.small))):
                    continue
                need.add(d)
        lastof = {}
        for s in new:
            lastof[s.eng] = s.idx
            if s.dma:
                lastof["q:" + s.eng] = s.idx
        for e, i in lastof.items():
            if not e.startswith("q:") and not self.steps[i].dma:
                need.add(i)
        for d in sorted(need):
            dd = self.steps[d]
            if dd.sem is None:
                name = f"c:{dd.eng}"
                self._sem(name)
                dd.sem = name
                dd.inc = 1
                self.cnt[name] += 1
                dd.cum = self.cnt[name]
        bar = dict(self.last_event)
        engs_here = [e for e in ENGS if any(s.eng == e for s in new)]
        with self.nc.Block(f"fl{self.nflush}") as block:
            for ename in engs_here:
                def section(eng, ename=ename):
                    first = True
                    for s in new:
                        if s.eng != ename:
                            continue
                        waits = {}
                        if first:
                            for (sn, cv) in bar.values():
                                if sn is not None:
                                    waits[sn] = max(waits.get(sn, 0), cv)
                            first = False
                        if s.dma:
                            pdepth = s.cum - 16 * QDEPTH
                            if pdepth > 0:
                                waits[s.sem] = max(waits.get(s.sem, 0), pdepth)
                        for d in s.deps:
                            dd = self.steps[d]
                            if dd.sem is not None:
                                waits[dd.sem] = max(waits.get(dd.sem, 0), dd.cum)
                            elif d < self.fstart:
                                ev = self.last_event.get(dd.eng)
                                if ev is not None and ev[0] is not None:
                                    waits[ev[0]] = max(waits.get(ev[0], 0), ev[1])
                        hww = self.hw[ename]
                        for sn, v in sorted(waits.items()):
                            if v > hww.get(sn, 0):
                                eng.wait_ge(self.sems[sn], v)
                                hww[sn] = v
                        inst = s.fn(eng)
                        if s.sem is not None:
                            if inst is None:
                                raise RuntimeError("inc step returned no inst")
                            inst.then_inc(self.sems[s.sem], s.inc)
                getattr(block, ename)(section)
        for e, i in lastof.items():
            s = self.steps[i]
            self.last_event[e] = (s.sem, s.cum)
        self.fstart = len(self.steps)
        self.nflush += 1

    def finish(self):
        self.flush()
        bar = dict(self.last_event)
        with self.nc.Block("fin") as block:
            def section(eng):
                for (sn, cv) in sorted(bar.values()):
                    if sn is not None:
                        eng.wait_ge(self.sems[sn], cv)
            block.sync(section)



def _host_prepare(inp):
    f32 = np.float32
    q_w = np.asarray(inp["L0_attn_q_w"], f32)
    k_w = np.asarray(inp["L0_attn_k_w"], f32)
    v_w = np.asarray(inp["L0_attn_v_w"], f32)

    def bd_prod(w):
        t = w.reshape(L, 16, 32, 4, 4)
        out = np.zeros((L, 16, 32, 4, 32, 4), f32)
        for n in range(32):
            out[:, :, n, :, n, :] = t[:, :, n].transpose(0, 1, 3, 2)
        return out.reshape(L, 16, 128, 128)

    def bd_T(w):
        t = w.reshape(L, 16, 32, 4, 4)
        out = np.zeros((L, 16, 32, 4, 32, 4), f32)
        for n in range(32):
            out[:, :, n, :, n, :] = t[:, :, n]
        return out.reshape(L, 16, 128, 128)

    bd = np.stack([bd_prod(q_w), bd_prod(k_w), bd_prod(v_w)], axis=1)
    bd = np.ascontiguousarray(bd.reshape(L, 48, 128, 128))
    bdvT = np.ascontiguousarray(bd_T(v_w))

    ig_w = np.asarray(inp["L0_attn_ig_w"], f32)
    fg_w = np.asarray(inp["L0_attn_fg_w"], f32)
    g8 = np.concatenate([ig_w, fg_w], axis=1)
    gT = np.ascontiguousarray(g8.transpose(0, 2, 1))
    g_qk = np.ascontiguousarray(gT[:, :4096].reshape(L, 32, 128, 8))
    g_v = np.ascontiguousarray(gT[:, 4096:].reshape(L, 16, 128, 8))
    gbias = np.ascontiguousarray(
        np.concatenate([np.asarray(inp["L0_attn_ig_b"], f32),
                        np.asarray(inp["L0_attn_fg_b"], f32)], axis=1).T)

    def cols(v, nchunk):
        a = np.asarray(v, f32).reshape(-1, nchunk, 128)
        return np.ascontiguousarray(a.transpose(2, 0, 1).reshape(128, -1))

    convw = np.asarray(inp["L0_attn_conv_w"], f32)
    convw_c = np.ascontiguousarray(
        convw.reshape(L, 16, 128, KCONV).transpose(2, 0, 1, 3)
        .reshape(128, L * 16 * KCONV))
    convb_c = cols(inp["L0_attn_conv_b"], 16)
    lnw_c = cols(inp["L0_attn_ln_w"], 8)
    skip_c = cols(inp["L0_attn_skip"], 16)
    onorm_c = cols(inp["L0_attn_onorm_w"], 16)
    postw_c = cols(inp["L0_attn_post_w"], 8)
    latb_c = cols(inp["L0_attn_lat_b"], 8)
    b1_c = np.ascontiguousarray(
        np.asarray(inp["L0_ffn_b1"], f32).reshape(EXP, 16, 128)
        .transpose(2, 0, 1).reshape(128, EXP * 16))
    b2_e = np.ascontiguousarray(np.asarray(inp["L0_ffn_b2"], f32))

    up_wT = np.ascontiguousarray(
        np.asarray(inp["L0_attn_up_w"], f32).transpose(0, 2, 1))
    down_wT = np.ascontiguousarray(
        np.asarray(inp["L0_attn_down_w"], f32).transpose(0, 2, 1))
    latT = np.ascontiguousarray(np.asarray(inp["L0_attn_lat_w"], f32).T)
    rT = np.ascontiguousarray(np.asarray(inp["L0_ffn_router_w"], f32).T)
    w1T = np.ascontiguousarray(
        np.asarray(inp["L0_ffn_w1"], f32).transpose(0, 2, 1))
    w2T = np.ascontiguousarray(
        np.asarray(inp["L0_ffn_w2"], f32).transpose(0, 2, 1))

    ident = np.eye(128, dtype=f32)
    masks = np.zeros((4, 128, 512), f32)
    p = np.arange(128)[:, None]
    fi = np.arange(512)[None, :]
    for o in range(4):
        masks[o] = (p <= fi - 128 * o).astype(f32)
    masks = np.ascontiguousarray(masks.transpose(1, 0, 2).reshape(128, 4 * 512))

    shared = dict(
        bd=bd, bdvT=bdvT, g_qk=g_qk, g_v=g_v, gbias=gbias,
        convw=convw_c, convb=convb_c, lnw=lnw_c, skipc=skip_c,
        onormc=onorm_c, postw=postw_c, latb=latb_c, b1c=b1_c, b2e=b2_e,
        up_wT=up_wT, down_wT=down_wT, latT=latT, rT=rT, w1T=w1T, w2T=w2T,
        ident=ident, masks=masks,
    )
    tokens = np.asarray(inp["L0_attn_tokens_batch"], f32)
    in_maps = []
    for core in range(NCORES):
        b = core // 2
        m = dict(shared)
        m["xT"] = np.ascontiguousarray(tokens[b].T)
        in_maps.append(m)
    return in_maps



def build_program(taps=()):
    nc = bass.Bass("TRN2", target_bir_lowering=False)
    em = Em(nc)
    dt = {}

    def din(name, shape):
        dt[name] = nc.dram_tensor(name, list(shape), F32, kind="ExternalInput")

    din("xT", (D, S))
    din("bd", (L, 48, 128, 128))
    din("bdvT", (L, 16, 128, 128))
    din("g_qk", (L, 32, 128, 8))
    din("g_v", (L, 16, 128, 8))
    din("gbias", (8, L))
    din("convw", (128, L * 16 * KCONV))
    din("convb", (128, L * 16))
    din("lnw", (128, L * 8))
    din("skipc", (128, L * 16))
    din("onormc", (128, L * 16))
    din("postw", (128, 8))
    din("latb", (128, 8))
    din("b1c", (128, EXP * 16))
    din("b2e", (EXP, D))
    din("up_wT", (L, D, 2 * INNER))
    din("down_wT", (L, INNER, D))
    din("latT", (D, D))
    din("rT", (D, EXP))
    din("w1T", (EXP, D, FF))
    din("w2T", (EXP, FF, D))
    din("ident", (128, 128))
    din("masks", (128, 4 * 512))

    o_out = nc.dram_tensor("o", [D, SHALF], F32, kind="ExternalOutput")
    o2_out = nc.dram_tensor("o2", [D, SHALF], F32, kind="ExternalOutput")

    def scr(name, shape, dtype):
        dt[name] = nc.dram_tensor(name, list(shape), dtype, kind="Internal")

    scr("sp_x", (D, S), F32)
    scr("sp_x1", (D, S), F32)
    scr("sp_xn", (D, S), BF16)
    scr("sp_z", (INNER, S), BF16)
    scr("sp_xc", (INNER, S), BF16)
    scr("sp_q", (INNER, S), BF16)
    scr("sp_k", (INNER, S), BF16)
    scr("sp_v", (S, INNER), BF16)
    scr("sp_hnT", (INNER, S), BF16)
    scr("sp_g8", (8, S), F32)
    scr("sp_rows", (12, S), F32)
    scr("sp_dbg", (128, 4096), F32)
    scr("sp_dbg2", (128, 8192), F32)
    scr("sp_cmb", (EXP, S), F32)
    scr("sp_acc", (D, S), F32)

    tap_t = {}
    for name in taps:
        h = dt[name]
        tap_t[name] = nc.dram_tensor(
            "tap_" + name, list(h.shape), h.dtype, kind="ExternalOutput")

    ctx = nc.ctx
    identb = ctx.enter_context(nc.sbuf_tensor("identb_s", [128, 128], BF16))
    identf = ctx.enter_context(nc.sbuf_tensor("identf_s", [128, 128], F32))
    masks_sb = ctx.enter_context(nc.sbuf_tensor("masks_s", [128, 4 * 512], BF16))
    ones_r = ctx.enter_context(nc.sbuf_tensor("ones_r", [1, 128], F32))
    ones_c = ctx.enter_context(nc.sbuf_tensor("ones_c", [128, 1], BF16))
    constv = ctx.enter_context(nc.sbuf_tensor("constv", [128, 4], F32))
    lnw_sb = ctx.enter_context(nc.sbuf_tensor("lnw_s", [128, L * 8], F32))
    convw_sb = ctx.enter_context(nc.sbuf_tensor("convw_s", [128, L * 64], F32))
    convb_sb = ctx.enter_context(nc.sbuf_tensor("convb_s", [128, L * 16], F32))
    skip_sb = ctx.enter_context(nc.sbuf_tensor("skipc_s", [128, L * 16], F32))
    onorm_sb = ctx.enter_context(nc.sbuf_tensor("onormc_s", [128, L * 16], F32))
    postw_sb = ctx.enter_context(nc.sbuf_tensor("postw_s", [128, 8], F32))
    latb_sb = ctx.enter_context(nc.sbuf_tensor("latb_s", [128, 8], F32))
    gbias_sb = ctx.enter_context(nc.sbuf_tensor("gbias_s", [8, L], F32))
    b1_sb = ctx.enter_context(nc.sbuf_tensor("b1c_s", [128, EXP * 16], F32))
    b2_sb = ctx.enter_context(nc.sbuf_tensor("b2e_s", [EXP, D], BF16))

    def dma(eng, out_ap, in_ap, reads=(), writes=()):
        return em.step(eng, lambda e, o=out_ap, i=in_ap: e.dma_start(o, i),
                       reads=reads, writes=writes, dma=True)

    dma("gpsimd", identb[:, :], dt["ident"].ap(), writes=["identb"])
    dma("sync", identf[:, :], dt["ident"].ap(), writes=["identf"])
    dma("gpsimd", masks_sb[:, :], dt["masks"].ap(), writes=["masks"])
    dma("sync", lnw_sb[:, :], dt["lnw"].ap(), writes=["lnw"])
    dma("sync", convw_sb[:, :], dt["convw"].ap(), writes=["convw"])
    dma("sync", convb_sb[:, :], dt["convb"].ap(), writes=["convb"])
    dma("sync", skip_sb[:, :], dt["skipc"].ap(), writes=["skipc"])
    dma("sync", onorm_sb[:, :], dt["onormc"].ap(), writes=["onormc"])
    dma("sync", postw_sb[:, :], dt["postw"].ap(), writes=["postw"])
    dma("sync", latb_sb[:, :], dt["latb"].ap(), writes=["latb"])
    dma("sync", gbias_sb[:, :], dt["gbias"].ap(), writes=["gbias"])
    dma("sync", b1_sb[:, :], dt["b1c"].ap(), writes=["b1c"])
    dma("gpsimd", b2_sb[:, :], dt["b2e"].ap(), writes=["b2e"])
    em.step("vector", lambda e: e.memset(ones_r[:, :], 1.0), writes=["ones_r"])
    em.step("vector", lambda e: e.memset(ones_c[:, :], 1.0), writes=["ones_c"])
    em.step("vector", lambda e: e.memset(constv[:, 0:1], 1e-5), writes=["constv"])
    em.step("vector", lambda e: e.memset(constv[:, 1:2], 1.0), writes=["constv"])
    em.step("vector", lambda e: e.memset(constv[:, 2:3], NEG_HALF_LOG_DH),
            writes=["constv"])

    def ln_block_q(pref, xtok, xblk, nch, w, out, otok, scale_col_fn, lb):
        pieces = []
        for c in range(nch):
            pieces.append(lambda c=c: _ln_cast(
                pref, xtok, xblk, nch, w, c, lb))
        pieces.append(lambda: _ln_stats_a(pref, w, lb))
        pieces.append(None)
        pieces.append(lambda: _ln_stats_b(pref, w, lb))
        for c in range(nch):
            pieces.append(lambda c=c: _ln_norm(
                pref, xtok, xblk, nch, w, out, otok, scale_col_fn, c, lb))
        return pieces

    def ln_block(pref, xtok, xblk, nch, w, out, otok, scale_col_fn, lb):
        for p in ln_block_q(pref, xtok, xblk, nch, w, out, otok,
                            scale_col_fn, lb):
            if p is not None:
                p()

    def _ln_cast(pref, xtok, xblk, nch, w, c, lb):
        xb16, sqb = lb["xb16"], lb["sqb"]
        pLN = lb["pLN"]
        ptok = pref + "pLN"
        pool_stats = lb.get("pool_stats", False)
        if True:
            cb = c % 2
            ceng = "gpsimd" if pool_stats else "vector"
            em.step(ceng, lambda e, c=c, cb=cb: e.tensor_copy(
                xb16[:, cb * w:(cb + 1) * w], xblk[:, c * w:(c + 1) * w]),
                reads=[xtok], writes=[pref + "xb16%d" % cb])
            if pool_stats:
                em.step("gpsimd", lambda e, c=c, cb=cb: e.tensor_tensor(
                    sqb[:, cb * w:(cb + 1) * w], xblk[:, c * w:(c + 1) * w],
                    xblk[:, c * w:(c + 1) * w], ALU.mult),
                    reads=[xtok], writes=[pref + "sqb%d" % cb])
            else:
                em.step("scalar", lambda e, c=c, cb=cb: e.activation(
                    sqb[:, cb * w:(cb + 1) * w],
                    xblk[:, c * w:(c + 1) * w], AF.Square),
                    reads=[xtok], writes=[pref + "sqb%d" % cb])

            def f_s(e, c=c, cb=cb):
                i = e.matmul(pLN[0:1, 0:w], ones_c[:, :],
                             xb16[:, cb * w:(cb + 1) * w],
                             start=(c == 0), stop=(c == nch - 1))
                i = e.matmul(pLN[0:1, w:2 * w], ones_c[:, :],
                             sqb[:, cb * w:(cb + 1) * w],
                             start=(c == 0), stop=(c == nch - 1))
                return i
            em.step("tensor", f_s,
                    reads=[pref + "xb16%d" % cb, pref + "sqb%d" % cb,
                           "ones_c"],
                    writes=[ptok])

    def _ln_stats_a(pref, w, lb):
        pLN = lb["pLN"]
        rowmu, rowrs, rowt = lb["rowmu"], lb["rowrs"], lb["rowt"]
        ptok = pref + "pLN"
        em.step("vector", lambda e: e.tensor_scalar(
            rowmu[:, :], pLN[0:1, 0:w], 1.0 / D, None, ALU.mult),
            reads=[ptok], writes=[pref + "rowmu"], small=True)
        em.step("vector", lambda e: e.tensor_scalar(
            rowt[:, :], pLN[0:1, w:2 * w], 1.0 / D, None, ALU.mult),
            reads=[ptok], writes=[pref + "rowt"], small=True)
        em.step("vector", lambda e: e.tensor_tensor(
            rowrs[:, :], rowmu[:, :], rowmu[:, :], ALU.mult),
            reads=[pref + "rowmu"], writes=[pref + "rowrs"], small=True)
        em.step("vector", lambda e: e.tensor_tensor(
            rowt[:, :], rowt[:, :], rowrs[:, :], ALU.subtract),
            reads=[pref + "rowt", pref + "rowrs"], writes=[pref + "rowt"], small=True)
        em.step("scalar", lambda e: e.activation(
            rowrs[:, :], rowt[:, :], AF.Sqrt, bias=constv[0:1, 0:1]),
            reads=[pref + "rowt", "constv"], writes=[pref + "rowrs"], small=True)
        em.step("vector", lambda e: e.reciprocal(rowrs[:, :], rowrs[:, :]),
                reads=[pref + "rowrs"], writes=[pref + "rowrs"], small=True)

    def _ln_stats_b(pref, w, lb):
        pB = lb["pB"]
        rowmu, rowrs = lb["rowmu"], lb["rowrs"]
        muB, rstdB = lb["muB"], lb["rstdB"]
        pbtok = lb.get("pbtok", pref + "pB")
        em.step("tensor", lambda e: e.matmul(
            pB[:, 0:w], ones_r[0:1, :], rowmu[:, :], start=True, stop=True),
            reads=[pref + "rowmu", "ones_r"], writes=[pbtok], small=True)
        em.step("scalar", lambda e: e.activation(
            muB[:, :], pB[:, 0:w], AF.Copy), reads=[pbtok],
            writes=[pref + "muB"])
        em.step("tensor", lambda e: e.matmul(
            pB[:, 0:w], ones_r[0:1, :], rowrs[:, :], start=True, stop=True),
            reads=[pref + "rowrs", "ones_r", pref + "muB"], writes=[pbtok], small=True)
        em.step("scalar", lambda e: e.activation(
            rstdB[:, :], pB[:, 0:w], AF.Copy), reads=[pbtok],
            writes=[pref + "rstdB"])

    def _ln_norm(pref, xtok, xblk, nch, w, out, otok, scale_col_fn, c, lb):
        rstdB, muB, t1 = lb["rstdB"], lb["muB"], lb["t1"]
        em.step("vector", lambda e, c=c: e.tensor_tensor(
            t1[:, :], xblk[:, c * w:(c + 1) * w], muB[:, :],
            ALU.subtract), reads=[xtok, pref + "muB"],
            writes=[pref + "t1"])
        sc = scale_col_fn(c) if scale_col_fn else None
        if sc is not None:
            em.step("vector", lambda e, c=c, sc=sc: e.scalar_tensor_tensor(
                out[:, c * w:(c + 1) * w], t1[:, :], sc, rstdB[:, :],
                ALU.mult, ALU.mult),
                reads=[pref + "t1", pref + "rstdB"], writes=[otok])
        else:
            em.step("vector", lambda e, c=c: e.tensor_tensor(
                out[:, c * w:(c + 1) * w], t1[:, :], rstdB[:, :],
                ALU.mult), reads=[pref + "t1", pref + "rstdB"],
                writes=[otok])

    spx_r = dt["sp_x"].ap().rearrange("(c p) t -> p c t", p=128)
    spx1_r = dt["sp_x1"].ap().rearrange("(c p) t -> p c t", p=128)
    spxn_r = dt["sp_xn"].ap().rearrange("(c p) t -> p c t", p=128)
    spz_r = dt["sp_z"].ap().rearrange("(c p) t -> p c t", p=128)
    spxc_r = dt["sp_xc"].ap().rearrange("(c p) t -> p c t", p=128)
    spq_r = dt["sp_q"].ap().rearrange("(c p) t -> p c t", p=128)
    spk_r = dt["sp_k"].ap().rearrange("(c p) t -> p c t", p=128)
    spv_r = dt["sp_v"].ap().rearrange("(s p) c -> p s c", p=128)
    sphnT_r = dt["sp_hnT"].ap().rearrange("(c p) t -> p c t", p=128)
    xT_r = dt["xT"].ap().rearrange("(c p) t -> p c t", p=128)
    o_r = o_out.ap().rearrange("(c p) t -> p c t", p=128)
    o2_r = o2_out.ap().rearrange("(c p) t -> p c t", p=128)

    with ExitStack() as _st:
        xblk = _st.enter_context(nc.sbuf_tensor("pp_x", [128, 2 * 8 * TBC], F32))
        oblk = _st.enter_context(nc.sbuf_tensor("pp_o", [128, 2 * 8 * TBC], F32))
        xb16 = _st.enter_context(nc.sbuf_tensor("pp_b16", [128, 2 * TBC], BF16))
        sqb = _st.enter_context(nc.sbuf_tensor("pp_sq", [128, 2 * TBC], BF16))
        rowmu = _st.enter_context(nc.sbuf_tensor("pp_rmu", [1, TBC], F32))
        rowrs = _st.enter_context(nc.sbuf_tensor("pp_rrs", [1, TBC], F32))
        rowt = _st.enter_context(nc.sbuf_tensor("pp_rt", [1, TBC], F32))
        muB = _st.enter_context(nc.sbuf_tensor("pp_muB", [128, TBC], F32))
        rstdB = _st.enter_context(nc.sbuf_tensor("pp_rsB", [128, TBC], F32))
        t1 = _st.enter_context(nc.sbuf_tensor("pp_t1", [128, TBC], F32))
        pLN = _st.enter_context(nc.psum_tensor("pp_pLN", [1, 2 * TBC], F32))
        pB = _st.enter_context(nc.psum_tensor("pp_pB", [128, TBC], F32))
        lb = dict(xb16=xb16, sqb=sqb, pLN=pLN, rowmu=rowmu, rowrs=rowrs,
                  rowt=rowt, pB=pB, muB=muB, rstdB=rstdB, t1=t1,
                  pool_stats=POOL_STATS)

        def pp_load(tb):
            par = tb % 2
            t0 = tb * TBC
            dma("scalar",
                xblk.ap().rearrange("p (b c t) -> p b c t", b=2, t=TBC)
                [:, par], xT_r[:, :, t0:t0 + TBC],
                writes=["pp_x%d" % par])
        pp_load(0)
        for tb in range(NTBC):
            t0 = tb * TBC
            par = tb % 2
            if tb + 1 < NTBC:
                pp_load(tb + 1)
            xv = xblk[:, par * 8 * TBC:(par + 1) * 8 * TBC]
            ov = oblk[:, par * 8 * TBC:(par + 1) * 8 * TBC]
            ln_block("pp", "pp_x%d" % par, xv, 8, TBC, ov,
                     "pp_o%d" % par, None, lb)
            dma("sync", spx_r[:, :, t0:t0 + TBC],
                ov.rearrange("p (c t) -> p c t", t=TBC),
                reads=["pp_o%d" % par], writes=["d:sp_x"])
        em.flush()

    for l in range(L_EFF):
        with ExitStack() as _st:
            uw1 = _st.enter_context(nc.sbuf_tensor(f"a_uw1{l}", [128, 8 * INNER], BF16))
            uw2 = _st.enter_context(nc.sbuf_tensor(f"a_uw2{l}", [128, 8 * INNER], BF16))
            bd_sb = _st.enter_context(nc.sbuf_tensor(f"a_bd{l}", [128, 48 * 128], BF16))
            gqk_sb = _st.enter_context(nc.sbuf_tensor(f"a_gqk{l}", [128, 32 * 8], BF16))
            bdvT_sb = _st.enter_context(nc.sbuf_tensor(f"a_bdvT{l}", [128, 16 * 128], BF16))
            gvr_sb = _st.enter_context(nc.sbuf_tensor(f"a_gvr{l}", [128, 16 * 8], BF16))
            gtv_sb = _st.enter_context(nc.sbuf_tensor(f"a_gtv{l}", [128, 16 * 8], BF16))
            xblk = _st.enter_context(nc.sbuf_tensor(f"a_x{l}", [128, 2 * 8 * TBA], F32))
            xnb = _st.enter_context(nc.sbuf_tensor(f"a_xnb{l}", [128, 2 * 8 * TBA], BF16))
            xb16 = _st.enter_context(nc.sbuf_tensor(f"a_b16{l}", [128, 2 * TBA], BF16))
            sqb = _st.enter_context(nc.sbuf_tensor(f"a_sq{l}", [128, 2 * TBA], BF16))
            rowmu = _st.enter_context(nc.sbuf_tensor(f"a_rmu{l}", [1, TBA], F32))
            rowrs = _st.enter_context(nc.sbuf_tensor(f"a_rrs{l}", [1, TBA], F32))
            rowt = _st.enter_context(nc.sbuf_tensor(f"a_rt{l}", [1, TBA], F32))
            muB = _st.enter_context(nc.sbuf_tensor(f"a_muB{l}", [128, TBA], F32))
            rstdB = _st.enter_context(nc.sbuf_tensor(f"a_rsB{l}", [128, TBA], F32))
            t1 = _st.enter_context(nc.sbuf_tensor(f"a_t1{l}", [128, TBA], F32))
            xmb = _st.enter_context(nc.sbuf_tensor(f"a_xm{l}", [128, 16 * (TBA + 3)], BF16))
            xcb = _st.enter_context(nc.sbuf_tensor(f"a_xc{l}", [128, 16 * TBA], BF16))
            cv1 = _st.enter_context(nc.sbuf_tensor(f"a_cv1{l}", [128, 2 * TBA], F32))
            cv2 = _st.enter_context(nc.sbuf_tensor(f"a_cv2{l}", [128, 2 * TBA], F32))
            qb = _st.enter_context(nc.sbuf_tensor(f"a_q{l}", [128, 16 * TBA], BF16))
            kb = _st.enter_context(nc.sbuf_tensor(f"a_k{l}", [128, 16 * TBA], BF16))
            vb = _st.enter_context(nc.sbuf_tensor(f"a_v{l}", [128, 2 * INNER], BF16))
            gates_a = _st.enter_context(nc.sbuf_tensor(f"a_g{l}", [8, TBA], F32))
            zb = _st.enter_context(nc.sbuf_tensor(f"a_zb{l}", [128, 8 * TBC], BF16))
            xn2 = _st.enter_context(nc.sbuf_tensor(f"a_xn2{l}", [128, 8 * TBC], BF16))
            pLN = _st.enter_context(nc.psum_tensor(f"a_pLN{l}", [1, 2 * TBA], F32))
            pB = _st.enter_context(nc.psum_tensor(f"a_pB{l}", [128, TBA], F32))
            pU0 = _st.enter_context(nc.psum_tensor(f"a_pU0{l}", [128, 2 * TBA], F32))
            pU1 = _st.enter_context(nc.psum_tensor(f"a_pU1{l}", [128, 2 * TBA], F32))
            pU2 = _st.enter_context(nc.psum_tensor(f"a_pU2{l}", [128, 2 * TBA], F32))
            pUs = [pU0, pU1, pU2]
            pUts = [f"a_pU0", f"a_pU1", f"a_pU2"]
            pVa = _st.enter_context(nc.psum_tensor(f"a_pVa{l}", [128, 512], F32))
            pVb = _st.enter_context(nc.psum_tensor(f"a_pVb{l}", [128, 512], F32))
            pV2 = _st.enter_context(nc.psum_tensor(f"a_pV2{l}", [128, 256], F32))
            lb = dict(xb16=xb16, sqb=sqb, pLN=pLN, rowmu=rowmu, rowrs=rowrs,
                      rowt=rowt, pB=pB, muB=muB, rstdB=rstdB, t1=t1,
                      pool_stats=POOL_STATS)
            dma("gpsimd", bdvT_sb[:, :].rearrange("p (j m) -> p j m", m=128),
                dt["bdvT"].ap()[l].rearrange("j p m -> p j m"),
                writes=["bdvT"])
            dma("gpsimd", gvr_sb[:, :].rearrange("p (j m) -> p j m", m=8),
                dt["g_v"].ap()[l].rearrange("j p m -> p j m"), writes=["gvr"])
            dma("gpsimd", gqk_sb[:, :].rearrange("p (j m) -> p j m", m=8),
                dt["g_qk"].ap()[l].rearrange("j p m -> p j m"),
                writes=["gqk"])
            for cq in range(4):
                dma("gpsimd",
                    uw1[:, :].rearrange("p (k m) -> p k m", m=INNER)
                    [:, :, cq * 512:(cq + 1) * 512],
                    dt["up_wT"].ap()[l].rearrange("(k p) m -> p k m", p=128)
                    [:, :, cq * 512:(cq + 1) * 512],
                    writes=["uw1c%d" % cq])
                if cq == 0:
                    dma("gpsimd",
                        bd_sb[:, :].rearrange("p (j m) -> p j m", m=128),
                        dt["bd"].ap()[l].rearrange("j p m -> p j m"),
                        writes=["bd"])
            dma("gpsimd", uw2[:, :].rearrange("p (k m) -> p k m", m=INNER),
                dt["up_wT"].ap()[l].rearrange("(k p) m -> p k m", p=128)
                [:, :, INNER:2 * INNER], writes=["uw2"])
            for c in range(16):
                pX = pVa if c % 2 == 0 else pVb
                pxt = "a_pVa" if c % 2 == 0 else "a_pVb"
                em.step("tensor", lambda e, c=c, pX=pX: e.matmul(
                    pX[:, 0:8], bdvT_sb[:, c * 128:(c + 1) * 128],
                    gvr_sb[:, c * 8:(c + 1) * 8], start=True, stop=True),
                    reads=["bdvT", "gvr", "gtv"], writes=[pxt])
                em.step("scalar", lambda e, c=c, pX=pX: e.activation(
                    gtv_sb[:, c * 8:(c + 1) * 8], pX[:, 0:8], AF.Copy),
                    reads=[pxt], writes=["gtv"])

            TW = TBA + 3

            def a_ln_q(tb):
                par = tb % 2
                t0 = tb * TBA
                xv = xblk[:, par * 8 * TBA:(par + 1) * 8 * TBA]
                xnv = xnb[:, par * 8 * TBA:(par + 1) * 8 * TBA]

                def p_load(par=par, t0=t0, xv=xv):
                    dma("sync", xv.rearrange("p (c t) -> p c t", t=TBA),
                        spx_r[:, :, t0:t0 + TBA], reads=["d:sp_x"],
                        writes=["a_x%d" % par])

                def p_store(par=par, t0=t0, xnv=xnv):
                    dma("sync", spxn_r[:, :, t0:t0 + TBA],
                        xnv.rearrange("p (c t) -> p c t", t=TBA),
                        reads=["a_xn%d" % par], writes=["d:sp_xn"])
                return ([p_load] +
                        ln_block_q("a", "a_x%d" % par, xv, 8, TBA, xnv,
                                   "a_xn%d" % par,
                                   (lambda c: lnw_sb[:,
                                              l * 8 + c:l * 8 + c + 1]),
                                   lb) +
                        [p_store])

            def a_ln(tb):
                for p in a_ln_q(tb):
                    if p is not None:
                        p()
            a_ln(0)
            for tb in range(NTBA):
                t0 = tb * TBA
                par = tb % 2
                xnv = xnb[:, par * 8 * TBA:(par + 1) * 8 * TBA]
                lnq = a_ln_q(tb + 1) if tb + 1 < NTBA else []
                for c in range(16):
                    if tb == 0:
                        em.step("vector", lambda e, c=c: e.memset(
                            xmb[:, c * TW:c * TW + 3], 0.0),
                            reads=["a_xm%d" % c], writes=["a_xm%d" % c])
                    else:
                        em.step("vector", lambda e, c=c: e.tensor_copy(
                            xmb[:, c * TW:c * TW + 3],
                            xmb[:, c * TW + TBA:c * TW + TBA + 3]),
                            reads=["a_xm%d" % c], writes=["a_xm%d" % c])
                def qkv_group(mq):
                    pV = pVa if mq % 2 == 0 else pVb
                    pvt = "a_pVa" if mq % 2 == 0 else "a_pVb"
                    vp = mq % 2

                    def f_qkm(e, m=mq, pV=pV):
                        e.matmul(pV[:, 0:TBA],
                                 bd_sb[:, m * 128:(m + 1) * 128],
                                 xcb[:, m * TBA:(m + 1) * TBA],
                                 start=True, stop=True)
                        i = e.matmul(pV[:, TBA:2 * TBA],
                                     bd_sb[:, (16 + m) * 128:
                                           (17 + m) * 128],
                                     xcb[:, m * TBA:(m + 1) * TBA],
                                     start=True, stop=True)
                        return i
                    em.step("tensor", f_qkm, reads=["bd", "a_xc%d" % mq],
                            writes=[pvt])
                    em.step("scalar", lambda e, m=mq, pV=pV: e.activation(
                        qb[:, m * TBA:(m + 1) * TBA], pV[:, 0:TBA],
                        AF.Copy), reads=[pvt], writes=["a_q"])
                    em.step("scalar", lambda e, m=mq, pV=pV: e.activation(
                        kb[:, m * TBA:(m + 1) * TBA], pV[:, TBA:2 * TBA],
                        AF.Copy), reads=[pvt], writes=["a_k"])

                    def f_vm(e, m=mq):
                        i = None
                        for sub in range(2):
                            i = e.matmul(
                                pV2[:, sub * 128:(sub + 1) * 128],
                                xmb[:, m * TW + 3 + sub * 128:
                                    m * TW + 3 + sub * 128 + 128],
                                bd_sb[:, (32 + m) * 128:(33 + m) * 128],
                                start=True, stop=True)
                        return i
                    em.step("tensor", f_vm, reads=["a_xm%d" % mq, "bd"],
                            writes=["a_pV2"])
                    em.step("scalar", lambda e, m=mq: e.activation(
                        vb.ap().rearrange("p (i n) -> p i n", n=INNER)
                        [:, :, m * 128:(m + 1) * 128],
                        pV2[:, :].rearrange("p (i s) -> p i s", s=128),
                        AF.Copy), reads=["a_pV2"], writes=["a_v"])

                for m in range(16):
                    pU = pUs[m % 3]
                    ptok = pUts[m % 3]

                    def f_up(e, m=m, pU=pU, xnv=xnv):
                        i = None
                        for k in range(8):
                            i = e.matmul(pU[:, 0:TBA],
                                         uw1[:, k * INNER + m * 128:
                                             k * INNER + m * 128 + 128],
                                         xnv[:, k * TBA:(k + 1) * TBA],
                                         start=(k == 0), stop=(k == 7))
                        return i
                    em.step("tensor", f_up,
                            reads=["uw1c%d" % (m // 4), "a_xn%d" % par],
                            writes=[ptok])
                    em.step("scalar", lambda e, m=m, pU=pU: e.activation(
                        xmb[:, m * TW + 3:m * TW + 3 + TBA], pU[:, 0:TBA],
                        AF.Copy), reads=[ptok],
                        writes=["a_xm%d" % m, "a_xm"])
                    if m >= 2:
                        qkv_group(m - 2)
                    base = m * TW
                    wl0 = (l * 16 + m) * KCONV
                    cp = m % 2
                    co = cp * TBA
                    em.step("vector", lambda e, base=base, wl0=wl0, co=co:
                            e.tensor_scalar(
                                cv1[:, co:co + TBA], xmb[:, base:base + TBA],
                                convw_sb[:, wl0:wl0 + 1], None, ALU.mult),
                            reads=["a_xm%d" % m, "convw"],
                            writes=["a_cv1%d" % cp])
                    em.step("vector", lambda e, base=base, wl0=wl0, co=co:
                            e.scalar_tensor_tensor(
                                cv2[:, co:co + TBA],
                                xmb[:, base + 1:base + 1 + TBA],
                                convw_sb[:, wl0 + 1:wl0 + 2],
                                cv1[:, co:co + TBA], ALU.mult, ALU.add),
                            reads=["a_xm%d" % m, "convw", "a_cv1%d" % cp],
                            writes=["a_cv2%d" % cp])
                    em.step("vector", lambda e, base=base, wl0=wl0, co=co:
                            e.scalar_tensor_tensor(
                                cv1[:, co:co + TBA],
                                xmb[:, base + 2:base + 2 + TBA],
                                convw_sb[:, wl0 + 2:wl0 + 3],
                                cv2[:, co:co + TBA], ALU.mult, ALU.add),
                            reads=["a_xm%d" % m, "convw", "a_cv2%d" % cp],
                            writes=["a_cv1%d" % cp])
                    em.step("vector", lambda e, base=base, wl0=wl0, co=co:
                            e.scalar_tensor_tensor(
                                cv2[:, co:co + TBA],
                                xmb[:, base + 3:base + 3 + TBA],
                                convw_sb[:, wl0 + 3:wl0 + 4],
                                cv1[:, co:co + TBA], ALU.mult, ALU.add),
                            reads=["a_xm%d" % m, "convw", "a_cv1%d" % cp],
                            writes=["a_cv2%d" % cp])
                    em.step("scalar", lambda e, m=m, co=co: e.activation(
                        xcb[:, m * TBA:(m + 1) * TBA], cv2[:, co:co + TBA],
                        AF.Silu,
                        bias=convb_sb[:, l * 16 + m:l * 16 + m + 1]),
                        reads=["a_cv2%d" % cp, "convb"],
                        writes=["a_xc%d" % m, "a_xc"])
                    if lnq and m < 14:
                        p = lnq.pop(0)
                        if p is not None:
                            p()
                for p in lnq:
                    if p is not None:
                        p()
                lnq = []
                qkv_group(14)
                qkv_group(15)

                def f_g(e):
                    i = None
                    for c in range(16):
                        i = e.matmul(pB[0:8, 0:TBA],
                                     gqk_sb[:, c * 8:(c + 1) * 8],
                                     qb[:, c * TBA:(c + 1) * TBA],
                                     start=(c == 0), stop=False)
                    for c in range(16):
                        i = e.matmul(pB[0:8, 0:TBA],
                                     gqk_sb[:, (16 + c) * 8:(17 + c) * 8],
                                     kb[:, c * TBA:(c + 1) * TBA],
                                     start=False, stop=False)
                    for c in range(16):
                        i = e.matmul(pB[0:8, 0:TBA],
                                     gtv_sb[:, c * 8:(c + 1) * 8],
                                     xmb[:, c * TW + 3:c * TW + 3 + TBA],
                                     start=False, stop=(c == 15))
                    return i
                em.step("tensor", f_g,
                        reads=["a_q", "a_k", "a_xm", "gqk", "gtv"] +
                              ["a_xm%d" % c for c in range(16)],
                        writes=["a_pB"])
                em.step("vector", lambda e: e.tensor_scalar(
                    gates_a[:, :], pB[0:8, 0:TBA],
                    gbias_sb[0:8, l:l + 1], None,
                    ALU.add), reads=["a_pB", "gbias"], writes=["a_gates"], small=True)
                dma("sync", dt["sp_g8"].ap()[:, t0:t0 + TBA], gates_a[:, :],
                    reads=["a_gates"], writes=["d:sp_g8"])
                dma("scalar", spxc_r[:, :, t0:t0 + TBA],
                    xcb.ap().rearrange("p (c t) -> p c t", t=TBA),
                    reads=["a_xc"], writes=["d:sp_xc"])
                dma("scalar", spq_r[:, :, t0:t0 + TBA],
                    qb.ap().rearrange("p (c t) -> p c t", t=TBA),
                    reads=["a_q"], writes=["d:sp_q"])
                dma("scalar", spk_r[:, :, t0:t0 + TBA],
                    kb.ap().rearrange("p (c t) -> p c t", t=TBA),
                    reads=["a_k"], writes=["d:sp_k"])
                dma("scalar", spv_r[:, t0 // 128:t0 // 128 + 2, :],
                    vb.ap().rearrange("p (s c) -> p s c", c=INNER),
                    reads=["a_v"], writes=["d:sp_v"])
            for tb in range(NTBC):
                t0 = tb * TBC
                dma("scalar", xn2.ap().rearrange("p (c t) -> p c t", t=TBC),
                    spxn_r[:, :, t0:t0 + TBC], reads=["d:sp_xn"],
                    writes=["a_xn2"])
                for m in range(16):
                    pU = pUs[m % 3]
                    ptok = pUts[m % 3]

                    def f_up2(e, m=m, pU=pU):
                        i = None
                        for k in range(8):
                            i = e.matmul(pU[:, 0:TBC],
                                         uw2[:, k * INNER + m * 128:
                                             k * INNER + m * 128 + 128],
                                         xn2[:, k * TBC:(k + 1) * TBC],
                                         start=(k == 0), stop=(k == 7))
                        return i
                    em.step("tensor", f_up2, reads=["uw2", "a_xn2"],
                            writes=[ptok])
                    em.step("scalar", lambda e, m=m, pU=pU: e.activation(
                        zb[:, (m % 8) * TBC:(m % 8 + 1) * TBC], pU[:, 0:TBC],
                        AF.Copy), reads=[ptok], writes=["a_zb"])
                    if m % 8 == 7:
                        g0 = (m // 8) * 8
                        dma("sync", spz_r[:, g0:g0 + 8, t0:t0 + TBC],
                            zb.ap().rearrange("p (c t) -> p c t", t=TBC),
                            reads=["a_zb"], writes=["d:sp_z"])
            em.flush()

        dwst = ExitStack()
        dwA = dwst.enter_context(
            nc.sbuf_tensor(f"c_dwA{l}", [128, 8 * D], BF16))

        with ExitStack() as _st:
            ig4 = _st.enter_context(nc.sbuf_tensor(f"s_ig{l}", [4, S], F32))
            fg4 = _st.enter_context(nc.sbuf_tensor(f"s_fg{l}", [4, S], F32))
            st1 = _st.enter_context(nc.sbuf_tensor(f"s_t1{l}", [4, S], F32))
            st2 = _st.enter_context(nc.sbuf_tensor(f"s_t2{l}", [4, S], F32))
            szero = _st.enter_context(nc.sbuf_tensor(f"s_zero{l}", [4, S], F32))
            dma("gpsimd", dwA.ap().rearrange("p (k m) -> p k m", m=D),
                dt["down_wT"].ap()[l].rearrange("(k p) m -> p k m", p=128)
                [:, 0:8, :], writes=["dwA"])
            dma("sync", ig4[:, :], dt["sp_g8"].ap()[0:4, :],
                reads=["d:sp_g8"], writes=["s_ig"])
            dma("sync", fg4[:, :], dt["sp_g8"].ap()[4:8, :],
                reads=["d:sp_g8"], writes=["s_fg"])
            em.step("vector", lambda e: e.memset(szero[:, :], 0.0),
                    writes=["s_zero"])
            em.step("scalar", lambda e: e.activation(
                st1[:, :], fg4[:, :], AF.Abs),
                reads=["s_fg"], writes=["s_t1"], small=True)
            em.step("scalar", lambda e: e.activation(
                st1[:, :], st1[:, :], AF.Exp, scale=-1.0),
                reads=["s_t1"], writes=["s_t1"], small=True)
            em.step("scalar", lambda e: e.activation(
                st1[:, :], st1[:, :], AF.Ln, bias=constv[0:4, 1:2]),
                reads=["s_t1", "constv"], writes=["s_t1"], small=True)
            em.step("vector", lambda e: e.tensor_scalar(
                st2[:, :], fg4[:, :], 0.0, None, ALU.min),
                reads=["s_fg"], writes=["s_t2"], small=True)
            em.step("vector", lambda e: e.tensor_tensor(
                st2[:, :], st2[:, :], st1[:, :], ALU.subtract),
                reads=["s_t2", "s_t1"], writes=["s_t2"], small=True)
            em.step("vector", lambda e: e.tensor_tensor_scan(
                st1[:, :], st2[:, :], szero[:, :], 0.0, ALU.add, ALU.add),
                reads=["s_t2", "s_zero"], writes=["s_t1"], small=True)
            em.step("vector", lambda e: e.tensor_tensor(
                fg4[:, :], ig4[:, :], st1[:, :], ALU.subtract),
                reads=["s_ig", "s_t1"], writes=["s_fg"], small=True)
            dma("sync", dt["sp_rows"].ap()[0:4, :], fg4[:, :],
                reads=["s_fg"], writes=["d:sp_rows"])
            em.step("vector", lambda e: e.tensor_tensor_scan(
                st2[:, :], fg4[:, :], szero[:, :], -3.0e38, ALU.max, ALU.add),
                reads=["s_fg", "s_zero"], writes=["s_t2"], small=True)
            dma("sync", dt["sp_rows"].ap()[8:12, :], st2[:, :],
                reads=["s_t2"], writes=["d:sp_rows"])
            em.step("vector", lambda e: e.tensor_tensor(
                st1[:, :], st1[:, :], st2[:, :], ALU.add),
                reads=["s_t1", "s_t2"], writes=["s_t1"], small=True)
            em.step("scalar", lambda e: e.activation(
                st1[:, :], st1[:, :], AF.Exp, scale=-1.0),
                reads=["s_t1"], writes=["s_t1"], small=True)
            dma("sync", dt["sp_rows"].ap()[4:8, :], st1[:, :],
                reads=["s_t1"], writes=["d:sp_rows"])
            em.flush()

        with ExitStack() as _st:
            k_sb = _st.enter_context(nc.sbuf_tensor(f"b_k{l}", [128, 4 * S], BF16))
            v_sb = _st.enter_context(nc.sbuf_tensor(f"b_v{l}", [128, 32 * DH], BF16))
            q_sb = _st.enter_context(nc.sbuf_tensor(f"b_q{l}", [128, 2 * 4 * 512], BF16))
            mrow = _st.enter_context(nc.sbuf_tensor(f"b_mr{l}", [1, 2 * S], F32))
            MB = _st.enter_context(nc.sbuf_tensor(f"b_MB{l}", [128, 2 * S], F32))
            g32 = _st.enter_context(nc.sbuf_tensor(f"b_g32{l}", [32, 2 * 128], F32))
            gcol = _st.enter_context(nc.sbuf_tensor(f"b_gcol{l}", [128, 2 * 32], F32))
            gcol2 = _st.enter_context(nc.sbuf_tensor(f"b_gcol2{l}", [128, 2 * 32], F32))
            ecol = _st.enter_context(nc.sbuf_tensor(f"b_ecol{l}", [128, 2 * 32], F32))
            epsb = _st.enter_context(nc.sbuf_tensor(f"b_eps{l}", [128, 3 * 512], F32))
            Eb = _st.enter_context(nc.sbuf_tensor(f"b_E{l}", [128, 3 * 512], BF16))
            ctb = _st.enter_context(nc.sbuf_tensor(f"b_ct{l}", [128, 3 * 512], BF16))
            nb = _st.enter_context(nc.sbuf_tensor(f"b_nb{l}", [128, 4], F32))
            ninv = _st.enter_context(nc.sbuf_tensor(f"b_ninv{l}", [128, 4], F32))
            stat = _st.enter_context(nc.sbuf_tensor(f"b_stat{l}", [128, 24], F32))
            hf = _st.enter_context(nc.sbuf_tensor(f"b_hf{l}", [128, 4 * 512], F32))
            hsq = _st.enter_context(nc.sbuf_tensor(f"b_hsq{l}", [128, 512], F32))
            hn = _st.enter_context(nc.sbuf_tensor(f"b_hn{l}", [128, 4 * 512], F32))
            hT = _st.enter_context(nc.sbuf_tensor(f"b_hT{l}", [128, 2048], BF16))
            pC0 = _st.enter_context(nc.psum_tensor(f"b_pC0{l}", [128, 512], F32))
            pC1 = _st.enter_context(nc.psum_tensor(f"b_pC1{l}", [128, 512], F32))
            pH0 = _st.enter_context(nc.psum_tensor(f"b_pH0{l}", [128, 512], F32))
            pH1 = _st.enter_context(nc.psum_tensor(f"b_pH1{l}", [128, 512], F32))
            pH2 = _st.enter_context(nc.psum_tensor(f"b_pH2{l}", [128, 512], F32))
            pH3 = _st.enter_context(nc.psum_tensor(f"b_pH3{l}", [128, 512], F32))
            pP = _st.enter_context(nc.psum_tensor(f"b_pP{l}", [128, 512], F32))
            pT = _st.enter_context(nc.psum_tensor(f"b_pT{l}", [128, 512], F32))
            pH = [pH0, pH1, pH2, pH3]
            pHt = ["b_pH0", "b_pH1", "b_pH2", "b_pH3"]
            pC = [pC0, pC1]
            pCt = ["b_pC0", "b_pC1"]
            def head_prologue(h):
                hp = h % 2
                dma("sync", mrow[0:1, hp * S:(hp + 1) * S],
                    dt["sp_rows"].ap()[8 + h:9 + h, :],
                    reads=["d:sp_rows"], writes=["b_mr%d" % hp])
                em.step("vector", lambda e, hp=hp: e.tensor_scalar(
                    mrow[0:1, hp * S:(hp + 1) * S],
                    mrow[0:1, hp * S:(hp + 1) * S], -1.0, None,
                    ALU.mult),
                    reads=["b_mr%d" % hp], writes=["b_mr%d" % hp],
                    small=True)
                for bb in range(8):
                    em.step("tensor", lambda e, bb=bb, hp=hp: e.matmul(
                        pT[:, :], ones_r[0:1, :],
                        mrow[0:1, hp * S + bb * 512:hp * S + (bb + 1) * 512],
                        start=True, stop=True),
                        reads=["b_mr%d" % hp, "ones_r", "b_MB%d" % hp],
                        writes=["b_pT"], small=True)
                    em.step("scalar", lambda e, bb=bb, hp=hp: e.activation(
                        MB[:, hp * S + bb * 512:hp * S + (bb + 1) * 512],
                        pT[:, :], AF.Copy),
                        reads=["b_pT"], writes=["b_MB%d" % hp])
                for (src0, dst, dtok) in ((0, gcol, "b_gcol"),
                                          (4, ecol, "b_ecol")):
                    dma("sync", g32[:, hp * 128:(hp + 1) * 128],
                        dt["sp_rows"].ap()[src0 + h:src0 + h + 1, :]
                        .rearrange("r (j f) -> (r j) f", f=128),
                        reads=["d:sp_rows"], writes=["b_g32%d" % hp])
                    em.step("tensor", lambda e, hp=hp: e.transpose(
                        pT[:, 0:32], g32[:, hp * 128:(hp + 1) * 128],
                        identf[0:32, 0:32]),
                        reads=["b_g32%d" % hp, "identf"], writes=["b_pT"])
                    em.step("scalar", lambda e, dst=dst, hp=hp, dtok=dtok:
                            e.activation(
                                dst[:, hp * 32:(hp + 1) * 32],
                                pT[:, 0:32], AF.Copy),
                            reads=["b_pT"], writes=[dtok + str(hp)])
                em.step("vector", lambda e, hp=hp: e.tensor_scalar(
                    gcol2[:, hp * 32:(hp + 1) * 32],
                    gcol[:, hp * 32:(hp + 1) * 32], NEG_HALF_LOG_DH, None,
                    ALU.add),
                    reads=["b_gcol%d" % hp], writes=["b_gcol2%d" % hp],
                    small=True)

            head_prologue(0)
            for h in range(NHEAD):
                hp = h % 2
                if h > 0 and not PREFETCH_PRO:
                    head_prologue(h)
                for ch in range(4):
                    dma("scalar",
                        k_sb.ap().rearrange("p (c t) -> p c t", t=S)
                        [:, :, ch * 1024:(ch + 1) * 1024],
                        spk_r[:, 4 * h:4 * h + 4,
                              ch * 1024:(ch + 1) * 1024],
                        reads=["d:sp_k"], writes=["b_kc%d" % ch])
                    dma("scalar",
                        v_sb.ap().rearrange("p (s c) -> p s c", c=DH)
                        [:, ch * 8:(ch + 1) * 8, :],
                        spv_r[:, ch * 8:(ch + 1) * 8, DH * h:DH * h + DH],
                        reads=["d:sp_v"], writes=["b_vc%d" % ch])
                pending_fin = []
                pending_fa = []
                pending_fb = []

                def qload(I, h=h):
                    qi = I % 2
                    dma("sync",
                        q_sb.ap().rearrange("p (qi c t) -> p qi c t",
                                            qi=2, t=512)[:, qi],
                        spq_r[:, 4 * h:4 * h + 4, I * 512:I * 512 + 512],
                        reads=["d:sp_q"], writes=["b_q" + str(qi)])
                qload(0)
                for I in range(8):
                    s0 = I * 512
                    qi = I % 2
                    qv = q_sb.ap().rearrange("p (qi c t) -> p qi c t",
                                             qi=2, t=512)[:, qi]
                    if I + 1 < 8:
                        qload(I + 1)
                    nJ = 4 * I + 4

                    def emit_qk(J):
                        pc, pct = pC[J % 2], pCt[J % 2]

                        def f_qk(e, J=J, pc=pc, qv=qv):
                            i = None
                            for d in range(4):
                                i = e.matmul(pc[:, :],
                                             k_sb[:, d * S + J * 128:
                                                  d * S + J * 128 + 128],
                                             qv[:, d, :],
                                             start=(d == 0), stop=(d == 3))
                            return i
                        em.step("tensor", f_qk,
                                reads=["b_kc%d" % (J // 8),
                                       "b_q" + str(qi)],
                                writes=[pct])

                    def emit_e(J, I=None):
                        ei = J % 3
                        o = J - 4 * I
                        s0 = I * 512
                        if o < 0:
                            em.step("scalar",
                                    lambda e, ei=ei, J=J, s0=s0, hp=hp:
                                    e.activation(
                                        Eb[:, ei * 512:(ei + 1) * 512],
                                        MB[:, hp * S + s0:hp * S + s0 + 512],
                                        AF.Exp,
                                        bias=gcol2[:, hp * 32 + J:
                                                   hp * 32 + J + 1]),
                                    reads=["b_MB%d" % hp, "b_gcol2%d" % hp],
                                    writes=["b_E" + str(ei)], small=True)
                        else:
                            em.step("vector",
                                    lambda e, ei=ei, J=J, s0=s0, hp=hp:
                                    e.tensor_scalar(
                                        epsb[:, ei * 512:(ei + 1) * 512],
                                        MB[:, hp * S + s0:hp * S + s0 + 512],
                                        gcol[:, hp * 32 + J:hp * 32 + J + 1],
                                        0.0, ALU.add,
                                        ALU.min),
                                    reads=["b_MB%d" % hp, "b_gcol%d" % hp],
                                    writes=["b_eps" + str(ei)], small=True)
                            em.step("scalar", lambda e, ei=ei: e.activation(
                                Eb[:, ei * 512:(ei + 1) * 512],
                                epsb[:, ei * 512:(ei + 1) * 512], AF.Exp,
                                bias=constv[:, 2:3]),
                                reads=["b_eps" + str(ei), "constv"],
                                writes=["b_E" + str(ei)])
                            em.step("gpsimd" if MASK_POOL else "vector",
                                    lambda e, ei=ei, o=o:
                                    e.tensor_tensor(
                                        Eb[:, ei * 512:(ei + 1) * 512],
                                        Eb[:, ei * 512:(ei + 1) * 512],
                                        masks_sb[:, o * 512:(o + 1) * 512],
                                        ALU.mult),
                                    reads=["b_E" + str(ei), "masks"],
                                    writes=["b_E" + str(ei)])

                    def emit_ct(J):
                        ei = J % 3
                        pc, pct = pC[J % 2], pCt[J % 2]
                        em.step("vector", lambda e, ei=ei, pc=pc:
                                e.tensor_tensor(
                                    ctb[:, ei * 512:(ei + 1) * 512],
                                    pc[:, :], Eb[:, ei * 512:(ei + 1) * 512],
                                    ALU.mult),
                                reads=[pct, "b_E" + str(ei)],
                                writes=["b_ct" + str(ei)])

                    def emit_av(J, nJ):
                        ei = J % 3

                        def f_av(e, J=J, ei=ei, nJ=nJ):
                            first = (J == 0)
                            i = None
                            for sub in range(4):
                                i = e.matmul(pH[sub][:, :],
                                             ctb[:, ei * 512 + sub * 128:
                                                 ei * 512 + sub * 128 + 128],
                                             v_sb[:, J * DH:(J + 1) * DH],
                                             start=first,
                                             stop=(J == nJ - 1))
                                i = e.matmul(pP[:, sub:sub + 1],
                                             ctb[:, ei * 512 + sub * 128:
                                                 ei * 512 + sub * 128 + 128],
                                             ones_c[:, :],
                                             start=(first and sub == 0),
                                             stop=(J == nJ - 1))
                            return i
                        em.step("tensor", f_av,
                                reads=["b_ct" + str(ei), "b_vc%d" % (J // 8),
                                       "ones_c"],
                                writes=pHt + ["b_pP"])

                    for J in range(nJ):
                        emit_qk(J)
                        if J == 2 and pending_fa:
                            for fn in pending_fa:
                                fn()
                            pending_fa = []
                        if J == 4 and pending_fb:
                            for fn in pending_fb:
                                fn()
                            pending_fb = []
                        if J == 7 and pending_fin:
                            for fn in pending_fin:
                                fn()
                            pending_fin = []
                        emit_e(J, I=I)
                        emit_ct(J)
                        if J >= 2:
                            emit_av(J - 2, nJ)
                    if pending_fa:
                        for fn in pending_fa:
                            fn()
                        pending_fa = []
                    if pending_fb:
                        for fn in pending_fb:
                            fn()
                        pending_fb = []
                    if pending_fin:
                        for fn in pending_fin:
                            fn()
                        pending_fin = []
                    emit_av(nJ - 2, nJ)
                    emit_av(nJ - 1, nJ)

                    def fin_a(I=I):
                        em.step("scalar", lambda e: e.activation(
                            nb[:, :], pP[:, 0:4], AF.Abs),
                            reads=["b_pP"], writes=["b_nb"], small=True)
                        em.step("vector", lambda e, I=I, hp=hp:
                            e.tensor_tensor(
                            nb[:, :], nb[:, :],
                            ecol[:, hp * 32 + 4 * I:hp * 32 + 4 * I + 4],
                            ALU.max), reads=["b_nb", "b_ecol%d" % hp],
                            writes=["b_nb"], small=True)
                        em.step("vector", lambda e: e.tensor_scalar(
                            nb[:, :], nb[:, :], 1e-6, None, ALU.add),
                            reads=["b_nb"], writes=["b_nb"], small=True)
                        em.step("vector", lambda e: e.reciprocal(
                            ninv[:, :], nb[:, :]), reads=["b_nb"],
                            writes=["b_ninv"], small=True)
                        for sub in range(4):
                            em.step("scalar", lambda e, sub=sub: e.activation(
                                hf[:, sub * 512:(sub + 1) * 512],
                                pH[sub][:, :], AF.Copy,
                                scale=ninv[:, sub:sub + 1],
                                accum_out=stat[:, sub:sub + 1]),
                                reads=[pHt[sub], "b_ninv"],
                                writes=["b_hf%d" % sub, "b_stat"],
                                small=True)
                    pending_fa.append(fin_a)

                    def fin_b():
                        for sub in range(4):
                            em.step("scalar", lambda e, sub=sub:
                                    e.activation(
                                        hsq[:, :],
                                        hf[:, sub * 512:(sub + 1) * 512],
                                        AF.Square,
                                        accum_out=stat[:, 4 + sub:5 + sub]),
                                    reads=["b_hf%d" % sub],
                                    writes=["b_hsq", "b_stat"], small=True)
                        em.step("vector", lambda e: e.tensor_scalar(
                            stat[:, 8:12], stat[:, 0:4], 1.0 / DH, None,
                            ALU.mult), reads=["b_stat"], writes=["b_stat"],
                            small=True)
                        em.step("vector", lambda e: e.tensor_tensor(
                            stat[:, 12:16], stat[:, 8:12], stat[:, 8:12],
                            ALU.mult), reads=["b_stat"], writes=["b_stat"],
                            small=True)
                        em.step("vector", lambda e: e.scalar_tensor_tensor(
                            stat[:, 16:20], stat[:, 4:8], 1.0 / DH,
                            stat[:, 12:16], ALU.mult, ALU.subtract),
                            reads=["b_stat"], writes=["b_stat"], small=True)
                        em.step("scalar", lambda e: e.activation(
                            stat[:, 16:20], stat[:, 16:20], AF.Sqrt,
                            bias=constv[:, 0:1]),
                            reads=["b_stat", "constv"], writes=["b_stat"],
                            small=True)
                        em.step("vector", lambda e: e.reciprocal(
                            stat[:, 20:24], stat[:, 16:20]),
                            reads=["b_stat"], writes=["b_stat"], small=True)
                        for sub in range(4):
                            em.step("vector", lambda e, sub=sub:
                                    e.tensor_scalar(
                                        hn[:, sub * 512:(sub + 1) * 512],
                                        hf[:, sub * 512:(sub + 1) * 512],
                                        stat[:, 8 + sub:9 + sub],
                                        stat[:, 20 + sub:21 + sub],
                                        ALU.subtract, ALU.mult),
                                    reads=["b_hf%d" % sub, "b_stat"],
                                    writes=["b_hn" + str(sub)], small=True)
                    pending_fb.append(fin_b)

                    for sub in range(4):
                        def fin_tr(sub=sub, I=I, h=h, s0=s0):
                            def f_tr(e, sub=sub):
                                i = None
                                for cdh in range(4):
                                    i = e.transpose(
                                        pT[:, cdh * 128:(cdh + 1) * 128],
                                        hn[:, sub * 512 + cdh * 128:
                                           sub * 512 + (cdh + 1) * 128],
                                        identf[:, :])
                                return i
                            em.step("tensor", f_tr,
                                    reads=["b_hn" + str(sub), "identf"],
                                    writes=["b_pT"])
                            em.step("vector", lambda e, sub=sub:
                                    e.tensor_copy(
                                        hT.ap().rearrange(
                                            "p (c i s) -> p c i s",
                                            c=4, i=4, s=128)[:, :, sub, :],
                                        pT[:, 0:512].rearrange(
                                            "p (c s) -> p c s", s=128)),
                                    reads=["b_pT"], writes=["b_hT"])
                        pending_fin.append(fin_tr)

                    def fin_dma(h=h, s0=s0):
                        dma("sync",
                            sphnT_r[:, 4 * h:4 * h + 4, s0:s0 + 512],
                            hT.ap().rearrange("p (c t) -> p c t", t=512),
                            reads=["b_hT"], writes=["d:sp_hnT"])
                    pending_fin.append(fin_dma)
                    if I == 5 and h + 1 < NHEAD and PREFETCH_PRO:
                        head_prologue(h + 1)
                if pending_fa:
                    for fn in pending_fa:
                        fn()
                    pending_fa = []
                if pending_fb:
                    for fn in pending_fb:
                        fn()
                    pending_fb = []
                if pending_fin:
                    for fn in pending_fin:
                        fn()
                    pending_fin = []
            em.flush()

        TBX = 256
        NTBX = S // TBX
        with ExitStack() as _st:
            dwB = _st.enter_context(nc.sbuf_tensor(f"c_dwB{l}", [128, 8 * D], BF16))
            hTb = _st.enter_context(nc.sbuf_tensor(f"c_hT{l}", [128, 2 * 16 * TBX], BF16))
            xcc = _st.enter_context(nc.sbuf_tensor(f"c_xc{l}", [128, 2 * 16 * TBX], BF16))
            zc = _st.enter_context(nc.sbuf_tensor(f"c_z{l}", [128, 2 * 16 * TBX], BF16))
            xbc = _st.enter_context(nc.sbuf_tensor(f"c_x{l}", [128, 2 * 8 * TBX], F32))
            cs1 = _st.enter_context(nc.sbuf_tensor(f"c_s1{l}", [128, 2 * TBX], F32))
            cs2 = _st.enter_context(nc.sbuf_tensor(f"c_s2{l}", [128, 2 * TBX], F32))
            cs3 = _st.enter_context(nc.sbuf_tensor(f"c_s3{l}", [128, 2 * TBX], F32))
            outc = _st.enter_context(nc.sbuf_tensor(f"c_out{l}", [128, 2 * 16 * TBX], BF16))
            xoc = _st.enter_context(nc.sbuf_tensor(f"c_xo{l}", [128, 2 * 8 * TBX], F32))
            pDA = _st.enter_context(nc.psum_tensor(f"c_pDA{l}", [128, 8 * TBX], F32))
            pDB = _st.enter_context(nc.psum_tensor(f"c_pDB{l}", [128, 8 * TBX], F32))
            dma("gpsimd", dwB.ap().rearrange("p (k m) -> p k m", m=D),
                dt["down_wT"].ap()[l].rearrange("(k p) m -> p k m", p=128)
                [:, 8:16, :], writes=["dwB"])

            def c_loads(tb):
                t0 = tb * TBX
                par = tb % 2
                o16 = par * 16 * TBX
                o8 = par * 8 * TBX
                dma("scalar",
                    hTb[:, o16:o16 + 16 * TBX]
                    .rearrange("p (c t) -> p c t", t=TBX),
                    sphnT_r[:, :, t0:t0 + TBX], reads=["d:sp_hnT"],
                    writes=["c_hT%d" % par])
                dma("scalar",
                    xcc[:, o16:o16 + 16 * TBX]
                    .rearrange("p (c t) -> p c t", t=TBX),
                    spxc_r[:, :, t0:t0 + TBX], reads=["d:sp_xc"],
                    writes=["c_xc%d" % par])
                dma("scalar",
                    zc[:, o16:o16 + 16 * TBX]
                    .rearrange("p (c t) -> p c t", t=TBX),
                    spz_r[:, :, t0:t0 + TBX], reads=["d:sp_z"],
                    writes=["c_z%d" % par])
                dma("sync",
                    xbc[:, o8:o8 + 8 * TBX]
                    .rearrange("p (c t) -> p c t", t=TBX),
                    spx_r[:, :, t0:t0 + TBX], reads=["d:sp_x"],
                    writes=["c_x%d" % par])
            c_loads(0)
            c_loads(1)
            for tb in range(NTBX):
                t0 = tb * TBX
                par = tb % 2
                o16 = par * 16 * TBX
                o8 = par * 8 * TBX
                pD = pDA if par == 0 else pDB
                pdtok = "c_pDA" if par == 0 else "c_pDB"
                for c in range(16):
                    cp = c % 2
                    co = cp * TBX
                    em.step("vector", lambda e, c=c, o16=o16, par=par, co=co:
                            e.tensor_scalar(
                                cs1[:, co:co + TBX],
                                hTb[:, o16 + c * TBX:o16 + (c + 1) * TBX],
                                onorm_sb[:, l * 16 + c:l * 16 + c + 1],
                                None, ALU.mult),
                            reads=["c_hT%d" % par, "onormc"],
                            writes=["c_s1%d" % cp])
                    em.step("vector", lambda e, c=c, o16=o16, par=par, co=co:
                            e.scalar_tensor_tensor(
                                cs2[:, co:co + TBX],
                                xcc[:, o16 + c * TBX:o16 + (c + 1) * TBX],
                                skip_sb[:, l * 16 + c:l * 16 + c + 1],
                                cs1[:, co:co + TBX], ALU.mult, ALU.add),
                            reads=["c_xc%d" % par, "skipc", "c_s1%d" % cp],
                            writes=["c_s2%d" % cp])
                    em.step("scalar", lambda e, c=c, o16=o16, par=par, co=co:
                            e.activation(
                                cs3[:, co:co + TBX],
                                zc[:, o16 + c * TBX:o16 + (c + 1) * TBX],
                                AF.Silu),
                            reads=["c_z%d" % par], writes=["c_s3%d" % cp])
                    em.step("gpsimd", lambda e, c=c, o16=o16, par=par, co=co:
                            e.tensor_tensor(
                                outc[:, o16 + c * TBX:o16 + (c + 1) * TBX],
                                cs2[:, co:co + TBX], cs3[:, co:co + TBX],
                                ALU.mult),
                            reads=["c_s2%d" % cp, "c_s3%d" % cp],
                            writes=["c_o%d_%d" % (par, c), "c_out%d" % par])

                    dwc = dwA if c < 8 else dwB
                    dwtok = "dwA" if c < 8 else "dwB"

                    def f_dnk(e, c=c, pD=pD, o16=o16, dwc=dwc):
                        i = None
                        for m in range(8):
                            i = e.matmul(pD[:, m * TBX:(m + 1) * TBX],
                                         dwc[:, (c % 8) * D + m * 128:
                                             (c % 8) * D + m * 128 + 128],
                                         outc[:, o16 + c * TBX:
                                              o16 + (c + 1) * TBX],
                                         start=(c == 0 and m % 2 == 0),
                                         stop=(c == 15))
                        return i
                    em.step("tensor", f_dnk,
                            reads=[dwtok, "c_o%d_%d" % (par, c)],
                            writes=[pdtok])
                for m in range(8):
                    em.step("vector", lambda e, m=m, pD=pD, o8=o8, par=par:
                            e.tensor_tensor(
                                xoc[:, o8 + m * TBX:o8 + (m + 1) * TBX],
                                pD[:, m * TBX:(m + 1) * TBX],
                                xbc[:, o8 + m * TBX:o8 + (m + 1) * TBX],
                                ALU.add),
                            reads=[pdtok, "c_x%d" % par],
                            writes=["c_xo%d" % par])
                dma("sync", spx_r[:, :, t0:t0 + TBX],
                    xoc[:, o8:o8 + 8 * TBX]
                    .rearrange("p (c t) -> p c t", t=TBX),
                    reads=["c_xo%d" % par], writes=["d:sp_x"])
                if tb + 2 < NTBX:
                    c_loads(tb + 2)
            em.flush()
        dwst.close()

    if LAT_ON:
        TBL = 256
        NTBL = S // TBL
        with ExitStack() as _st:
            latw = _st.enter_context(nc.sbuf_tensor("l_lat", [128, 8 * D], BF16))
            xbl = _st.enter_context(nc.sbuf_tensor("l_x", [128, 2 * 8 * TBL], F32))
            x0l = _st.enter_context(nc.sbuf_tensor("l_x0", [128, 2 * 8 * TBL], F32))
            xnl = _st.enter_context(nc.sbuf_tensor("l_xn", [128, 2 * 8 * TBL], BF16))
            xb16l = _st.enter_context(nc.sbuf_tensor("l_b16", [128, 2 * TBL], BF16))
            sqbl = _st.enter_context(nc.sbuf_tensor("l_sq", [128, 2 * TBL], BF16))
            rowmul = _st.enter_context(nc.sbuf_tensor("l_rmu", [1, TBL], F32))
            rowrsl = _st.enter_context(nc.sbuf_tensor("l_rrs", [1, TBL], F32))
            rowtl = _st.enter_context(nc.sbuf_tensor("l_rt", [1, TBL], F32))
            muBl = _st.enter_context(nc.sbuf_tensor("l_muB", [128, TBL], F32))
            rstdBl = _st.enter_context(nc.sbuf_tensor("l_rsB", [128, TBL], F32))
            t1l = _st.enter_context(nc.sbuf_tensor("l_t1", [128, TBL], F32))
            xol = _st.enter_context(nc.sbuf_tensor("l_xo", [128, 2 * 8 * TBL], F32))
            pLNl = _st.enter_context(nc.psum_tensor("l_pLN", [1, 2 * TBL], F32))
            pBl = _st.enter_context(nc.psum_tensor("l_pB", [128, TBL], F32))
            pLa = _st.enter_context(nc.psum_tensor("l_pLa", [128, 8 * TBL], F32))
            lb = dict(xb16=xb16l, sqb=sqbl, pLN=pLNl, rowmu=rowmul,
                      rowrs=rowrsl, rowt=rowtl, pB=pBl, muB=muBl,
                      rstdB=rstdBl, t1=t1l, pool_stats=POOL_STATS)
            dma("gpsimd", latw.ap().rearrange("p (k m) -> p k m", m=D),
                dt["latT"].ap().rearrange("(k p) m -> p k m", p=128),
                writes=["latw"])

            def l_loads(tb):
                par = tb % 2
                t0 = tb * TBL
                dma("sync",
                    xbl[:, par * 8 * TBL:(par + 1) * 8 * TBL]
                    .rearrange("p (c t) -> p c t", t=TBL),
                    spx_r[:, :, t0:t0 + TBL], reads=["d:sp_x"],
                    writes=["l_x%d" % par])
                dma("scalar",
                    x0l[:, par * 8 * TBL:(par + 1) * 8 * TBL]
                    .rearrange("p (c t) -> p c t", t=TBL),
                    xT_r[:, :, t0:t0 + TBL], writes=["l_x0%d" % par])

            def l_pieces(tb):
                par = tb % 2
                xv = xbl[:, par * 8 * TBL:(par + 1) * 8 * TBL]
                xnv = xnl[:, par * 8 * TBL:(par + 1) * 8 * TBL]
                return ln_block_q("l", "l_x%d" % par, xv, 8, TBL, xnv,
                                  "l_xn%d" % par,
                                  (lambda c: postw_sb[:, c:c + 1]), lb)

            l_loads(0)
            l_loads(1)
            pcs = l_pieces(0)
            for p in pcs[:11]:
                if p is not None:
                    p()
            norms = pcs[11:]
            for tb in range(NTBL):
                par = tb % 2
                t0 = tb * TBL
                nxt = l_pieces(tb + 1) if tb + 1 < NTBL else []
                nxt_c, nxt_n = nxt[:11], nxt[11:]
                for k in range(8):
                    norms[k]()

                    def f_latk(e, k=k, par=par):
                        i = None
                        for m in range(8):
                            i = e.matmul(pLa[:, m * TBL:(m + 1) * TBL],
                                         latw[:, k * D + m * 128:
                                              k * D + m * 128 + 128],
                                         xnl[:, par * 8 * TBL + k * TBL:
                                             par * 8 * TBL + (k + 1) * TBL],
                                         start=(k == 0 and m % 2 == 0),
                                         stop=(k == 7))
                        return i
                    em.step("tensor", f_latk,
                            reads=["latw", "l_xn%d" % par], writes=["l_pLa"])
                    if k >= 2:
                        for p in nxt_c[:2]:
                            if p is not None:
                                p()
                        nxt_c = nxt_c[2:]
                for p in nxt_c:
                    if p is not None:
                        p()
                for m in range(8):
                    em.step("vector", lambda e, m=m, par=par:
                            e.scalar_tensor_tensor(
                                xol[:, par * 8 * TBL + m * TBL:
                                    par * 8 * TBL + (m + 1) * TBL],
                                pLa[:, m * TBL:(m + 1) * TBL],
                                latb_sb[:, m:m + 1],
                                x0l[:, par * 8 * TBL + m * TBL:
                                    par * 8 * TBL + (m + 1) * TBL],
                                ALU.add, ALU.add),
                            reads=["l_pLa", "l_x0%d" % par, "latb"],
                            writes=["l_xo%d" % par])
                dma("sync", spx1_r[:, :, t0:t0 + TBL],
                    xol[:, par * 8 * TBL:(par + 1) * 8 * TBL]
                    .rearrange("p (c t) -> p c t", t=TBL),
                    reads=["l_xo%d" % par], writes=["d:sp_x1"])
                if tb + 2 < NTBL:
                    l_loads(tb + 2)
                norms = nxt_n
            em.flush()

    if MOE_ON:
        with ExitStack() as _st:
            x1m = _st.enter_context(nc.sbuf_tensor("m_x1", [128, 2 * 8 * TBM], F32))
            xnm = _st.enter_context(nc.sbuf_tensor("m_xn", [128, 8 * TBM], BF16))
            xnf = _st.enter_context(nc.sbuf_tensor("m_xnf", [128, 8 * TBM], F32))
            rt_sb = _st.enter_context(nc.sbuf_tensor("m_rt", [128, 8 * 8], F32))
            xb16m = _st.enter_context(nc.sbuf_tensor("m_b16", [128, 2 * TBM], BF16))
            sqbm = _st.enter_context(nc.sbuf_tensor("m_sq", [128, 2 * TBM], BF16))
            rowmum = _st.enter_context(nc.sbuf_tensor("m_rmu", [1, TBM], F32))
            rowrsm = _st.enter_context(nc.sbuf_tensor("m_rrs", [1, TBM], F32))
            rowtm = _st.enter_context(nc.sbuf_tensor("m_rt2", [1, TBM], F32))
            muBm = _st.enter_context(nc.sbuf_tensor("m_muB", [128, TBM], F32))
            rstdBm = _st.enter_context(nc.sbuf_tensor("m_rsB", [128, TBM], F32))
            t1m = _st.enter_context(nc.sbuf_tensor("m_t1", [128, TBM], F32))
            lgT = _st.enter_context(nc.sbuf_tensor("m_lg", [8, TBM], F32))
            lgtok = _st.enter_context(nc.sbuf_tensor("m_lgt", [128, 2 * 8], F32))
            m1c = _st.enter_context(nc.sbuf_tensor("m_m1", [128, 2], F32))
            m2c = _st.enter_context(nc.sbuf_tensor("m_m2", [128, 2], F32))
            mskc = _st.enter_context(nc.sbuf_tensor("m_msk", [128, 8], F32))
            e1c = _st.enter_context(nc.sbuf_tensor("m_e1", [128, 2 * 8], F32))
            cmb = _st.enter_context(nc.sbuf_tensor("m_cmb", [128, 2 * 8], F32))
            cTf = _st.enter_context(nc.sbuf_tensor("m_cT", [8, TBM], F32))
            pM = _st.enter_context(nc.psum_tensor("m_pM", [128, 512], F32))
            pLNm = _st.enter_context(nc.psum_tensor("m_pLN", [1, 2 * TBM], F32))
            lb = dict(xb16=xb16m, sqb=sqbm, pLN=pLNm, rowmu=rowmum,
                      rowrs=rowrsm, rowt=rowtm, pB=pM, muB=muBm,
                      rstdB=rstdBm, t1=t1m, pbtok="m_pM",
                      pool_stats=POOL_STATS)
            dma("sync", rt_sb.ap().rearrange("p (k m) -> p k m", m=8),
                dt["rT"].ap().rearrange("(k p) m -> p k m", p=128),
                writes=["rt"])

            def r_load(tb):
                par = tb % 2
                t0 = tb * TBM
                dma("scalar",
                    x1m[:, par * 8 * TBM:(par + 1) * 8 * TBM]
                    .rearrange("p (c t) -> p c t", t=TBM),
                    spx1_r[:, :, t0:t0 + TBM], reads=["d:sp_x1"],
                    writes=["m_x1%d" % par])

            def r_pieces(tb):
                par = tb % 2
                xv = x1m[:, par * 8 * TBM:(par + 1) * 8 * TBM]
                return ln_block_q("m", "m_x1%d" % par, xv, 8, TBM, xnf,
                                  "m_xnf", None, lb)

            r_load(0)
            r_load(1)
            pcs = r_pieces(0)
            for p in pcs[:11]:
                if p is not None:
                    p()
            norms = pcs[11:]
            for tb in range(S // TBM):
                t0 = tb * TBM
                nxt = r_pieces(tb + 1) if tb + 1 < S // TBM else []
                nxt_c, nxt_n = nxt[:11], nxt[11:]
                for k in range(8):
                    norms[k]()

                    def f_rk(e, k=k):
                        return e.matmul(pM[0:8, 0:TBM],
                                        rt_sb[:, k * 8:(k + 1) * 8],
                                        xnf[:, k * TBM:(k + 1) * TBM],
                                        start=(k == 0), stop=(k == 7))
                    em.step("tensor", f_rk, reads=["rt", "m_xnf", "m_lg"],
                            writes=["m_pM"])
                    em.step("gpsimd", lambda e, k=k: e.tensor_copy(
                        xnm[:, k * TBM:(k + 1) * TBM],
                        xnf[:, k * TBM:(k + 1) * TBM]),
                        reads=["m_xnf"], writes=["m_xn"])
                dma("sync", spxn_r[:, :, t0:t0 + TBM],
                    xnm.ap().rearrange("p (c t) -> p c t", t=TBM),
                    reads=["m_xn"], writes=["d:sp_xn"])
                if tb + 2 < S // TBM:
                    r_load(tb + 2)
                em.step("scalar", lambda e: e.activation(
                    lgT[:, :], pM[0:8, 0:TBM], AF.Copy), reads=["m_pM"],
                    writes=["m_lg"])
                for sub in range(2):
                    em.step("tensor", lambda e, sub=sub: e.transpose(
                        pM[:, 256:264], lgT[:, sub * 128:(sub + 1) * 128],
                        identf[0:8, 0:8]), reads=["m_lg", "identf"],
                        writes=["m_pM"])
                    em.step("scalar", lambda e, sub=sub: e.activation(
                        lgtok[:, sub * 8:(sub + 1) * 8], pM[:, 256:264],
                        AF.Copy), reads=["m_pM"], writes=["m_lgt"], small=True)
                _top2(em, lgtok, m1c, m2c, mskc, e1c, cmb)
                for sub in range(2):
                    em.step("tensor", lambda e, sub=sub: e.transpose(
                        pM[0:8, 256:384], cmb[:, sub * 8:(sub + 1) * 8],
                        identf[:, :]), reads=["m_cmb", "identf"],
                        writes=["m_pM"], small=True)
                    em.step("scalar", lambda e, sub=sub: e.activation(
                        cTf[0:8, sub * 128:(sub + 1) * 128],
                        pM[0:8, 256:384], AF.Copy), reads=["m_pM"],
                        writes=["m_cT"])
                dma("sync", dt["sp_cmb"].ap()[:, t0:t0 + TBM], cTf[0:8, :],
                    reads=["m_cT"], writes=["d:sp_cmb"])
                for p in nxt_c:
                    if p is not None:
                        p()
                norms = nxt_n
                if DBG_B and tb == 0:
                    dma("sync", dt["sp_dbg2"].ap()[:, 0:16], lgtok[:, :],
                        reads=["m_lgt"], writes=["d:sp_dbg2"])
                    dma("sync", dt["sp_dbg2"].ap()[:, 16:32], cmb[:, :],
                        reads=["m_cmb"], writes=["d:sp_dbg2"])
            em.flush()

        with ExitStack() as _st:
            w1a = _st.enter_context(nc.sbuf_tensor("m_w1a", [128, 8 * FF], BF16))
            w1b = _st.enter_context(nc.sbuf_tensor("m_w1b", [128, 8 * FF], BF16))
            w2a = _st.enter_context(nc.sbuf_tensor("m_w2a", [128, 16 * D], BF16))
            w2b = _st.enter_context(nc.sbuf_tensor("m_w2b", [128, 16 * D], BF16))
            xnm = _st.enter_context(nc.sbuf_tensor("m_xn2", [128, 2 * 8 * TBM], BF16))
            cTb = _st.enter_context(nc.sbuf_tensor("m_cTb", [8, 2 * TBM], BF16))
            crow = _st.enter_context(nc.sbuf_tensor("m_crow", [1, 2 * TBM], F32))
            cB = _st.enter_context(nc.sbuf_tensor("m_cB", [128, 2 * TBM], BF16))
            hmb = _st.enter_context(nc.sbuf_tensor("m_hm", [128, 16 * TBM], BF16))
            hgb = _st.enter_context(nc.sbuf_tensor("m_hg", [128, 2 * TBM], BF16))
            oub = _st.enter_context(nc.sbuf_tensor("m_ou", [128, 8 * TBM], F32))
            pW0 = _st.enter_context(nc.psum_tensor("m_pW0", [128, TBM], F32))
            pW1 = _st.enter_context(nc.psum_tensor("m_pW1", [128, TBM], F32))
            pO0 = _st.enter_context(nc.psum_tensor("m_pO0", [128, 512], F32))
            pO1 = _st.enter_context(nc.psum_tensor("m_pO1", [128, 512], F32))
            pO2 = _st.enter_context(nc.psum_tensor("m_pO2", [128, 512], F32))
            pO3 = _st.enter_context(nc.psum_tensor("m_pO3", [128, 512], F32))
            pM2 = _st.enter_context(nc.psum_tensor("m_pM2", [128, TBM], F32))
            pO = [pO0, pO1, pO2, pO3]
            pOt = ["m_pO0", "m_pO1", "m_pO2", "m_pO3"]
            w1s, w2s = [w1a, w1b], [w2a, w2b]
            w1toks, w2toks = ["m_w1a", "m_w1b"], ["m_w2a", "m_w2b"]
            dma("gpsimd", w1a.ap().rearrange("p (k m) -> p k m", m=FF),
                dt["w1T"].ap()[0].rearrange("(k p) m -> p k m", p=128),
                writes=["m_w1a"])
            dma("gpsimd", w2a.ap().rearrange("p (k m) -> p k m", m=D),
                dt["w2T"].ap()[0].rearrange("(k p) m -> p k m", p=128),
                writes=["m_w2a"])
            for ee in range(EXP):
                wi = ee % 2
                w1c, w2c = w1s[wi], w2s[wi]
                w1tok, w2tok = w1toks[wi], w2toks[wi]
                if ee + 1 < EXP:
                    dma("gpsimd",
                        w1s[(ee + 1) % 2].ap()
                        .rearrange("p (k m) -> p k m", m=FF),
                        dt["w1T"].ap()[ee + 1]
                        .rearrange("(k p) m -> p k m", p=128),
                        writes=[w1toks[(ee + 1) % 2]])
                    dma("gpsimd",
                        w2s[(ee + 1) % 2].ap()
                        .rearrange("p (k m) -> p k m", m=D),
                        dt["w2T"].ap()[ee + 1]
                        .rearrange("(k p) m -> p k m", p=128),
                        writes=[w2toks[(ee + 1) % 2]])
                def m_loads(tb, ee=ee):
                    t0 = tb * TBM
                    par = tb % 2
                    dma("scalar",
                        xnm[:, par * 8 * TBM:(par + 1) * 8 * TBM]
                        .rearrange("p (c t) -> p c t", t=TBM),
                        spxn_r[:, :, t0:t0 + TBM], reads=["d:sp_xn"],
                        writes=["m_xn%d" % par])
                    dma("scalar", crow[:, par * TBM:(par + 1) * TBM],
                        dt["sp_cmb"].ap()[ee:ee + 1, t0:t0 + TBM],
                        reads=["d:sp_cmb"], writes=["m_crow%d" % par])
                    if ee == 0:
                        dma("gpsimd", cTb[:, par * TBM:(par + 1) * TBM],
                            dt["sp_cmb"].ap()[:, t0:t0 + TBM],
                            reads=["d:sp_cmb"], writes=["m_cTb%d" % par])
                m_loads(0)
                for tb in range(S // TBM):
                    t0 = tb * TBM
                    par = tb % 2
                    xnv = xnm[:, par * 8 * TBM:(par + 1) * 8 * TBM]
                    if tb + 1 < S // TBM:
                        m_loads(tb + 1)
                    em.step("tensor", lambda e, par=par: e.matmul(
                        pM2[:, :], ones_r[0:1, :],
                        crow[0:1, par * TBM:(par + 1) * TBM],
                        start=True, stop=True),
                        reads=["m_crow%d" % par, "ones_r", "m_cB%d" % par],
                        writes=["m_pM2"])
                    em.step("scalar", lambda e, par=par: e.activation(
                        cB[:, par * TBM:(par + 1) * TBM], pM2[:, :],
                        AF.Copy), reads=["m_pM2"],
                        writes=["m_cB%d" % par])
                    if ee == 0:
                        def f_b2(e, par=par):
                            i = None
                            for hc in range(8):
                                i = e.matmul(
                                    pO[hc // 2][:, (hc % 2) * TBM:
                                                (hc % 2) * TBM + TBM],
                                    b2_sb[:, hc * 128:(hc + 1) * 128],
                                    cTb[:, par * TBM:(par + 1) * TBM],
                                    start=(hc % 2 == 0),
                                    stop=False)
                            return i
                        em.step("tensor", f_b2,
                                reads=["b2e", "m_cTb%d" % par],
                                writes=pOt)
                    for mc in range(16):
                        pW = pW0 if mc % 2 == 0 else pW1
                        pwt = "m_pW0" if mc % 2 == 0 else "m_pW1"

                        def f_w1(e, mc=mc, pW=pW, w1c=w1c, xnv=xnv):
                            i = None
                            for k in range(8):
                                i = e.matmul(pW[:, :],
                                             w1c[:, k * FF + mc * 128:
                                                 k * FF + mc * 128 + 128],
                                             xnv[:, k * TBM:(k + 1) * TBM],
                                             start=(k == 0),
                                             stop=(k == 7))
                            return i
                        em.step("tensor", f_w1, reads=[w1tok, "m_xn%d" % par],
                                writes=[pwt])
                        hp = mc % 2
                        em.step("scalar", lambda e, mc=mc, pW=pW, ee=ee,
                                hp=hp: e.activation(
                                    hgb[:, hp * TBM:(hp + 1) * TBM],
                                    pW[:, :], AF.Gelu,
                                    bias=b1_sb[:, ee * 16 + mc:
                                               ee * 16 + mc + 1]),
                                reads=[pwt, "b1c"], writes=["m_hg%d" % hp])
                        em.step("vector", lambda e, mc=mc, par=par,
                                hp=hp: e.tensor_tensor(
                                    hmb[:, mc * TBM:(mc + 1) * TBM],
                                    hgb[:, hp * TBM:(hp + 1) * TBM],
                                    cB[:, par * TBM:(par + 1) * TBM],
                                    ALU.mult),
                                reads=["m_hg%d" % hp, "m_cB%d" % par],
                                writes=["m_hm"])
                    for hc in range(8):
                        def f_w2(e, hc=hc, ee=ee, w2c=w2c):
                            i = None
                            for k in range(16):
                                i = e.matmul(
                                    pO[hc // 2][:, (hc % 2) * TBM:
                                                (hc % 2) * TBM + TBM],
                                    w2c[:, k * D + hc * 128:
                                        k * D + hc * 128 + 128],
                                    hmb[:, k * TBM:(k + 1) * TBM],
                                    start=(ee != 0 and k == 0
                                           and hc % 2 == 0),
                                    stop=(k == 15))
                            return i
                        em.step("tensor", f_w2, reads=[w2tok, "m_hm"],
                                writes=[pOt[hc // 2]])
                        em.step("scalar", lambda e, hc=hc: e.activation(
                            oub[:, hc * TBM:(hc + 1) * TBM],
                            pO[hc // 2][:, (hc % 2) * TBM:
                                        (hc % 2) * TBM + TBM], AF.Copy),
                            reads=[pOt[hc // 2]], writes=["m_ou"])
                    if ee == 0:
                        dma("gpsimd",
                            dt["sp_acc"].ap()
                            .rearrange("(c p) t -> p c t", p=128)
                            [:, :, t0:t0 + TBM],
                            oub.ap().rearrange("p (c t) -> p c t", t=TBM),
                            reads=["m_ou"], writes=["d:sp_acc"])
                    else:
                        _dst = (dt["sp_acc"].ap()
                                .rearrange("(c p) t -> p c t", p=128)
                                [:, :, t0:t0 + TBM])
                        _srcap = oub.ap().rearrange("p (c t) -> p c t", t=TBM)
                        em.step("gpsimd",
                                lambda e, o=_dst, i=_srcap: e.dma_start(
                                    o, i, accum_op=ALU.add),
                                reads=["m_ou"], writes=["d:sp_acc"],
                                dma=True)
            em.flush()

        with ExitStack() as _st:
            x1m = _st.enter_context(nc.sbuf_tensor("f_x1", [128, 8 * TBC], F32))
            accm = _st.enter_context(nc.sbuf_tensor("f_ac", [128, 8 * TBC], F32))
            xom = _st.enter_context(nc.sbuf_tensor("f_xo", [128, 8 * TBC], F32))
            spacc_r = dt["sp_acc"].ap().rearrange("(c p) t -> p c t", p=128)
            for tb in range(NTBC):
                t0 = tb * TBC
                dma("sync", x1m.ap().rearrange("p (c t) -> p c t", t=TBC),
                    spx1_r[:, :, t0:t0 + TBC], reads=["d:sp_x1"],
                    writes=["f_x1"])
                dma("sync", accm.ap().rearrange("p (c t) -> p c t", t=TBC),
                    spacc_r[:, :, t0:t0 + TBC], reads=["d:sp_acc"],
                    writes=["f_ac"])
                for c in range(8):
                    em.step("vector", lambda e, c=c: e.tensor_tensor(
                        xom[:, c * TBC:(c + 1) * TBC],
                        x1m[:, c * TBC:(c + 1) * TBC],
                        accm[:, c * TBC:(c + 1) * TBC], ALU.add),
                        reads=["f_x1", "f_ac"], writes=["f_xo"])
                if t0 < SHALF:
                    dstv, dtok, tloc = o_r, "d:o", t0
                else:
                    dstv, dtok, tloc = o2_r, "d:o2", t0 - SHALF
                dma("sync", dstv[:, :, tloc:tloc + TBC],
                    xom.ap().rearrange("p (c t) -> p c t", t=TBC),
                    reads=["f_xo"], writes=[dtok])
            em.flush()

    for name in taps:
        dma("sync", tap_t[name].ap(), dt[name].ap(),
            reads=["d:" + name], writes=["tap:" + name])
    em.finish()
    return nc


def _top2(em, lgtok, m1c, m2c, mskc, e1c, cmb):
    for sub in range(2):
        sl = lgtok[:, sub * 8:(sub + 1) * 8]
        em.step("vector", lambda e, sl=sl, sub=sub: e.reduce_max(
            m1c[:, sub:sub + 1], sl, AX.X), reads=["m_lgt"], writes=["m_m1"], small=True)
        em.step("vector", lambda e, sub=sub: e.tensor_scalar(
            m1c[:, sub:sub + 1], m1c[:, sub:sub + 1], -1.0, None, ALU.mult),
            reads=["m_m1"], writes=["m_m1"], small=True)
        em.step("scalar", lambda e, sl=sl, sub=sub: e.activation(
            e1c[:, sub * 8:(sub + 1) * 8], sl, AF.Exp,
            bias=m1c[:, sub:sub + 1]), reads=["m_lgt", "m_m1"],
            writes=["m_e1"], small=True)
        em.step("vector", lambda e, sl=sl, sub=sub: e.tensor_scalar(
            mskc[:, :], sl, m1c[:, sub:sub + 1], 0.0, ALU.add, ALU.is_equal),
            reads=["m_lgt", "m_m1"], writes=["m_msk"], small=True)
        em.step("vector", lambda e, sl=sl: e.scalar_tensor_tensor(
            mskc[:, :], mskc[:, :], -1e30, sl, ALU.mult, ALU.add),
            reads=["m_msk", "m_lgt"], writes=["m_msk"], small=True)
        em.step("vector", lambda e, sub=sub: e.reduce_max(
            m2c[:, sub:sub + 1], mskc[:, :], AX.X), reads=["m_msk"],
            writes=["m_m2"], small=True)
        em.step("vector", lambda e, sl=sl, sub=sub: e.tensor_scalar(
            mskc[:, :], sl, m2c[:, sub:sub + 1], None, ALU.is_ge),
            reads=["m_lgt", "m_m2"], writes=["m_msk"], small=True)
        em.step("scalar", lambda e, sub=sub: e.activation(
            m2c[:, sub:sub + 1], m2c[:, sub:sub + 1], AF.Exp,
            bias=m1c[:, sub:sub + 1]), reads=["m_m2", "m_m1"],
            writes=["m_m2"], small=True)
        em.step("vector", lambda e, sub=sub: e.tensor_scalar(
            m2c[:, sub:sub + 1], m2c[:, sub:sub + 1], 1.0, None, ALU.add),
            reads=["m_m2"], writes=["m_m2"], small=True)
        em.step("vector", lambda e, sub=sub: e.reciprocal(
            m2c[:, sub:sub + 1], m2c[:, sub:sub + 1]), reads=["m_m2"],
            writes=["m_m2"], small=True)
        em.step("vector", lambda e, sub=sub: e.tensor_tensor(
            cmb[:, sub * 8:(sub + 1) * 8], e1c[:, sub * 8:(sub + 1) * 8],
            mskc[:, :], ALU.mult), reads=["m_e1", "m_msk"],
            writes=["m_cmb"], small=True)
        em.step("vector", lambda e, sub=sub: e.tensor_scalar(
            cmb[:, sub * 8:(sub + 1) * 8], cmb[:, sub * 8:(sub + 1) * 8],
            m2c[:, sub:sub + 1], None, ALU.mult), reads=["m_cmb", "m_m2"],
            writes=["m_cmb"], small=True)


_PROGRAM_CACHE = {}


def kernel(**inputs):
    if "prog" not in _PROGRAM_CACHE:
        _PROGRAM_CACHE["prog"] = build_program()
    nc = _PROGRAM_CACHE["prog"]
    in_maps = _host_prepare(inputs)
    res = run_bass_kernel_spmd(nc, in_maps, core_ids=list(range(NCORES)))
    out = np.zeros((B, S, D), np.float32)
    for core in range(NCORES):
        b = core // 2
        half = core % 2
        name = "o" if half == 0 else "o2"
        shard = np.asarray(res.results[core][name])
        out[b, half * SHALF:(half + 1) * SHALF, :] = shard.T
    return out


# version 31 (final)
# speedup vs baseline: 1.0188x; 1.0188x over previous
import os
from contextlib import ExitStack
import numpy as np

import concourse.bass as bass
import concourse.mybir as mybir
from concourse.bass_utils import run_bass_kernel_spmd

F32 = mybir.dt.float32
BF16 = mybir.dt.bfloat16
AF = mybir.ActivationFunctionType
ALU = mybir.AluOpType
AX = mybir.AxisListType

B, S, D = 4, 4096, 1024
INNER = 2048
KCONV = 4
NHEAD = 4
DH = 512
L = 4
EXP, FF = 8, 2048
NCORES = 8
SHALF = S // 2

TBA = 256
NTBA = S // TBA
TBC = 512
NTBC = S // TBC
TBM = 256

NEG_HALF_LOG_DH = -0.5 * float(np.log(DH))

L_EFF = L
MOE_ON = True
LAT_ON = True
DBG_B = False
POOL_STATS = True

ENGS = ("sync", "gpsimd", "scalar", "vector", "tensor")
QDEPTH = 6
SAME_ENGINE_SYNC = True
PREFETCH_PRO = True
MASK_POOL = True


class _Step:
    __slots__ = ("idx", "eng", "fn", "dma", "deps", "sem", "inc", "cum",
                 "small")

    def __init__(self, idx, eng, fn, dma):
        self.idx, self.eng, self.fn, self.dma = idx, eng, fn, dma
        self.small = False
        self.deps = set()
        self.sem = None
        self.inc = 0
        self.cum = 0


class Em:

    def __init__(self, nc):
        self.nc = nc
        self.sems = {}
        self.cnt = {}
        self.steps = []
        self.fstart = 0
        self.lastw = {}
        self.readers = {}
        self.hw = {e: {} for e in ENGS}
        self.qidx = {}
        self.last_event = {}
        self.nflush = 0

    def _sem(self, name):
        if name not in self.sems:
            h = self.nc.ctx.enter_context(
                self.nc.semaphore(name.replace(":", "_")))
            self.sems[name] = h
            self.cnt[name] = 0
        return self.sems[name]

    def step(self, eng, fn, reads=(), writes=(), dma=False, small=False):
        i = len(self.steps)
        s = _Step(i, eng, fn, dma)
        s.small = small
        for t in reads:
            w = self.lastw.get(t)
            if w is not None:
                s.deps.add(w)
        for t in writes:
            w = self.lastw.get(t)
            if w is not None:
                s.deps.add(w)
            for r in self.readers.get(t, ()):
                s.deps.add(r)
        for t in reads:
            self.readers.setdefault(t, []).append(i)
        for t in writes:
            self.lastw[t] = i
            self.readers[t] = []
        if dma:
            q = self.qidx.get(eng, 0)
            self.qidx[eng] = q + 1
            name = f"q:{eng}:{q % QDEPTH}"
            self._sem(name)
            s.sem = name
            s.inc = 16
            self.cnt[name] += 16
            s.cum = self.cnt[name]
        self.steps.append(s)
        return i

    def flush(self):
        new = self.steps[self.fstart:]
        if not new:
            return
        need = set()
        for s in new:
            for d in s.deps:
                if d < self.fstart:
                    continue
                dd = self.steps[d]
                if dd.dma:
                    continue
                if dd.eng == s.eng and not s.dma and (
                        s.eng == "tensor"
                        or not (SAME_ENGINE_SYNC and (s.small or dd.small))):
                    continue
                need.add(d)
        lastof = {}
        for s in new:
            lastof[s.eng] = s.idx
            if s.dma:
                lastof["q:" + s.eng] = s.idx
        for e, i in lastof.items():
            if not e.startswith("q:") and not self.steps[i].dma:
                need.add(i)
        for d in sorted(need):
            dd = self.steps[d]
            if dd.sem is None:
                name = f"c:{dd.eng}"
                self._sem(name)
                dd.sem = name
                dd.inc = 1
                self.cnt[name] += 1
                dd.cum = self.cnt[name]
        bar = {k: v for k, v in self.last_event.items()
               if k != "q:gpsimd"}
        engs_here = [e for e in ENGS if any(s.eng == e for s in new)]
        with self.nc.Block(f"fl{self.nflush}") as block:
            for ename in engs_here:
                def section(eng, ename=ename):
                    first = True
                    for s in new:
                        if s.eng != ename:
                            continue
                        waits = {}
                        if first:
                            for (sn, cv) in bar.values():
                                if sn is not None:
                                    waits[sn] = max(waits.get(sn, 0), cv)
                            first = False
                        if s.dma:
                            pdepth = s.cum - 16 * QDEPTH
                            if pdepth > 0:
                                waits[s.sem] = max(waits.get(s.sem, 0), pdepth)
                        for d in s.deps:
                            dd = self.steps[d]
                            if dd.sem is not None:
                                waits[dd.sem] = max(waits.get(dd.sem, 0), dd.cum)
                            elif d < self.fstart:
                                ev = self.last_event.get(dd.eng)
                                if ev is not None and ev[0] is not None:
                                    waits[ev[0]] = max(waits.get(ev[0], 0), ev[1])
                        hww = self.hw[ename]
                        for sn, v in sorted(waits.items()):
                            if v > hww.get(sn, 0):
                                eng.wait_ge(self.sems[sn], v)
                                hww[sn] = v
                        inst = s.fn(eng)
                        if s.sem is not None:
                            if inst is None:
                                raise RuntimeError("inc step returned no inst")
                            inst.then_inc(self.sems[s.sem], s.inc)
                getattr(block, ename)(section)
        for e, i in lastof.items():
            s = self.steps[i]
            self.last_event[e] = (s.sem, s.cum)
        self.fstart = len(self.steps)
        self.nflush += 1

    def finish(self):
        self.flush()
        bar = dict(self.last_event)
        with self.nc.Block("fin") as block:
            def section(eng):
                for (sn, cv) in sorted(bar.values()):
                    if sn is not None:
                        eng.wait_ge(self.sems[sn], cv)
            block.sync(section)



def _host_prepare(inp):
    f32 = np.float32
    q_w = np.asarray(inp["L0_attn_q_w"], f32)
    k_w = np.asarray(inp["L0_attn_k_w"], f32)
    v_w = np.asarray(inp["L0_attn_v_w"], f32)

    def bd_prod(w):
        t = w.reshape(L, 16, 32, 4, 4)
        out = np.zeros((L, 16, 32, 4, 32, 4), f32)
        for n in range(32):
            out[:, :, n, :, n, :] = t[:, :, n].transpose(0, 1, 3, 2)
        return out.reshape(L, 16, 128, 128)

    def bd_T(w):
        t = w.reshape(L, 16, 32, 4, 4)
        out = np.zeros((L, 16, 32, 4, 32, 4), f32)
        for n in range(32):
            out[:, :, n, :, n, :] = t[:, :, n]
        return out.reshape(L, 16, 128, 128)

    bd = np.stack([bd_prod(q_w), bd_prod(k_w), bd_prod(v_w)], axis=1)
    bd = np.ascontiguousarray(bd.reshape(L, 48, 128, 128))
    bdvT = np.ascontiguousarray(bd_T(v_w))

    ig_w = np.asarray(inp["L0_attn_ig_w"], f32)
    fg_w = np.asarray(inp["L0_attn_fg_w"], f32)
    g8 = np.concatenate([ig_w, fg_w], axis=1)
    gT = np.ascontiguousarray(g8.transpose(0, 2, 1))
    g_qk = np.ascontiguousarray(gT[:, :4096].reshape(L, 32, 128, 8))
    g_v = np.ascontiguousarray(gT[:, 4096:].reshape(L, 16, 128, 8))
    gbias = np.ascontiguousarray(
        np.concatenate([np.asarray(inp["L0_attn_ig_b"], f32),
                        np.asarray(inp["L0_attn_fg_b"], f32)], axis=1).T)

    def cols(v, nchunk):
        a = np.asarray(v, f32).reshape(-1, nchunk, 128)
        return np.ascontiguousarray(a.transpose(2, 0, 1).reshape(128, -1))

    convw = np.asarray(inp["L0_attn_conv_w"], f32)
    convw_c = np.ascontiguousarray(
        convw.reshape(L, 16, 128, KCONV).transpose(2, 0, 1, 3)
        .reshape(128, L * 16 * KCONV))
    convb_c = cols(inp["L0_attn_conv_b"], 16)
    lnw_c = cols(inp["L0_attn_ln_w"], 8)
    skip_c = cols(inp["L0_attn_skip"], 16)
    onorm_c = cols(inp["L0_attn_onorm_w"], 16)
    postw_c = cols(inp["L0_attn_post_w"], 8)
    latb_c = cols(inp["L0_attn_lat_b"], 8)
    b1_c = np.ascontiguousarray(
        np.asarray(inp["L0_ffn_b1"], f32).reshape(EXP, 16, 128)
        .transpose(2, 0, 1).reshape(128, EXP * 16))
    b2_e = np.ascontiguousarray(np.asarray(inp["L0_ffn_b2"], f32))

    up_wT = np.ascontiguousarray(
        np.asarray(inp["L0_attn_up_w"], f32).transpose(0, 2, 1))
    down_wT = np.ascontiguousarray(
        np.asarray(inp["L0_attn_down_w"], f32).transpose(0, 2, 1))
    latT = np.ascontiguousarray(np.asarray(inp["L0_attn_lat_w"], f32).T)
    rT = np.ascontiguousarray(np.asarray(inp["L0_ffn_router_w"], f32).T)
    w1T = np.ascontiguousarray(
        np.asarray(inp["L0_ffn_w1"], f32).transpose(0, 2, 1))
    w2T = np.ascontiguousarray(
        np.asarray(inp["L0_ffn_w2"], f32).transpose(0, 2, 1))

    ident = np.eye(128, dtype=f32)
    masks = np.zeros((4, 128, 512), f32)
    p = np.arange(128)[:, None]
    fi = np.arange(512)[None, :]
    for o in range(4):
        masks[o] = (p <= fi - 128 * o).astype(f32)
    masks = np.ascontiguousarray(masks.transpose(1, 0, 2).reshape(128, 4 * 512))

    shared = dict(
        bd=bd, bdvT=bdvT, g_qk=g_qk, g_v=g_v, gbias=gbias,
        convw=convw_c, convb=convb_c, lnw=lnw_c, skipc=skip_c,
        onormc=onorm_c, postw=postw_c, latb=latb_c, b1c=b1_c, b2e=b2_e,
        up_wT=up_wT, down_wT=down_wT, latT=latT, rT=rT, w1T=w1T, w2T=w2T,
        ident=ident, masks=masks,
    )
    tokens = np.asarray(inp["L0_attn_tokens_batch"], f32)
    in_maps = []
    for core in range(NCORES):
        b = core // 2
        m = dict(shared)
        m["xT"] = np.ascontiguousarray(tokens[b].T)
        in_maps.append(m)
    return in_maps



def build_program(taps=()):
    nc = bass.Bass("TRN2", target_bir_lowering=False)
    em = Em(nc)
    dt = {}

    def din(name, shape):
        dt[name] = nc.dram_tensor(name, list(shape), F32, kind="ExternalInput")

    din("xT", (D, S))
    din("bd", (L, 48, 128, 128))
    din("bdvT", (L, 16, 128, 128))
    din("g_qk", (L, 32, 128, 8))
    din("g_v", (L, 16, 128, 8))
    din("gbias", (8, L))
    din("convw", (128, L * 16 * KCONV))
    din("convb", (128, L * 16))
    din("lnw", (128, L * 8))
    din("skipc", (128, L * 16))
    din("onormc", (128, L * 16))
    din("postw", (128, 8))
    din("latb", (128, 8))
    din("b1c", (128, EXP * 16))
    din("b2e", (EXP, D))
    din("up_wT", (L, D, 2 * INNER))
    din("down_wT", (L, INNER, D))
    din("latT", (D, D))
    din("rT", (D, EXP))
    din("w1T", (EXP, D, FF))
    din("w2T", (EXP, FF, D))
    din("ident", (128, 128))
    din("masks", (128, 4 * 512))

    o_out = nc.dram_tensor("o", [D, SHALF], F32, kind="ExternalOutput")
    o2_out = nc.dram_tensor("o2", [D, SHALF], F32, kind="ExternalOutput")

    def scr(name, shape, dtype):
        dt[name] = nc.dram_tensor(name, list(shape), dtype, kind="Internal")

    scr("sp_x", (D, S), F32)
    scr("sp_x1", (D, S), F32)
    scr("sp_xn", (D, S), BF16)
    scr("sp_z", (INNER, S), BF16)
    scr("sp_xc", (INNER, S), BF16)
    scr("sp_q", (INNER, S), BF16)
    scr("sp_k", (INNER, S), BF16)
    scr("sp_v", (S, INNER), BF16)
    scr("sp_hnT", (INNER, S), BF16)
    scr("sp_g8", (8, S), F32)
    scr("sp_rows", (12, S), F32)
    scr("sp_dbg", (128, 4096), F32)
    scr("sp_dbg2", (128, 8192), F32)
    scr("sp_cmb", (EXP, S), F32)
    scr("sp_acc", (D, S), F32)

    tap_t = {}
    for name in taps:
        h = dt[name]
        tap_t[name] = nc.dram_tensor(
            "tap_" + name, list(h.shape), h.dtype, kind="ExternalOutput")

    ctx = nc.ctx
    identb = ctx.enter_context(nc.sbuf_tensor("identb_s", [128, 128], BF16))
    identf = ctx.enter_context(nc.sbuf_tensor("identf_s", [128, 128], F32))
    masks_sb = ctx.enter_context(nc.sbuf_tensor("masks_s", [128, 4 * 512], BF16))
    ones_r = ctx.enter_context(nc.sbuf_tensor("ones_r", [1, 128], F32))
    ones_c = ctx.enter_context(nc.sbuf_tensor("ones_c", [128, 1], BF16))
    constv = ctx.enter_context(nc.sbuf_tensor("constv", [128, 4], F32))
    lnw_sb = ctx.enter_context(nc.sbuf_tensor("lnw_s", [128, L * 8], F32))
    convw_sb = ctx.enter_context(nc.sbuf_tensor("convw_s", [128, L * 64], F32))
    convb_sb = ctx.enter_context(nc.sbuf_tensor("convb_s", [128, L * 16], F32))
    skip_sb = ctx.enter_context(nc.sbuf_tensor("skipc_s", [128, L * 16], F32))
    onorm_sb = ctx.enter_context(nc.sbuf_tensor("onormc_s", [128, L * 16], F32))
    postw_sb = ctx.enter_context(nc.sbuf_tensor("postw_s", [128, 8], F32))
    latb_sb = ctx.enter_context(nc.sbuf_tensor("latb_s", [128, 8], F32))
    gbias_sb = ctx.enter_context(nc.sbuf_tensor("gbias_s", [8, L], F32))
    b1_sb = ctx.enter_context(nc.sbuf_tensor("b1c_s", [128, EXP * 16], F32))
    b2_sb = ctx.enter_context(nc.sbuf_tensor("b2e_s", [EXP, D], BF16))

    def dma(eng, out_ap, in_ap, reads=(), writes=()):
        return em.step(eng, lambda e, o=out_ap, i=in_ap: e.dma_start(o, i),
                       reads=reads, writes=writes, dma=True)

    dma("gpsimd", identb[:, :], dt["ident"].ap(), writes=["identb"])
    dma("sync", identf[:, :], dt["ident"].ap(), writes=["identf"])
    dma("gpsimd", masks_sb[:, :], dt["masks"].ap(), writes=["masks"])
    dma("sync", lnw_sb[:, :], dt["lnw"].ap(), writes=["lnw"])
    dma("sync", convw_sb[:, :], dt["convw"].ap(), writes=["convw"])
    dma("sync", convb_sb[:, :], dt["convb"].ap(), writes=["convb"])
    dma("sync", skip_sb[:, :], dt["skipc"].ap(), writes=["skipc"])
    dma("sync", onorm_sb[:, :], dt["onormc"].ap(), writes=["onormc"])
    dma("sync", postw_sb[:, :], dt["postw"].ap(), writes=["postw"])
    dma("sync", latb_sb[:, :], dt["latb"].ap(), writes=["latb"])
    dma("sync", gbias_sb[:, :], dt["gbias"].ap(), writes=["gbias"])
    dma("sync", b1_sb[:, :], dt["b1c"].ap(), writes=["b1c"])
    dma("gpsimd", b2_sb[:, :], dt["b2e"].ap(), writes=["b2e"])
    em.step("vector", lambda e: e.memset(ones_r[:, :], 1.0), writes=["ones_r"])
    em.step("vector", lambda e: e.memset(ones_c[:, :], 1.0), writes=["ones_c"])
    em.step("vector", lambda e: e.memset(constv[:, 0:1], 1e-5), writes=["constv"])
    em.step("vector", lambda e: e.memset(constv[:, 1:2], 1.0), writes=["constv"])
    em.step("vector", lambda e: e.memset(constv[:, 2:3], NEG_HALF_LOG_DH),
            writes=["constv"])

    def ln_block_q(pref, xtok, xblk, nch, w, out, otok, scale_col_fn, lb):
        pieces = []
        for c in range(nch):
            pieces.append(lambda c=c: _ln_cast(
                pref, xtok, xblk, nch, w, c, lb))
        pieces.append(lambda: _ln_stats_a(pref, w, lb))
        pieces.append(None)
        pieces.append(lambda: _ln_stats_b(pref, w, lb))
        for c in range(nch):
            pieces.append(lambda c=c: _ln_norm(
                pref, xtok, xblk, nch, w, out, otok, scale_col_fn, c, lb))
        return pieces

    def ln_block(pref, xtok, xblk, nch, w, out, otok, scale_col_fn, lb):
        for p in ln_block_q(pref, xtok, xblk, nch, w, out, otok,
                            scale_col_fn, lb):
            if p is not None:
                p()

    def _ln_cast(pref, xtok, xblk, nch, w, c, lb):
        xb16, sqb = lb["xb16"], lb["sqb"]
        pLN = lb["pLN"]
        ptok = pref + "pLN"
        pool_stats = lb.get("pool_stats", False)
        if True:
            cb = c % 2
            ceng = "gpsimd" if pool_stats else "vector"
            em.step(ceng, lambda e, c=c, cb=cb: e.tensor_copy(
                xb16[:, cb * w:(cb + 1) * w], xblk[:, c * w:(c + 1) * w]),
                reads=[xtok], writes=[pref + "xb16%d" % cb])
            if pool_stats:
                em.step("gpsimd", lambda e, c=c, cb=cb: e.tensor_tensor(
                    sqb[:, cb * w:(cb + 1) * w], xblk[:, c * w:(c + 1) * w],
                    xblk[:, c * w:(c + 1) * w], ALU.mult),
                    reads=[xtok], writes=[pref + "sqb%d" % cb])
            else:
                em.step("scalar", lambda e, c=c, cb=cb: e.activation(
                    sqb[:, cb * w:(cb + 1) * w],
                    xblk[:, c * w:(c + 1) * w], AF.Square),
                    reads=[xtok], writes=[pref + "sqb%d" % cb])

            def f_s(e, c=c, cb=cb):
                i = e.matmul(pLN[0:1, 0:w], ones_c[:, :],
                             xb16[:, cb * w:(cb + 1) * w],
                             start=(c == 0), stop=(c == nch - 1))
                i = e.matmul(pLN[0:1, w:2 * w], ones_c[:, :],
                             sqb[:, cb * w:(cb + 1) * w],
                             start=(c == 0), stop=(c == nch - 1))
                return i
            em.step("tensor", f_s,
                    reads=[pref + "xb16%d" % cb, pref + "sqb%d" % cb,
                           "ones_c"],
                    writes=[ptok])

    def _ln_stats_a(pref, w, lb):
        pLN = lb["pLN"]
        rowmu, rowrs, rowt = lb["rowmu"], lb["rowrs"], lb["rowt"]
        ptok = pref + "pLN"
        em.step("vector", lambda e: e.tensor_scalar(
            rowmu[:, :], pLN[0:1, 0:w], 1.0 / D, None, ALU.mult),
            reads=[ptok], writes=[pref + "rowmu"], small=True)
        em.step("vector", lambda e: e.tensor_scalar(
            rowt[:, :], pLN[0:1, w:2 * w], 1.0 / D, None, ALU.mult),
            reads=[ptok], writes=[pref + "rowt"], small=True)
        em.step("vector", lambda e: e.tensor_tensor(
            rowrs[:, :], rowmu[:, :], rowmu[:, :], ALU.mult),
            reads=[pref + "rowmu"], writes=[pref + "rowrs"], small=True)
        em.step("vector", lambda e: e.tensor_tensor(
            rowt[:, :], rowt[:, :], rowrs[:, :], ALU.subtract),
            reads=[pref + "rowt", pref + "rowrs"], writes=[pref + "rowt"], small=True)
        em.step("scalar", lambda e: e.activation(
            rowrs[:, :], rowt[:, :], AF.Sqrt, bias=constv[0:1, 0:1]),
            reads=[pref + "rowt", "constv"], writes=[pref + "rowrs"], small=True)
        em.step("vector", lambda e: e.reciprocal(rowrs[:, :], rowrs[:, :]),
                reads=[pref + "rowrs"], writes=[pref + "rowrs"], small=True)

    def _ln_stats_b(pref, w, lb):
        pB = lb["pB"]
        rowmu, rowrs = lb["rowmu"], lb["rowrs"]
        muB, rstdB = lb["muB"], lb["rstdB"]
        pbtok = lb.get("pbtok", pref + "pB")
        em.step("tensor", lambda e: e.matmul(
            pB[:, 0:w], ones_r[0:1, :], rowmu[:, :], start=True, stop=True),
            reads=[pref + "rowmu", "ones_r"], writes=[pbtok], small=True)
        em.step("scalar", lambda e: e.activation(
            muB[:, :], pB[:, 0:w], AF.Copy), reads=[pbtok],
            writes=[pref + "muB"])
        em.step("tensor", lambda e: e.matmul(
            pB[:, 0:w], ones_r[0:1, :], rowrs[:, :], start=True, stop=True),
            reads=[pref + "rowrs", "ones_r", pref + "muB"], writes=[pbtok], small=True)
        em.step("scalar", lambda e: e.activation(
            rstdB[:, :], pB[:, 0:w], AF.Copy), reads=[pbtok],
            writes=[pref + "rstdB"])

    def _ln_norm(pref, xtok, xblk, nch, w, out, otok, scale_col_fn, c, lb):
        rstdB, muB, t1 = lb["rstdB"], lb["muB"], lb["t1"]
        em.step("vector", lambda e, c=c: e.tensor_tensor(
            t1[:, :], xblk[:, c * w:(c + 1) * w], muB[:, :],
            ALU.subtract), reads=[xtok, pref + "muB"],
            writes=[pref + "t1"])
        sc = scale_col_fn(c) if scale_col_fn else None
        if sc is not None:
            em.step("vector", lambda e, c=c, sc=sc: e.scalar_tensor_tensor(
                out[:, c * w:(c + 1) * w], t1[:, :], sc, rstdB[:, :],
                ALU.mult, ALU.mult),
                reads=[pref + "t1", pref + "rstdB"], writes=[otok])
        else:
            em.step("vector", lambda e, c=c: e.tensor_tensor(
                out[:, c * w:(c + 1) * w], t1[:, :], rstdB[:, :],
                ALU.mult), reads=[pref + "t1", pref + "rstdB"],
                writes=[otok])

    spx_r = dt["sp_x"].ap().rearrange("(c p) t -> p c t", p=128)
    spx1_r = dt["sp_x1"].ap().rearrange("(c p) t -> p c t", p=128)
    spxn_r = dt["sp_xn"].ap().rearrange("(c p) t -> p c t", p=128)
    spz_r = dt["sp_z"].ap().rearrange("(c p) t -> p c t", p=128)
    spxc_r = dt["sp_xc"].ap().rearrange("(c p) t -> p c t", p=128)
    spq_r = dt["sp_q"].ap().rearrange("(c p) t -> p c t", p=128)
    spk_r = dt["sp_k"].ap().rearrange("(c p) t -> p c t", p=128)
    spv_r = dt["sp_v"].ap().rearrange("(s p) c -> p s c", p=128)
    sphnT_r = dt["sp_hnT"].ap().rearrange("(c p) t -> p c t", p=128)
    xT_r = dt["xT"].ap().rearrange("(c p) t -> p c t", p=128)
    o_r = o_out.ap().rearrange("(c p) t -> p c t", p=128)
    o2_r = o2_out.ap().rearrange("(c p) t -> p c t", p=128)

    with ExitStack() as _st:
        xblk = _st.enter_context(nc.sbuf_tensor("pp_x", [128, 2 * 8 * TBC], F32))
        oblk = _st.enter_context(nc.sbuf_tensor("pp_o", [128, 2 * 8 * TBC], F32))
        xb16 = _st.enter_context(nc.sbuf_tensor("pp_b16", [128, 2 * TBC], BF16))
        sqb = _st.enter_context(nc.sbuf_tensor("pp_sq", [128, 2 * TBC], BF16))
        rowmu = _st.enter_context(nc.sbuf_tensor("pp_rmu", [1, TBC], F32))
        rowrs = _st.enter_context(nc.sbuf_tensor("pp_rrs", [1, TBC], F32))
        rowt = _st.enter_context(nc.sbuf_tensor("pp_rt", [1, TBC], F32))
        muB = _st.enter_context(nc.sbuf_tensor("pp_muB", [128, TBC], F32))
        rstdB = _st.enter_context(nc.sbuf_tensor("pp_rsB", [128, TBC], F32))
        t1 = _st.enter_context(nc.sbuf_tensor("pp_t1", [128, TBC], F32))
        pLN = _st.enter_context(nc.psum_tensor("pp_pLN", [1, 2 * TBC], F32))
        pB = _st.enter_context(nc.psum_tensor("pp_pB", [128, TBC], F32))
        lb = dict(xb16=xb16, sqb=sqb, pLN=pLN, rowmu=rowmu, rowrs=rowrs,
                  rowt=rowt, pB=pB, muB=muB, rstdB=rstdB, t1=t1,
                  pool_stats=POOL_STATS)

        def pp_load(tb):
            par = tb % 2
            t0 = tb * TBC
            dma("scalar",
                xblk.ap().rearrange("p (b c t) -> p b c t", b=2, t=TBC)
                [:, par], xT_r[:, :, t0:t0 + TBC],
                writes=["pp_x%d" % par])
        pp_load(0)
        for tb in range(NTBC):
            t0 = tb * TBC
            par = tb % 2
            if tb + 1 < NTBC:
                pp_load(tb + 1)
            xv = xblk[:, par * 8 * TBC:(par + 1) * 8 * TBC]
            ov = oblk[:, par * 8 * TBC:(par + 1) * 8 * TBC]
            ln_block("pp", "pp_x%d" % par, xv, 8, TBC, ov,
                     "pp_o%d" % par, None, lb)
            dma("sync", spx_r[:, :, t0:t0 + TBC],
                ov.rearrange("p (c t) -> p c t", t=TBC),
                reads=["pp_o%d" % par], writes=["d:sp_x"])
        em.flush()

    for l in range(L_EFF):
        with ExitStack() as _st:
            uw1 = _st.enter_context(nc.sbuf_tensor(f"a_uw1{l}", [128, 8 * INNER], BF16))
            uw2 = _st.enter_context(nc.sbuf_tensor(f"a_uw2{l}", [128, 8 * INNER], BF16))
            bd_sb = _st.enter_context(nc.sbuf_tensor(f"a_bd{l}", [128, 48 * 128], BF16))
            gqk_sb = _st.enter_context(nc.sbuf_tensor(f"a_gqk{l}", [128, 32 * 8], BF16))
            bdvT_sb = _st.enter_context(nc.sbuf_tensor(f"a_bdvT{l}", [128, 16 * 128], BF16))
            gvr_sb = _st.enter_context(nc.sbuf_tensor(f"a_gvr{l}", [128, 16 * 8], BF16))
            gtv_sb = _st.enter_context(nc.sbuf_tensor(f"a_gtv{l}", [128, 16 * 8], BF16))
            xblk = _st.enter_context(nc.sbuf_tensor(f"a_x{l}", [128, 2 * 8 * TBA], F32))
            xnb = _st.enter_context(nc.sbuf_tensor(f"a_xnb{l}", [128, 2 * 8 * TBA], BF16))
            xb16 = _st.enter_context(nc.sbuf_tensor(f"a_b16{l}", [128, 2 * TBA], BF16))
            sqb = _st.enter_context(nc.sbuf_tensor(f"a_sq{l}", [128, 2 * TBA], BF16))
            rowmu = _st.enter_context(nc.sbuf_tensor(f"a_rmu{l}", [1, TBA], F32))
            rowrs = _st.enter_context(nc.sbuf_tensor(f"a_rrs{l}", [1, TBA], F32))
            rowt = _st.enter_context(nc.sbuf_tensor(f"a_rt{l}", [1, TBA], F32))
            muB = _st.enter_context(nc.sbuf_tensor(f"a_muB{l}", [128, TBA], F32))
            rstdB = _st.enter_context(nc.sbuf_tensor(f"a_rsB{l}", [128, TBA], F32))
            t1 = _st.enter_context(nc.sbuf_tensor(f"a_t1{l}", [128, TBA], F32))
            xmb = _st.enter_context(nc.sbuf_tensor(f"a_xm{l}", [128, 16 * (TBA + 3)], BF16))
            xcb = _st.enter_context(nc.sbuf_tensor(f"a_xc{l}", [128, 16 * TBA], BF16))
            cv1 = _st.enter_context(nc.sbuf_tensor(f"a_cv1{l}", [128, 2 * TBA], F32))
            cv2 = _st.enter_context(nc.sbuf_tensor(f"a_cv2{l}", [128, 2 * TBA], F32))
            qb = _st.enter_context(nc.sbuf_tensor(f"a_q{l}", [128, 16 * TBA], BF16))
            kb = _st.enter_context(nc.sbuf_tensor(f"a_k{l}", [128, 16 * TBA], BF16))
            vb = _st.enter_context(nc.sbuf_tensor(f"a_v{l}", [128, 2 * INNER], BF16))
            gates_a = _st.enter_context(nc.sbuf_tensor(f"a_g{l}", [8, TBA], F32))
            zb = _st.enter_context(nc.sbuf_tensor(f"a_zb{l}", [128, 8 * TBC], BF16))
            xn2 = _st.enter_context(nc.sbuf_tensor(f"a_xn2{l}", [128, 8 * TBC], BF16))
            pLN = _st.enter_context(nc.psum_tensor(f"a_pLN{l}", [1, 2 * TBA], F32))
            pB = _st.enter_context(nc.psum_tensor(f"a_pB{l}", [128, TBA], F32))
            pU0 = _st.enter_context(nc.psum_tensor(f"a_pU0{l}", [128, 2 * TBA], F32))
            pU1 = _st.enter_context(nc.psum_tensor(f"a_pU1{l}", [128, 2 * TBA], F32))
            pU2 = _st.enter_context(nc.psum_tensor(f"a_pU2{l}", [128, 2 * TBA], F32))
            pUs = [pU0, pU1, pU2]
            pUts = [f"a_pU0", f"a_pU1", f"a_pU2"]
            pVa = _st.enter_context(nc.psum_tensor(f"a_pVa{l}", [128, 512], F32))
            pVb = _st.enter_context(nc.psum_tensor(f"a_pVb{l}", [128, 512], F32))
            pV2 = _st.enter_context(nc.psum_tensor(f"a_pV2{l}", [128, 256], F32))
            lb = dict(xb16=xb16, sqb=sqb, pLN=pLN, rowmu=rowmu, rowrs=rowrs,
                      rowt=rowt, pB=pB, muB=muB, rstdB=rstdB, t1=t1,
                      pool_stats=POOL_STATS)
            dma("gpsimd", bdvT_sb[:, :].rearrange("p (j m) -> p j m", m=128),
                dt["bdvT"].ap()[l].rearrange("j p m -> p j m"),
                writes=["bdvT"])
            dma("gpsimd", gvr_sb[:, :].rearrange("p (j m) -> p j m", m=8),
                dt["g_v"].ap()[l].rearrange("j p m -> p j m"), writes=["gvr"])
            dma("gpsimd", gqk_sb[:, :].rearrange("p (j m) -> p j m", m=8),
                dt["g_qk"].ap()[l].rearrange("j p m -> p j m"),
                writes=["gqk"])
            for cq in range(4):
                dma("gpsimd",
                    uw1[:, :].rearrange("p (k m) -> p k m", m=INNER)
                    [:, :, cq * 512:(cq + 1) * 512],
                    dt["up_wT"].ap()[l].rearrange("(k p) m -> p k m", p=128)
                    [:, :, cq * 512:(cq + 1) * 512],
                    writes=["uw1c%d" % cq])
                if cq == 0:
                    dma("gpsimd",
                        bd_sb[:, :].rearrange("p (j m) -> p j m", m=128),
                        dt["bd"].ap()[l].rearrange("j p m -> p j m"),
                        writes=["bd"])
            dma("gpsimd", uw2[:, :].rearrange("p (k m) -> p k m", m=INNER),
                dt["up_wT"].ap()[l].rearrange("(k p) m -> p k m", p=128)
                [:, :, INNER:2 * INNER], writes=["uw2"])
            for c in range(16):
                pX = pVa if c % 2 == 0 else pVb
                pxt = "a_pVa" if c % 2 == 0 else "a_pVb"
                em.step("tensor", lambda e, c=c, pX=pX: e.matmul(
                    pX[:, 0:8], bdvT_sb[:, c * 128:(c + 1) * 128],
                    gvr_sb[:, c * 8:(c + 1) * 8], start=True, stop=True),
                    reads=["bdvT", "gvr", "gtv"], writes=[pxt])
                em.step("scalar", lambda e, c=c, pX=pX: e.activation(
                    gtv_sb[:, c * 8:(c + 1) * 8], pX[:, 0:8], AF.Copy),
                    reads=[pxt], writes=["gtv"])

            TW = TBA + 3

            def a_ln_q(tb):
                par = tb % 2
                t0 = tb * TBA
                xv = xblk[:, par * 8 * TBA:(par + 1) * 8 * TBA]
                xnv = xnb[:, par * 8 * TBA:(par + 1) * 8 * TBA]

                def p_load(par=par, t0=t0, xv=xv):
                    dma("sync", xv.rearrange("p (c t) -> p c t", t=TBA),
                        spx_r[:, :, t0:t0 + TBA], reads=["d:sp_x"],
                        writes=["a_x%d" % par])

                def p_store(par=par, t0=t0, xnv=xnv):
                    dma("sync", spxn_r[:, :, t0:t0 + TBA],
                        xnv.rearrange("p (c t) -> p c t", t=TBA),
                        reads=["a_xn%d" % par], writes=["d:sp_xn"])
                return ([p_load] +
                        ln_block_q("a", "a_x%d" % par, xv, 8, TBA, xnv,
                                   "a_xn%d" % par,
                                   (lambda c: lnw_sb[:,
                                              l * 8 + c:l * 8 + c + 1]),
                                   lb) +
                        [p_store])

            def a_ln(tb):
                for p in a_ln_q(tb):
                    if p is not None:
                        p()
            a_ln(0)
            for tb in range(NTBA):
                t0 = tb * TBA
                par = tb % 2
                xnv = xnb[:, par * 8 * TBA:(par + 1) * 8 * TBA]
                lnq = a_ln_q(tb + 1) if tb + 1 < NTBA else []
                for c in range(16):
                    if tb == 0:
                        em.step("vector", lambda e, c=c: e.memset(
                            xmb[:, c * TW:c * TW + 3], 0.0),
                            reads=["a_xm%d" % c], writes=["a_xm%d" % c])
                    else:
                        em.step("vector", lambda e, c=c: e.tensor_copy(
                            xmb[:, c * TW:c * TW + 3],
                            xmb[:, c * TW + TBA:c * TW + TBA + 3]),
                            reads=["a_xm%d" % c], writes=["a_xm%d" % c])
                def qkv_group(mq):
                    pV = pVa if mq % 2 == 0 else pVb
                    pvt = "a_pVa" if mq % 2 == 0 else "a_pVb"
                    vp = mq % 2

                    def f_qkm(e, m=mq, pV=pV):
                        e.matmul(pV[:, 0:TBA],
                                 bd_sb[:, m * 128:(m + 1) * 128],
                                 xcb[:, m * TBA:(m + 1) * TBA],
                                 start=True, stop=True)
                        i = e.matmul(pV[:, TBA:2 * TBA],
                                     bd_sb[:, (16 + m) * 128:
                                           (17 + m) * 128],
                                     xcb[:, m * TBA:(m + 1) * TBA],
                                     start=True, stop=True)
                        return i
                    em.step("tensor", f_qkm, reads=["bd", "a_xc%d" % mq],
                            writes=[pvt])
                    em.step("scalar", lambda e, m=mq, pV=pV: e.activation(
                        qb[:, m * TBA:(m + 1) * TBA], pV[:, 0:TBA],
                        AF.Copy), reads=[pvt], writes=["a_q"])
                    em.step("scalar", lambda e, m=mq, pV=pV: e.activation(
                        kb[:, m * TBA:(m + 1) * TBA], pV[:, TBA:2 * TBA],
                        AF.Copy), reads=[pvt], writes=["a_k"])

                    def f_vm(e, m=mq):
                        i = None
                        for sub in range(2):
                            i = e.matmul(
                                pV2[:, sub * 128:(sub + 1) * 128],
                                xmb[:, m * TW + 3 + sub * 128:
                                    m * TW + 3 + sub * 128 + 128],
                                bd_sb[:, (32 + m) * 128:(33 + m) * 128],
                                start=True, stop=True)
                        return i
                    em.step("tensor", f_vm, reads=["a_xm%d" % mq, "bd"],
                            writes=["a_pV2"])
                    em.step("scalar", lambda e, m=mq: e.activation(
                        vb.ap().rearrange("p (i n) -> p i n", n=INNER)
                        [:, :, m * 128:(m + 1) * 128],
                        pV2[:, :].rearrange("p (i s) -> p i s", s=128),
                        AF.Copy), reads=["a_pV2"], writes=["a_v"])

                for m in range(16):
                    pU = pUs[m % 3]
                    ptok = pUts[m % 3]

                    def f_up(e, m=m, pU=pU, xnv=xnv):
                        i = None
                        for k in range(8):
                            i = e.matmul(pU[:, 0:TBA],
                                         uw1[:, k * INNER + m * 128:
                                             k * INNER + m * 128 + 128],
                                         xnv[:, k * TBA:(k + 1) * TBA],
                                         start=(k == 0), stop=(k == 7))
                        return i
                    em.step("tensor", f_up,
                            reads=["uw1c%d" % (m // 4), "a_xn%d" % par],
                            writes=[ptok])
                    em.step("scalar", lambda e, m=m, pU=pU: e.activation(
                        xmb[:, m * TW + 3:m * TW + 3 + TBA], pU[:, 0:TBA],
                        AF.Copy), reads=[ptok],
                        writes=["a_xm%d" % m, "a_xm"])
                    if m >= 2:
                        qkv_group(m - 2)
                    base = m * TW
                    wl0 = (l * 16 + m) * KCONV
                    cp = m % 2
                    co = cp * TBA
                    em.step("vector", lambda e, base=base, wl0=wl0, co=co:
                            e.tensor_scalar(
                                cv1[:, co:co + TBA], xmb[:, base:base + TBA],
                                convw_sb[:, wl0:wl0 + 1], None, ALU.mult),
                            reads=["a_xm%d" % m, "convw"],
                            writes=["a_cv1%d" % cp])
                    em.step("vector", lambda e, base=base, wl0=wl0, co=co:
                            e.scalar_tensor_tensor(
                                cv2[:, co:co + TBA],
                                xmb[:, base + 1:base + 1 + TBA],
                                convw_sb[:, wl0 + 1:wl0 + 2],
                                cv1[:, co:co + TBA], ALU.mult, ALU.add),
                            reads=["a_xm%d" % m, "convw", "a_cv1%d" % cp],
                            writes=["a_cv2%d" % cp])
                    em.step("vector", lambda e, base=base, wl0=wl0, co=co:
                            e.scalar_tensor_tensor(
                                cv1[:, co:co + TBA],
                                xmb[:, base + 2:base + 2 + TBA],
                                convw_sb[:, wl0 + 2:wl0 + 3],
                                cv2[:, co:co + TBA], ALU.mult, ALU.add),
                            reads=["a_xm%d" % m, "convw", "a_cv2%d" % cp],
                            writes=["a_cv1%d" % cp])
                    em.step("vector", lambda e, base=base, wl0=wl0, co=co:
                            e.scalar_tensor_tensor(
                                cv2[:, co:co + TBA],
                                xmb[:, base + 3:base + 3 + TBA],
                                convw_sb[:, wl0 + 3:wl0 + 4],
                                cv1[:, co:co + TBA], ALU.mult, ALU.add),
                            reads=["a_xm%d" % m, "convw", "a_cv1%d" % cp],
                            writes=["a_cv2%d" % cp])
                    em.step("scalar", lambda e, m=m, co=co: e.activation(
                        xcb[:, m * TBA:(m + 1) * TBA], cv2[:, co:co + TBA],
                        AF.Silu,
                        bias=convb_sb[:, l * 16 + m:l * 16 + m + 1]),
                        reads=["a_cv2%d" % cp, "convb"],
                        writes=["a_xc%d" % m, "a_xc"])
                    if lnq and m < 14:
                        p = lnq.pop(0)
                        if p is not None:
                            p()
                for p in lnq:
                    if p is not None:
                        p()
                lnq = []
                qkv_group(14)
                qkv_group(15)

                def f_g(e):
                    i = None
                    for c in range(16):
                        i = e.matmul(pB[0:8, 0:TBA],
                                     gqk_sb[:, c * 8:(c + 1) * 8],
                                     qb[:, c * TBA:(c + 1) * TBA],
                                     start=(c == 0), stop=False)
                    for c in range(16):
                        i = e.matmul(pB[0:8, 0:TBA],
                                     gqk_sb[:, (16 + c) * 8:(17 + c) * 8],
                                     kb[:, c * TBA:(c + 1) * TBA],
                                     start=False, stop=False)
                    for c in range(16):
                        i = e.matmul(pB[0:8, 0:TBA],
                                     gtv_sb[:, c * 8:(c + 1) * 8],
                                     xmb[:, c * TW + 3:c * TW + 3 + TBA],
                                     start=False, stop=(c == 15))
                    return i
                em.step("tensor", f_g,
                        reads=["a_q", "a_k", "a_xm", "gqk", "gtv"] +
                              ["a_xm%d" % c for c in range(16)],
                        writes=["a_pB"])
                em.step("vector", lambda e: e.tensor_scalar(
                    gates_a[:, :], pB[0:8, 0:TBA],
                    gbias_sb[0:8, l:l + 1], None,
                    ALU.add), reads=["a_pB", "gbias"], writes=["a_gates"], small=True)
                dma("sync", dt["sp_g8"].ap()[:, t0:t0 + TBA], gates_a[:, :],
                    reads=["a_gates"], writes=["d:sp_g8"])
                dma("scalar", spxc_r[:, :, t0:t0 + TBA],
                    xcb.ap().rearrange("p (c t) -> p c t", t=TBA),
                    reads=["a_xc"], writes=["d:sp_xc"])
                dma("scalar", spq_r[:, :, t0:t0 + TBA],
                    qb.ap().rearrange("p (c t) -> p c t", t=TBA),
                    reads=["a_q"], writes=["d:sp_q"])
                dma("scalar", spk_r[:, :, t0:t0 + TBA],
                    kb.ap().rearrange("p (c t) -> p c t", t=TBA),
                    reads=["a_k"], writes=["d:sp_k"])
                dma("scalar", spv_r[:, t0 // 128:t0 // 128 + 2, :],
                    vb.ap().rearrange("p (s c) -> p s c", c=INNER),
                    reads=["a_v"], writes=["d:sp_v"])
            for tb in range(NTBC):
                t0 = tb * TBC
                dma("scalar", xn2.ap().rearrange("p (c t) -> p c t", t=TBC),
                    spxn_r[:, :, t0:t0 + TBC], reads=["d:sp_xn"],
                    writes=["a_xn2"])
                for m in range(16):
                    pU = pUs[m % 3]
                    ptok = pUts[m % 3]

                    def f_up2(e, m=m, pU=pU):
                        i = None
                        for k in range(8):
                            i = e.matmul(pU[:, 0:TBC],
                                         uw2[:, k * INNER + m * 128:
                                             k * INNER + m * 128 + 128],
                                         xn2[:, k * TBC:(k + 1) * TBC],
                                         start=(k == 0), stop=(k == 7))
                        return i
                    em.step("tensor", f_up2, reads=["uw2", "a_xn2"],
                            writes=[ptok])
                    em.step("scalar", lambda e, m=m, pU=pU: e.activation(
                        zb[:, (m % 8) * TBC:(m % 8 + 1) * TBC], pU[:, 0:TBC],
                        AF.Copy), reads=[ptok], writes=["a_zb"])
                    if m % 8 == 7:
                        g0 = (m // 8) * 8
                        dma("sync", spz_r[:, g0:g0 + 8, t0:t0 + TBC],
                            zb.ap().rearrange("p (c t) -> p c t", t=TBC),
                            reads=["a_zb"], writes=["d:sp_z"])
            em.flush()

        dwst = ExitStack()
        dwA = dwst.enter_context(
            nc.sbuf_tensor(f"c_dwA{l}", [128, 8 * D], BF16))
        kvst = ExitStack()
        k_sb = kvst.enter_context(
            nc.sbuf_tensor(f"b_k{l}", [128, 4 * S], BF16))
        v_sb = kvst.enter_context(
            nc.sbuf_tensor(f"b_v{l}", [128, 32 * DH], BF16))

        with ExitStack() as _st:
            ig4 = _st.enter_context(nc.sbuf_tensor(f"s_ig{l}", [4, S], F32))
            fg4 = _st.enter_context(nc.sbuf_tensor(f"s_fg{l}", [4, S], F32))
            st1 = _st.enter_context(nc.sbuf_tensor(f"s_t1{l}", [4, S], F32))
            st2 = _st.enter_context(nc.sbuf_tensor(f"s_t2{l}", [4, S], F32))
            szero = _st.enter_context(nc.sbuf_tensor(f"s_zero{l}", [4, S], F32))
            for ch in range(4):
                dma("gpsimd",
                    k_sb.ap().rearrange("p (c t) -> p c t", t=S)
                    [:, :, ch * 1024:(ch + 1) * 1024],
                    spk_r[:, 0:4, ch * 1024:(ch + 1) * 1024],
                    reads=["d:sp_k"], writes=["b_kc%d" % ch])
                dma("gpsimd",
                    v_sb.ap().rearrange("p (s c) -> p s c", c=DH)
                    [:, ch * 8:(ch + 1) * 8, :],
                    spv_r[:, ch * 8:(ch + 1) * 8, 0:DH],
                    reads=["d:sp_v"], writes=["b_vc%d" % ch])
            dma("gpsimd", dwA.ap().rearrange("p (k m) -> p k m", m=D),
                dt["down_wT"].ap()[l].rearrange("(k p) m -> p k m", p=128)
                [:, 0:8, :], writes=["dwA"])
            dma("sync", ig4[:, :], dt["sp_g8"].ap()[0:4, :],
                reads=["d:sp_g8"], writes=["s_ig"])
            dma("sync", fg4[:, :], dt["sp_g8"].ap()[4:8, :],
                reads=["d:sp_g8"], writes=["s_fg"])
            em.step("vector", lambda e: e.memset(szero[:, :], 0.0),
                    writes=["s_zero"])
            em.step("scalar", lambda e: e.activation(
                st1[:, :], fg4[:, :], AF.Abs),
                reads=["s_fg"], writes=["s_t1"], small=True)
            em.step("scalar", lambda e: e.activation(
                st1[:, :], st1[:, :], AF.Exp, scale=-1.0),
                reads=["s_t1"], writes=["s_t1"], small=True)
            em.step("scalar", lambda e: e.activation(
                st1[:, :], st1[:, :], AF.Ln, bias=constv[0:4, 1:2]),
                reads=["s_t1", "constv"], writes=["s_t1"], small=True)
            em.step("vector", lambda e: e.tensor_scalar(
                st2[:, :], fg4[:, :], 0.0, None, ALU.min),
                reads=["s_fg"], writes=["s_t2"], small=True)
            em.step("vector", lambda e: e.tensor_tensor(
                st2[:, :], st2[:, :], st1[:, :], ALU.subtract),
                reads=["s_t2", "s_t1"], writes=["s_t2"], small=True)
            em.step("vector", lambda e: e.tensor_tensor_scan(
                st1[:, :], st2[:, :], szero[:, :], 0.0, ALU.add, ALU.add),
                reads=["s_t2", "s_zero"], writes=["s_t1"], small=True)
            em.step("vector", lambda e: e.tensor_tensor(
                fg4[:, :], ig4[:, :], st1[:, :], ALU.subtract),
                reads=["s_ig", "s_t1"], writes=["s_fg"], small=True)
            dma("sync", dt["sp_rows"].ap()[0:4, :], fg4[:, :],
                reads=["s_fg"], writes=["d:sp_rows"])
            em.step("vector", lambda e: e.tensor_tensor_scan(
                st2[:, :], fg4[:, :], szero[:, :], -3.0e38, ALU.max, ALU.add),
                reads=["s_fg", "s_zero"], writes=["s_t2"], small=True)
            dma("sync", dt["sp_rows"].ap()[8:12, :], st2[:, :],
                reads=["s_t2"], writes=["d:sp_rows"])
            em.step("vector", lambda e: e.tensor_tensor(
                st1[:, :], st1[:, :], st2[:, :], ALU.add),
                reads=["s_t1", "s_t2"], writes=["s_t1"], small=True)
            em.step("scalar", lambda e: e.activation(
                st1[:, :], st1[:, :], AF.Exp, scale=-1.0),
                reads=["s_t1"], writes=["s_t1"], small=True)
            dma("sync", dt["sp_rows"].ap()[4:8, :], st1[:, :],
                reads=["s_t1"], writes=["d:sp_rows"])
            em.flush()

        with ExitStack() as _st:
            q_sb = _st.enter_context(nc.sbuf_tensor(f"b_q{l}", [128, 2 * 4 * 512], BF16))
            mrow = _st.enter_context(nc.sbuf_tensor(f"b_mr{l}", [1, 2 * S], F32))
            MB = _st.enter_context(nc.sbuf_tensor(f"b_MB{l}", [128, 2 * 512], F32))
            g32 = _st.enter_context(nc.sbuf_tensor(f"b_g32{l}", [32, 2 * 128], F32))
            gcol = _st.enter_context(nc.sbuf_tensor(f"b_gcol{l}", [128, 2 * 32], F32))
            gcol2 = _st.enter_context(nc.sbuf_tensor(f"b_gcol2{l}", [128, 2 * 32], F32))
            ecol = _st.enter_context(nc.sbuf_tensor(f"b_ecol{l}", [128, 2 * 32], F32))
            epsb = _st.enter_context(nc.sbuf_tensor(f"b_eps{l}", [128, 3 * 512], F32))
            Eb = _st.enter_context(nc.sbuf_tensor(f"b_E{l}", [128, 3 * 512], BF16))
            ctb = _st.enter_context(nc.sbuf_tensor(f"b_ct{l}", [128, 3 * 512], BF16))
            nb = _st.enter_context(nc.sbuf_tensor(f"b_nb{l}", [128, 4], F32))
            ninv = _st.enter_context(nc.sbuf_tensor(f"b_ninv{l}", [128, 4], F32))
            stat = _st.enter_context(nc.sbuf_tensor(f"b_stat{l}", [128, 24], F32))
            hf = _st.enter_context(nc.sbuf_tensor(f"b_hf{l}", [128, 4 * 512], F32))
            hsq = _st.enter_context(nc.sbuf_tensor(f"b_hsq{l}", [128, 512], F32))
            hn = _st.enter_context(nc.sbuf_tensor(f"b_hn{l}", [128, 4 * 512], F32))
            hT = _st.enter_context(nc.sbuf_tensor(f"b_hT{l}", [128, 2048], BF16))
            pC0 = _st.enter_context(nc.psum_tensor(f"b_pC0{l}", [128, 512], F32))
            pC1 = _st.enter_context(nc.psum_tensor(f"b_pC1{l}", [128, 512], F32))
            pH0 = _st.enter_context(nc.psum_tensor(f"b_pH0{l}", [128, 512], F32))
            pH1 = _st.enter_context(nc.psum_tensor(f"b_pH1{l}", [128, 512], F32))
            pH2 = _st.enter_context(nc.psum_tensor(f"b_pH2{l}", [128, 512], F32))
            pH3 = _st.enter_context(nc.psum_tensor(f"b_pH3{l}", [128, 512], F32))
            pP = _st.enter_context(nc.psum_tensor(f"b_pP{l}", [128, 512], F32))
            pT = _st.enter_context(nc.psum_tensor(f"b_pT{l}", [128, 512], F32))
            pH = [pH0, pH1, pH2, pH3]
            pHt = ["b_pH0", "b_pH1", "b_pH2", "b_pH3"]
            pC = [pC0, pC1]
            pCt = ["b_pC0", "b_pC1"]
            def head_prologue(h):
                hp = h % 2
                dma("sync", mrow[0:1, hp * S:(hp + 1) * S],
                    dt["sp_rows"].ap()[8 + h:9 + h, :],
                    reads=["d:sp_rows"], writes=["b_mr%d" % hp])
                em.step("vector", lambda e, hp=hp: e.tensor_scalar(
                    mrow[0:1, hp * S:(hp + 1) * S],
                    mrow[0:1, hp * S:(hp + 1) * S], -1.0, None,
                    ALU.mult),
                    reads=["b_mr%d" % hp], writes=["b_mr%d" % hp],
                    small=True)
                for (src0, dst, dtok) in ((0, gcol, "b_gcol"),
                                          (4, ecol, "b_ecol")):
                    dma("sync", g32[:, hp * 128:(hp + 1) * 128],
                        dt["sp_rows"].ap()[src0 + h:src0 + h + 1, :]
                        .rearrange("r (j f) -> (r j) f", f=128),
                        reads=["d:sp_rows"], writes=["b_g32%d" % hp])
                    em.step("tensor", lambda e, hp=hp: e.transpose(
                        pT[:, 0:32], g32[:, hp * 128:(hp + 1) * 128],
                        identf[0:32, 0:32]),
                        reads=["b_g32%d" % hp, "identf"], writes=["b_pT"])
                    em.step("scalar", lambda e, dst=dst, hp=hp, dtok=dtok:
                            e.activation(
                                dst[:, hp * 32:(hp + 1) * 32],
                                pT[:, 0:32], AF.Copy),
                            reads=["b_pT"], writes=[dtok + str(hp)])
                em.step("vector", lambda e, hp=hp: e.tensor_scalar(
                    gcol2[:, hp * 32:(hp + 1) * 32],
                    gcol[:, hp * 32:(hp + 1) * 32], NEG_HALF_LOG_DH, None,
                    ALU.add),
                    reads=["b_gcol%d" % hp], writes=["b_gcol2%d" % hp],
                    small=True)

            def emit_mb(I, h):
                hp = h % 2
                mp = I % 2
                em.step("tensor", lambda e, I=I, hp=hp: e.matmul(
                    pT[:, :], ones_r[0:1, :],
                    mrow[0:1, hp * S + I * 512:hp * S + (I + 1) * 512],
                    start=True, stop=True),
                    reads=["b_mr%d" % hp, "ones_r", "b_MBc%d" % mp],
                    writes=["b_pT"], small=True)
                em.step("scalar", lambda e, mp=mp: e.activation(
                    MB[:, mp * 512:(mp + 1) * 512], pT[:, :], AF.Copy),
                    reads=["b_pT"], writes=["b_MBc%d" % mp])

            head_prologue(0)
            emit_mb(0, 0)
            for h in range(NHEAD):
                hp = h % 2
                if h > 0 and not PREFETCH_PRO:
                    head_prologue(h)
                if h > 0:
                    for ch in range(4):
                        dma("scalar",
                            k_sb.ap().rearrange("p (c t) -> p c t", t=S)
                            [:, :, ch * 1024:(ch + 1) * 1024],
                            spk_r[:, 4 * h:4 * h + 4,
                                  ch * 1024:(ch + 1) * 1024],
                            reads=["d:sp_k"], writes=["b_kc%d" % ch])
                        dma("scalar",
                            v_sb.ap().rearrange("p (s c) -> p s c", c=DH)
                            [:, ch * 8:(ch + 1) * 8, :],
                            spv_r[:, ch * 8:(ch + 1) * 8,
                                  DH * h:DH * h + DH],
                            reads=["d:sp_v"], writes=["b_vc%d" % ch])
                pending_fin = []
                pending_fa = []
                pending_fb = []

                def qload(I, h=h):
                    qi = I % 2
                    dma("sync",
                        q_sb.ap().rearrange("p (qi c t) -> p qi c t",
                                            qi=2, t=512)[:, qi],
                        spq_r[:, 4 * h:4 * h + 4, I * 512:I * 512 + 512],
                        reads=["d:sp_q"], writes=["b_q" + str(qi)])
                qload(0)
                for I in range(8):
                    s0 = I * 512
                    qi = I % 2
                    qv = q_sb.ap().rearrange("p (qi c t) -> p qi c t",
                                             qi=2, t=512)[:, qi]
                    if I + 1 < 8:
                        qload(I + 1)
                    nJ = 4 * I + 4

                    def emit_qk(J):
                        pc, pct = pC[J % 2], pCt[J % 2]

                        def f_qk(e, J=J, pc=pc, qv=qv):
                            i = None
                            for d in range(4):
                                i = e.matmul(pc[:, :],
                                             k_sb[:, d * S + J * 128:
                                                  d * S + J * 128 + 128],
                                             qv[:, d, :],
                                             start=(d == 0), stop=(d == 3))
                            return i
                        em.step("tensor", f_qk,
                                reads=["b_kc%d" % (J // 8),
                                       "b_q" + str(qi)],
                                writes=[pct])

                    def emit_e(J, I=None):
                        ei = J % 3
                        o = J - 4 * I
                        s0 = I * 512
                        if o < 0:
                            em.step("scalar",
                                    lambda e, ei=ei, J=J, mp=I % 2, hp=hp:
                                    e.activation(
                                        Eb[:, ei * 512:(ei + 1) * 512],
                                        MB[:, mp * 512:(mp + 1) * 512],
                                        AF.Exp,
                                        bias=gcol2[:, hp * 32 + J:
                                                   hp * 32 + J + 1]),
                                    reads=["b_MBc%d" % (I % 2),
                                           "b_gcol2%d" % hp],
                                    writes=["b_E" + str(ei)], small=True)
                        else:
                            em.step("vector",
                                    lambda e, ei=ei, J=J, mp=I % 2, hp=hp:
                                    e.tensor_scalar(
                                        epsb[:, ei * 512:(ei + 1) * 512],
                                        MB[:, mp * 512:(mp + 1) * 512],
                                        gcol[:, hp * 32 + J:hp * 32 + J + 1],
                                        0.0, ALU.add,
                                        ALU.min),
                                    reads=["b_MBc%d" % (I % 2),
                                           "b_gcol%d" % hp],
                                    writes=["b_eps" + str(ei)], small=True)
                            em.step("scalar", lambda e, ei=ei: e.activation(
                                Eb[:, ei * 512:(ei + 1) * 512],
                                epsb[:, ei * 512:(ei + 1) * 512], AF.Exp,
                                bias=constv[:, 2:3]),
                                reads=["b_eps" + str(ei), "constv"],
                                writes=["b_E" + str(ei)])
                            em.step("gpsimd" if MASK_POOL else "vector",
                                    lambda e, ei=ei, o=o:
                                    e.tensor_tensor(
                                        Eb[:, ei * 512:(ei + 1) * 512],
                                        Eb[:, ei * 512:(ei + 1) * 512],
                                        masks_sb[:, o * 512:(o + 1) * 512],
                                        ALU.mult),
                                    reads=["b_E" + str(ei), "masks"],
                                    writes=["b_E" + str(ei)])

                    def emit_ct(J):
                        ei = J % 3
                        pc, pct = pC[J % 2], pCt[J % 2]
                        em.step("vector", lambda e, ei=ei, pc=pc:
                                e.tensor_tensor(
                                    ctb[:, ei * 512:(ei + 1) * 512],
                                    pc[:, :], Eb[:, ei * 512:(ei + 1) * 512],
                                    ALU.mult),
                                reads=[pct, "b_E" + str(ei)],
                                writes=["b_ct" + str(ei)])

                    def emit_av(J, nJ):
                        ei = J % 3

                        def f_av(e, J=J, ei=ei, nJ=nJ):
                            first = (J == 0)
                            i = None
                            for sub in range(4):
                                i = e.matmul(pH[sub][:, :],
                                             ctb[:, ei * 512 + sub * 128:
                                                 ei * 512 + sub * 128 + 128],
                                             v_sb[:, J * DH:(J + 1) * DH],
                                             start=first,
                                             stop=(J == nJ - 1))
                                i = e.matmul(pP[:, sub:sub + 1],
                                             ctb[:, ei * 512 + sub * 128:
                                                 ei * 512 + sub * 128 + 128],
                                             ones_c[:, :],
                                             start=(first and sub == 0),
                                             stop=(J == nJ - 1))
                            return i
                        em.step("tensor", f_av,
                                reads=["b_ct" + str(ei), "b_vc%d" % (J // 8),
                                       "ones_c"],
                                writes=pHt + ["b_pP"])

                    for J in range(nJ):
                        emit_qk(J)
                        if J == 1:
                            if I + 1 < 8:
                                emit_mb(I + 1, h)
                            elif h + 1 < NHEAD:
                                emit_mb(0, h + 1)
                        if J == 2 and pending_fa:
                            for fn in pending_fa:
                                fn()
                            pending_fa = []
                        if J == 4 and pending_fb:
                            for fn in pending_fb:
                                fn()
                            pending_fb = []
                        if J == 7 and pending_fin:
                            for fn in pending_fin:
                                fn()
                            pending_fin = []
                        emit_e(J, I=I)
                        emit_ct(J)
                        if J >= 2:
                            emit_av(J - 2, nJ)
                    if pending_fa:
                        for fn in pending_fa:
                            fn()
                        pending_fa = []
                    if pending_fb:
                        for fn in pending_fb:
                            fn()
                        pending_fb = []
                    if pending_fin:
                        for fn in pending_fin:
                            fn()
                        pending_fin = []
                    emit_av(nJ - 2, nJ)
                    emit_av(nJ - 1, nJ)

                    def fin_a(I=I):
                        em.step("scalar", lambda e: e.activation(
                            nb[:, :], pP[:, 0:4], AF.Abs),
                            reads=["b_pP"], writes=["b_nb"], small=True)
                        em.step("vector", lambda e, I=I, hp=hp:
                            e.tensor_tensor(
                            nb[:, :], nb[:, :],
                            ecol[:, hp * 32 + 4 * I:hp * 32 + 4 * I + 4],
                            ALU.max), reads=["b_nb", "b_ecol%d" % hp],
                            writes=["b_nb"], small=True)
                        em.step("vector", lambda e: e.tensor_scalar(
                            nb[:, :], nb[:, :], 1e-6, None, ALU.add),
                            reads=["b_nb"], writes=["b_nb"], small=True)
                        em.step("vector", lambda e: e.reciprocal(
                            ninv[:, :], nb[:, :]), reads=["b_nb"],
                            writes=["b_ninv"], small=True)
                        for sub in range(4):
                            em.step("scalar", lambda e, sub=sub: e.activation(
                                hf[:, sub * 512:(sub + 1) * 512],
                                pH[sub][:, :], AF.Copy,
                                scale=ninv[:, sub:sub + 1],
                                accum_out=stat[:, sub:sub + 1]),
                                reads=[pHt[sub], "b_ninv"],
                                writes=["b_hf%d" % sub, "b_stat"],
                                small=True)
                    pending_fa.append(fin_a)

                    def fin_b():
                        for sub in range(4):
                            em.step("scalar", lambda e, sub=sub:
                                    e.activation(
                                        hsq[:, :],
                                        hf[:, sub * 512:(sub + 1) * 512],
                                        AF.Square,
                                        accum_out=stat[:, 4 + sub:5 + sub]),
                                    reads=["b_hf%d" % sub],
                                    writes=["b_hsq", "b_stat"], small=True)
                        em.step("vector", lambda e: e.tensor_scalar(
                            stat[:, 8:12], stat[:, 0:4], 1.0 / DH, None,
                            ALU.mult), reads=["b_stat"], writes=["b_stat"],
                            small=True)
                        em.step("vector", lambda e: e.tensor_tensor(
                            stat[:, 12:16], stat[:, 8:12], stat[:, 8:12],
                            ALU.mult), reads=["b_stat"], writes=["b_stat"],
                            small=True)
                        em.step("vector", lambda e: e.scalar_tensor_tensor(
                            stat[:, 16:20], stat[:, 4:8], 1.0 / DH,
                            stat[:, 12:16], ALU.mult, ALU.subtract),
                            reads=["b_stat"], writes=["b_stat"], small=True)
                        em.step("scalar", lambda e: e.activation(
                            stat[:, 16:20], stat[:, 16:20], AF.Sqrt,
                            bias=constv[:, 0:1]),
                            reads=["b_stat", "constv"], writes=["b_stat"],
                            small=True)
                        em.step("vector", lambda e: e.reciprocal(
                            stat[:, 20:24], stat[:, 16:20]),
                            reads=["b_stat"], writes=["b_stat"], small=True)
                        for sub in range(4):
                            em.step("vector", lambda e, sub=sub:
                                    e.tensor_scalar(
                                        hn[:, sub * 512:(sub + 1) * 512],
                                        hf[:, sub * 512:(sub + 1) * 512],
                                        stat[:, 8 + sub:9 + sub],
                                        stat[:, 20 + sub:21 + sub],
                                        ALU.subtract, ALU.mult),
                                    reads=["b_hf%d" % sub, "b_stat"],
                                    writes=["b_hn" + str(sub)], small=True)
                    pending_fb.append(fin_b)

                    for sub in range(4):
                        def fin_tr(sub=sub, I=I, h=h, s0=s0):
                            def f_tr(e, sub=sub):
                                i = None
                                for cdh in range(4):
                                    i = e.transpose(
                                        pT[:, cdh * 128:(cdh + 1) * 128],
                                        hn[:, sub * 512 + cdh * 128:
                                           sub * 512 + (cdh + 1) * 128],
                                        identf[:, :])
                                return i
                            em.step("tensor", f_tr,
                                    reads=["b_hn" + str(sub), "identf"],
                                    writes=["b_pT"])
                            em.step("vector", lambda e, sub=sub:
                                    e.tensor_copy(
                                        hT.ap().rearrange(
                                            "p (c i s) -> p c i s",
                                            c=4, i=4, s=128)[:, :, sub, :],
                                        pT[:, 0:512].rearrange(
                                            "p (c s) -> p c s", s=128)),
                                    reads=["b_pT"], writes=["b_hT"])
                        pending_fin.append(fin_tr)

                    def fin_dma(h=h, s0=s0):
                        dma("sync",
                            sphnT_r[:, 4 * h:4 * h + 4, s0:s0 + 512],
                            hT.ap().rearrange("p (c t) -> p c t", t=512),
                            reads=["b_hT"], writes=["d:sp_hnT"])
                    pending_fin.append(fin_dma)
                    if I == 5 and h + 1 < NHEAD and PREFETCH_PRO:
                        head_prologue(h + 1)
                if pending_fa:
                    for fn in pending_fa:
                        fn()
                    pending_fa = []
                if pending_fb:
                    for fn in pending_fb:
                        fn()
                    pending_fb = []
                if pending_fin:
                    for fn in pending_fin:
                        fn()
                    pending_fin = []
            em.flush()

        kvst.close()

        TBX = 256
        NTBX = S // TBX
        with ExitStack() as _st:
            dwB = _st.enter_context(nc.sbuf_tensor(f"c_dwB{l}", [128, 8 * D], BF16))
            hTb = _st.enter_context(nc.sbuf_tensor(f"c_hT{l}", [128, 2 * 16 * TBX], BF16))
            xcc = _st.enter_context(nc.sbuf_tensor(f"c_xc{l}", [128, 2 * 16 * TBX], BF16))
            zc = _st.enter_context(nc.sbuf_tensor(f"c_z{l}", [128, 2 * 16 * TBX], BF16))
            xbc = _st.enter_context(nc.sbuf_tensor(f"c_x{l}", [128, 2 * 8 * TBX], F32))
            cs1 = _st.enter_context(nc.sbuf_tensor(f"c_s1{l}", [128, 2 * TBX], F32))
            cs2 = _st.enter_context(nc.sbuf_tensor(f"c_s2{l}", [128, 2 * TBX], F32))
            cs3 = _st.enter_context(nc.sbuf_tensor(f"c_s3{l}", [128, 2 * TBX], F32))
            outc = _st.enter_context(nc.sbuf_tensor(f"c_out{l}", [128, 2 * 16 * TBX], BF16))
            xoc = _st.enter_context(nc.sbuf_tensor(f"c_xo{l}", [128, 2 * 8 * TBX], F32))
            pDA = _st.enter_context(nc.psum_tensor(f"c_pDA{l}", [128, 8 * TBX], F32))
            pDB = _st.enter_context(nc.psum_tensor(f"c_pDB{l}", [128, 8 * TBX], F32))
            dma("gpsimd", dwB.ap().rearrange("p (k m) -> p k m", m=D),
                dt["down_wT"].ap()[l].rearrange("(k p) m -> p k m", p=128)
                [:, 8:16, :], writes=["dwB"])

            def c_loads(tb):
                t0 = tb * TBX
                par = tb % 2
                o16 = par * 16 * TBX
                o8 = par * 8 * TBX
                dma("scalar",
                    hTb[:, o16:o16 + 16 * TBX]
                    .rearrange("p (c t) -> p c t", t=TBX),
                    sphnT_r[:, :, t0:t0 + TBX], reads=["d:sp_hnT"],
                    writes=["c_hT%d" % par])
                dma("scalar",
                    xcc[:, o16:o16 + 16 * TBX]
                    .rearrange("p (c t) -> p c t", t=TBX),
                    spxc_r[:, :, t0:t0 + TBX], reads=["d:sp_xc"],
                    writes=["c_xc%d" % par])
                dma("scalar",
                    zc[:, o16:o16 + 16 * TBX]
                    .rearrange("p (c t) -> p c t", t=TBX),
                    spz_r[:, :, t0:t0 + TBX], reads=["d:sp_z"],
                    writes=["c_z%d" % par])
                dma("sync",
                    xbc[:, o8:o8 + 8 * TBX]
                    .rearrange("p (c t) -> p c t", t=TBX),
                    spx_r[:, :, t0:t0 + TBX], reads=["d:sp_x"],
                    writes=["c_x%d" % par])
            c_loads(0)
            c_loads(1)
            for tb in range(NTBX):
                t0 = tb * TBX
                par = tb % 2
                o16 = par * 16 * TBX
                o8 = par * 8 * TBX
                pD = pDA if par == 0 else pDB
                pdtok = "c_pDA" if par == 0 else "c_pDB"
                for c in range(16):
                    cp = c % 2
                    co = cp * TBX
                    em.step("vector", lambda e, c=c, o16=o16, par=par, co=co:
                            e.tensor_scalar(
                                cs1[:, co:co + TBX],
                                hTb[:, o16 + c * TBX:o16 + (c + 1) * TBX],
                                onorm_sb[:, l * 16 + c:l * 16 + c + 1],
                                None, ALU.mult),
                            reads=["c_hT%d" % par, "onormc"],
                            writes=["c_s1%d" % cp])
                    em.step("vector", lambda e, c=c, o16=o16, par=par, co=co:
                            e.scalar_tensor_tensor(
                                cs2[:, co:co + TBX],
                                xcc[:, o16 + c * TBX:o16 + (c + 1) * TBX],
                                skip_sb[:, l * 16 + c:l * 16 + c + 1],
                                cs1[:, co:co + TBX], ALU.mult, ALU.add),
                            reads=["c_xc%d" % par, "skipc", "c_s1%d" % cp],
                            writes=["c_s2%d" % cp])
                    em.step("scalar", lambda e, c=c, o16=o16, par=par, co=co:
                            e.activation(
                                cs3[:, co:co + TBX],
                                zc[:, o16 + c * TBX:o16 + (c + 1) * TBX],
                                AF.Silu),
                            reads=["c_z%d" % par], writes=["c_s3%d" % cp])
                    em.step("gpsimd", lambda e, c=c, o16=o16, par=par, co=co:
                            e.tensor_tensor(
                                outc[:, o16 + c * TBX:o16 + (c + 1) * TBX],
                                cs2[:, co:co + TBX], cs3[:, co:co + TBX],
                                ALU.mult),
                            reads=["c_s2%d" % cp, "c_s3%d" % cp],
                            writes=["c_o%d_%d" % (par, c), "c_out%d" % par])

                    dwc = dwA if c < 8 else dwB
                    dwtok = "dwA" if c < 8 else "dwB"

                    def f_dnk(e, c=c, pD=pD, o16=o16, dwc=dwc):
                        i = None
                        for m in range(8):
                            i = e.matmul(pD[:, m * TBX:(m + 1) * TBX],
                                         dwc[:, (c % 8) * D + m * 128:
                                             (c % 8) * D + m * 128 + 128],
                                         outc[:, o16 + c * TBX:
                                              o16 + (c + 1) * TBX],
                                         start=(c == 0 and m % 2 == 0),
                                         stop=(c == 15))
                        return i
                    em.step("tensor", f_dnk,
                            reads=[dwtok, "c_o%d_%d" % (par, c)],
                            writes=[pdtok])
                for m in range(8):
                    em.step("vector", lambda e, m=m, pD=pD, o8=o8, par=par:
                            e.tensor_tensor(
                                xoc[:, o8 + m * TBX:o8 + (m + 1) * TBX],
                                pD[:, m * TBX:(m + 1) * TBX],
                                xbc[:, o8 + m * TBX:o8 + (m + 1) * TBX],
                                ALU.add),
                            reads=[pdtok, "c_x%d" % par],
                            writes=["c_xo%d" % par])
                dma("sync", spx_r[:, :, t0:t0 + TBX],
                    xoc[:, o8:o8 + 8 * TBX]
                    .rearrange("p (c t) -> p c t", t=TBX),
                    reads=["c_xo%d" % par], writes=["d:sp_x"])
                if tb + 2 < NTBX:
                    c_loads(tb + 2)
            em.flush()
        dwst.close()

    if LAT_ON:
        TBL = 256
        NTBL = S // TBL
        with ExitStack() as _st:
            latw = _st.enter_context(nc.sbuf_tensor("l_lat", [128, 8 * D], BF16))
            xbl = _st.enter_context(nc.sbuf_tensor("l_x", [128, 2 * 8 * TBL], F32))
            x0l = _st.enter_context(nc.sbuf_tensor("l_x0", [128, 2 * 8 * TBL], F32))
            xnl = _st.enter_context(nc.sbuf_tensor("l_xn", [128, 2 * 8 * TBL], BF16))
            xb16l = _st.enter_context(nc.sbuf_tensor("l_b16", [128, 2 * TBL], BF16))
            sqbl = _st.enter_context(nc.sbuf_tensor("l_sq", [128, 2 * TBL], BF16))
            rowmul = _st.enter_context(nc.sbuf_tensor("l_rmu", [1, TBL], F32))
            rowrsl = _st.enter_context(nc.sbuf_tensor("l_rrs", [1, TBL], F32))
            rowtl = _st.enter_context(nc.sbuf_tensor("l_rt", [1, TBL], F32))
            muBl = _st.enter_context(nc.sbuf_tensor("l_muB", [128, TBL], F32))
            rstdBl = _st.enter_context(nc.sbuf_tensor("l_rsB", [128, TBL], F32))
            t1l = _st.enter_context(nc.sbuf_tensor("l_t1", [128, TBL], F32))
            xol = _st.enter_context(nc.sbuf_tensor("l_xo", [128, 2 * 8 * TBL], F32))
            pLNl = _st.enter_context(nc.psum_tensor("l_pLN", [1, 2 * TBL], F32))
            pBl = _st.enter_context(nc.psum_tensor("l_pB", [128, TBL], F32))
            pLa = _st.enter_context(nc.psum_tensor("l_pLa", [128, 8 * TBL], F32))
            lb = dict(xb16=xb16l, sqb=sqbl, pLN=pLNl, rowmu=rowmul,
                      rowrs=rowrsl, rowt=rowtl, pB=pBl, muB=muBl,
                      rstdB=rstdBl, t1=t1l, pool_stats=POOL_STATS)
            dma("gpsimd", latw.ap().rearrange("p (k m) -> p k m", m=D),
                dt["latT"].ap().rearrange("(k p) m -> p k m", p=128),
                writes=["latw"])

            def l_loads(tb):
                par = tb % 2
                t0 = tb * TBL
                dma("sync",
                    xbl[:, par * 8 * TBL:(par + 1) * 8 * TBL]
                    .rearrange("p (c t) -> p c t", t=TBL),
                    spx_r[:, :, t0:t0 + TBL], reads=["d:sp_x"],
                    writes=["l_x%d" % par])
                dma("scalar",
                    x0l[:, par * 8 * TBL:(par + 1) * 8 * TBL]
                    .rearrange("p (c t) -> p c t", t=TBL),
                    xT_r[:, :, t0:t0 + TBL], writes=["l_x0%d" % par])

            def l_pieces(tb):
                par = tb % 2
                xv = xbl[:, par * 8 * TBL:(par + 1) * 8 * TBL]
                xnv = xnl[:, par * 8 * TBL:(par + 1) * 8 * TBL]
                return ln_block_q("l", "l_x%d" % par, xv, 8, TBL, xnv,
                                  "l_xn%d" % par,
                                  (lambda c: postw_sb[:, c:c + 1]), lb)

            l_loads(0)
            l_loads(1)
            pcs = l_pieces(0)
            for p in pcs[:11]:
                if p is not None:
                    p()
            norms = pcs[11:]
            for tb in range(NTBL):
                par = tb % 2
                t0 = tb * TBL
                nxt = l_pieces(tb + 1) if tb + 1 < NTBL else []
                nxt_c, nxt_n = nxt[:11], nxt[11:]
                for k in range(8):
                    norms[k]()

                    def f_latk(e, k=k, par=par):
                        i = None
                        for m in range(8):
                            i = e.matmul(pLa[:, m * TBL:(m + 1) * TBL],
                                         latw[:, k * D + m * 128:
                                              k * D + m * 128 + 128],
                                         xnl[:, par * 8 * TBL + k * TBL:
                                             par * 8 * TBL + (k + 1) * TBL],
                                         start=(k == 0 and m % 2 == 0),
                                         stop=(k == 7))
                        return i
                    em.step("tensor", f_latk,
                            reads=["latw", "l_xn%d" % par], writes=["l_pLa"])
                    if k >= 2:
                        for p in nxt_c[:2]:
                            if p is not None:
                                p()
                        nxt_c = nxt_c[2:]
                for p in nxt_c:
                    if p is not None:
                        p()
                for m in range(8):
                    em.step("vector", lambda e, m=m, par=par:
                            e.scalar_tensor_tensor(
                                xol[:, par * 8 * TBL + m * TBL:
                                    par * 8 * TBL + (m + 1) * TBL],
                                pLa[:, m * TBL:(m + 1) * TBL],
                                latb_sb[:, m:m + 1],
                                x0l[:, par * 8 * TBL + m * TBL:
                                    par * 8 * TBL + (m + 1) * TBL],
                                ALU.add, ALU.add),
                            reads=["l_pLa", "l_x0%d" % par, "latb"],
                            writes=["l_xo%d" % par])
                dma("sync", spx1_r[:, :, t0:t0 + TBL],
                    xol[:, par * 8 * TBL:(par + 1) * 8 * TBL]
                    .rearrange("p (c t) -> p c t", t=TBL),
                    reads=["l_xo%d" % par], writes=["d:sp_x1"])
                if tb + 2 < NTBL:
                    l_loads(tb + 2)
                norms = nxt_n
            em.flush()

    if MOE_ON:
        with ExitStack() as _st:
            x1m = _st.enter_context(nc.sbuf_tensor("m_x1", [128, 2 * 8 * TBM], F32))
            xnm = _st.enter_context(nc.sbuf_tensor("m_xn", [128, 8 * TBM], BF16))
            xnf = _st.enter_context(nc.sbuf_tensor("m_xnf", [128, 8 * TBM], F32))
            rt_sb = _st.enter_context(nc.sbuf_tensor("m_rt", [128, 8 * 8], F32))
            xb16m = _st.enter_context(nc.sbuf_tensor("m_b16", [128, 2 * TBM], BF16))
            sqbm = _st.enter_context(nc.sbuf_tensor("m_sq", [128, 2 * TBM], BF16))
            rowmum = _st.enter_context(nc.sbuf_tensor("m_rmu", [1, TBM], F32))
            rowrsm = _st.enter_context(nc.sbuf_tensor("m_rrs", [1, TBM], F32))
            rowtm = _st.enter_context(nc.sbuf_tensor("m_rt2", [1, TBM], F32))
            muBm = _st.enter_context(nc.sbuf_tensor("m_muB", [128, TBM], F32))
            rstdBm = _st.enter_context(nc.sbuf_tensor("m_rsB", [128, TBM], F32))
            t1m = _st.enter_context(nc.sbuf_tensor("m_t1", [128, TBM], F32))
            lgT = _st.enter_context(nc.sbuf_tensor("m_lg", [8, TBM], F32))
            lgtok = _st.enter_context(nc.sbuf_tensor("m_lgt", [128, 2 * 8], F32))
            m1c = _st.enter_context(nc.sbuf_tensor("m_m1", [128, 2], F32))
            m2c = _st.enter_context(nc.sbuf_tensor("m_m2", [128, 2], F32))
            mskc = _st.enter_context(nc.sbuf_tensor("m_msk", [128, 8], F32))
            e1c = _st.enter_context(nc.sbuf_tensor("m_e1", [128, 2 * 8], F32))
            cmb = _st.enter_context(nc.sbuf_tensor("m_cmb", [128, 2 * 8], F32))
            cTf = _st.enter_context(nc.sbuf_tensor("m_cT", [8, TBM], F32))
            pM = _st.enter_context(nc.psum_tensor("m_pM", [128, 512], F32))
            pLNm = _st.enter_context(nc.psum_tensor("m_pLN", [1, 2 * TBM], F32))
            lb = dict(xb16=xb16m, sqb=sqbm, pLN=pLNm, rowmu=rowmum,
                      rowrs=rowrsm, rowt=rowtm, pB=pM, muB=muBm,
                      rstdB=rstdBm, t1=t1m, pbtok="m_pM",
                      pool_stats=POOL_STATS)
            dma("sync", rt_sb.ap().rearrange("p (k m) -> p k m", m=8),
                dt["rT"].ap().rearrange("(k p) m -> p k m", p=128),
                writes=["rt"])

            def r_load(tb):
                par = tb % 2
                t0 = tb * TBM
                dma("scalar",
                    x1m[:, par * 8 * TBM:(par + 1) * 8 * TBM]
                    .rearrange("p (c t) -> p c t", t=TBM),
                    spx1_r[:, :, t0:t0 + TBM], reads=["d:sp_x1"],
                    writes=["m_x1%d" % par])

            def r_pieces(tb):
                par = tb % 2
                xv = x1m[:, par * 8 * TBM:(par + 1) * 8 * TBM]
                return ln_block_q("m", "m_x1%d" % par, xv, 8, TBM, xnf,
                                  "m_xnf", None, lb)

            r_load(0)
            r_load(1)
            pcs = r_pieces(0)
            for p in pcs[:11]:
                if p is not None:
                    p()
            norms = pcs[11:]
            for tb in range(S // TBM):
                t0 = tb * TBM
                nxt = r_pieces(tb + 1) if tb + 1 < S // TBM else []
                nxt_c, nxt_n = nxt[:11], nxt[11:]
                for k in range(8):
                    norms[k]()

                    def f_rk(e, k=k):
                        return e.matmul(pM[0:8, 0:TBM],
                                        rt_sb[:, k * 8:(k + 1) * 8],
                                        xnf[:, k * TBM:(k + 1) * TBM],
                                        start=(k == 0), stop=(k == 7))
                    em.step("tensor", f_rk, reads=["rt", "m_xnf", "m_lg"],
                            writes=["m_pM"])
                    em.step("gpsimd", lambda e, k=k: e.tensor_copy(
                        xnm[:, k * TBM:(k + 1) * TBM],
                        xnf[:, k * TBM:(k + 1) * TBM]),
                        reads=["m_xnf"], writes=["m_xn"])
                dma("sync", spxn_r[:, :, t0:t0 + TBM],
                    xnm.ap().rearrange("p (c t) -> p c t", t=TBM),
                    reads=["m_xn"], writes=["d:sp_xn"])
                if tb + 2 < S // TBM:
                    r_load(tb + 2)
                em.step("scalar", lambda e: e.activation(
                    lgT[:, :], pM[0:8, 0:TBM], AF.Copy), reads=["m_pM"],
                    writes=["m_lg"])
                for sub in range(2):
                    em.step("tensor", lambda e, sub=sub: e.transpose(
                        pM[:, 256:264], lgT[:, sub * 128:(sub + 1) * 128],
                        identf[0:8, 0:8]), reads=["m_lg", "identf"],
                        writes=["m_pM"])
                    em.step("scalar", lambda e, sub=sub: e.activation(
                        lgtok[:, sub * 8:(sub + 1) * 8], pM[:, 256:264],
                        AF.Copy), reads=["m_pM"], writes=["m_lgt"], small=True)
                _top2(em, lgtok, m1c, m2c, mskc, e1c, cmb)
                for sub in range(2):
                    em.step("tensor", lambda e, sub=sub: e.transpose(
                        pM[0:8, 256:384], cmb[:, sub * 8:(sub + 1) * 8],
                        identf[:, :]), reads=["m_cmb", "identf"],
                        writes=["m_pM"], small=True)
                    em.step("scalar", lambda e, sub=sub: e.activation(
                        cTf[0:8, sub * 128:(sub + 1) * 128],
                        pM[0:8, 256:384], AF.Copy), reads=["m_pM"],
                        writes=["m_cT"])
                dma("sync", dt["sp_cmb"].ap()[:, t0:t0 + TBM], cTf[0:8, :],
                    reads=["m_cT"], writes=["d:sp_cmb"])
                for p in nxt_c:
                    if p is not None:
                        p()
                norms = nxt_n
                if DBG_B and tb == 0:
                    dma("sync", dt["sp_dbg2"].ap()[:, 0:16], lgtok[:, :],
                        reads=["m_lgt"], writes=["d:sp_dbg2"])
                    dma("sync", dt["sp_dbg2"].ap()[:, 16:32], cmb[:, :],
                        reads=["m_cmb"], writes=["d:sp_dbg2"])
            em.flush()

        with ExitStack() as _st:
            w1a = _st.enter_context(nc.sbuf_tensor("m_w1a", [128, 8 * FF], BF16))
            w1b = _st.enter_context(nc.sbuf_tensor("m_w1b", [128, 8 * FF], BF16))
            w2a = _st.enter_context(nc.sbuf_tensor("m_w2a", [128, 16 * D], BF16))
            w2b = _st.enter_context(nc.sbuf_tensor("m_w2b", [128, 16 * D], BF16))
            xnm = _st.enter_context(nc.sbuf_tensor("m_xn2", [128, 2 * 8 * TBM], BF16))
            cTb = _st.enter_context(nc.sbuf_tensor("m_cTb", [8, 2 * TBM], BF16))
            crow = _st.enter_context(nc.sbuf_tensor("m_crow", [1, 2 * TBM], F32))
            cB = _st.enter_context(nc.sbuf_tensor("m_cB", [128, 2 * TBM], BF16))
            hmb = _st.enter_context(nc.sbuf_tensor("m_hm", [128, 16 * TBM], BF16))
            hgb = _st.enter_context(nc.sbuf_tensor("m_hg", [128, 2 * TBM], BF16))
            oub = _st.enter_context(nc.sbuf_tensor("m_ou", [128, 8 * TBM], F32))
            pW0 = _st.enter_context(nc.psum_tensor("m_pW0", [128, TBM], F32))
            pW1 = _st.enter_context(nc.psum_tensor("m_pW1", [128, TBM], F32))
            pO0 = _st.enter_context(nc.psum_tensor("m_pO0", [128, 512], F32))
            pO1 = _st.enter_context(nc.psum_tensor("m_pO1", [128, 512], F32))
            pO2 = _st.enter_context(nc.psum_tensor("m_pO2", [128, 512], F32))
            pO3 = _st.enter_context(nc.psum_tensor("m_pO3", [128, 512], F32))
            pM2 = _st.enter_context(nc.psum_tensor("m_pM2", [128, TBM], F32))
            pO = [pO0, pO1, pO2, pO3]
            pOt = ["m_pO0", "m_pO1", "m_pO2", "m_pO3"]
            w1s, w2s = [w1a, w1b], [w2a, w2b]
            w1toks, w2toks = ["m_w1a", "m_w1b"], ["m_w2a", "m_w2b"]
            dma("gpsimd", w1a.ap().rearrange("p (k m) -> p k m", m=FF),
                dt["w1T"].ap()[0].rearrange("(k p) m -> p k m", p=128),
                writes=["m_w1a"])
            dma("gpsimd", w2a.ap().rearrange("p (k m) -> p k m", m=D),
                dt["w2T"].ap()[0].rearrange("(k p) m -> p k m", p=128),
                writes=["m_w2a"])
            for ee in range(EXP):
                wi = ee % 2
                w1c, w2c = w1s[wi], w2s[wi]
                w1tok, w2tok = w1toks[wi], w2toks[wi]
                if ee + 1 < EXP:
                    dma("gpsimd",
                        w1s[(ee + 1) % 2].ap()
                        .rearrange("p (k m) -> p k m", m=FF),
                        dt["w1T"].ap()[ee + 1]
                        .rearrange("(k p) m -> p k m", p=128),
                        writes=[w1toks[(ee + 1) % 2]])
                    dma("gpsimd",
                        w2s[(ee + 1) % 2].ap()
                        .rearrange("p (k m) -> p k m", m=D),
                        dt["w2T"].ap()[ee + 1]
                        .rearrange("(k p) m -> p k m", p=128),
                        writes=[w2toks[(ee + 1) % 2]])
                def m_loads(tb, ee=ee):
                    t0 = tb * TBM
                    par = tb % 2
                    dma("scalar",
                        xnm[:, par * 8 * TBM:(par + 1) * 8 * TBM]
                        .rearrange("p (c t) -> p c t", t=TBM),
                        spxn_r[:, :, t0:t0 + TBM], reads=["d:sp_xn"],
                        writes=["m_xn%d" % par])
                    dma("scalar", crow[:, par * TBM:(par + 1) * TBM],
                        dt["sp_cmb"].ap()[ee:ee + 1, t0:t0 + TBM],
                        reads=["d:sp_cmb"], writes=["m_crow%d" % par])
                    if ee == 0:
                        dma("gpsimd", cTb[:, par * TBM:(par + 1) * TBM],
                            dt["sp_cmb"].ap()[:, t0:t0 + TBM],
                            reads=["d:sp_cmb"], writes=["m_cTb%d" % par])
                m_loads(0)
                for tb in range(S // TBM):
                    t0 = tb * TBM
                    par = tb % 2
                    xnv = xnm[:, par * 8 * TBM:(par + 1) * 8 * TBM]
                    if tb + 1 < S // TBM:
                        m_loads(tb + 1)
                    em.step("tensor", lambda e, par=par: e.matmul(
                        pM2[:, :], ones_r[0:1, :],
                        crow[0:1, par * TBM:(par + 1) * TBM],
                        start=True, stop=True),
                        reads=["m_crow%d" % par, "ones_r", "m_cB%d" % par],
                        writes=["m_pM2"])
                    em.step("scalar", lambda e, par=par: e.activation(
                        cB[:, par * TBM:(par + 1) * TBM], pM2[:, :],
                        AF.Copy), reads=["m_pM2"],
                        writes=["m_cB%d" % par])
                    if ee == 0:
                        def f_b2(e, par=par):
                            i = None
                            for hc in range(8):
                                i = e.matmul(
                                    pO[hc // 2][:, (hc % 2) * TBM:
                                                (hc % 2) * TBM + TBM],
                                    b2_sb[:, hc * 128:(hc + 1) * 128],
                                    cTb[:, par * TBM:(par + 1) * TBM],
                                    start=(hc % 2 == 0),
                                    stop=False)
                            return i
                        em.step("tensor", f_b2,
                                reads=["b2e", "m_cTb%d" % par],
                                writes=pOt)
                    for mc in range(16):
                        pW = pW0 if mc % 2 == 0 else pW1
                        pwt = "m_pW0" if mc % 2 == 0 else "m_pW1"

                        def f_w1(e, mc=mc, pW=pW, w1c=w1c, xnv=xnv):
                            i = None
                            for k in range(8):
                                i = e.matmul(pW[:, :],
                                             w1c[:, k * FF + mc * 128:
                                                 k * FF + mc * 128 + 128],
                                             xnv[:, k * TBM:(k + 1) * TBM],
                                             start=(k == 0),
                                             stop=(k == 7))
                            return i
                        em.step("tensor", f_w1, reads=[w1tok, "m_xn%d" % par],
                                writes=[pwt])
                        hp = mc % 2
                        em.step("scalar", lambda e, mc=mc, pW=pW, ee=ee,
                                hp=hp: e.activation(
                                    hgb[:, hp * TBM:(hp + 1) * TBM],
                                    pW[:, :], AF.Gelu,
                                    bias=b1_sb[:, ee * 16 + mc:
                                               ee * 16 + mc + 1]),
                                reads=[pwt, "b1c"], writes=["m_hg%d" % hp])
                        em.step("vector", lambda e, mc=mc, par=par,
                                hp=hp: e.tensor_tensor(
                                    hmb[:, mc * TBM:(mc + 1) * TBM],
                                    hgb[:, hp * TBM:(hp + 1) * TBM],
                                    cB[:, par * TBM:(par + 1) * TBM],
                                    ALU.mult),
                                reads=["m_hg%d" % hp, "m_cB%d" % par],
                                writes=["m_hm"])
                    for hc in range(8):
                        def f_w2(e, hc=hc, ee=ee, w2c=w2c):
                            i = None
                            for k in range(16):
                                i = e.matmul(
                                    pO[hc // 2][:, (hc % 2) * TBM:
                                                (hc % 2) * TBM + TBM],
                                    w2c[:, k * D + hc * 128:
                                        k * D + hc * 128 + 128],
                                    hmb[:, k * TBM:(k + 1) * TBM],
                                    start=(ee != 0 and k == 0
                                           and hc % 2 == 0),
                                    stop=(k == 15))
                            return i
                        em.step("tensor", f_w2, reads=[w2tok, "m_hm"],
                                writes=[pOt[hc // 2]])
                        em.step("scalar", lambda e, hc=hc: e.activation(
                            oub[:, hc * TBM:(hc + 1) * TBM],
                            pO[hc // 2][:, (hc % 2) * TBM:
                                        (hc % 2) * TBM + TBM], AF.Copy),
                            reads=[pOt[hc // 2]], writes=["m_ou"])
                    if ee == 0:
                        dma("gpsimd",
                            dt["sp_acc"].ap()
                            .rearrange("(c p) t -> p c t", p=128)
                            [:, :, t0:t0 + TBM],
                            oub.ap().rearrange("p (c t) -> p c t", t=TBM),
                            reads=["m_ou"], writes=["d:sp_acc"])
                    else:
                        _dst = (dt["sp_acc"].ap()
                                .rearrange("(c p) t -> p c t", p=128)
                                [:, :, t0:t0 + TBM])
                        _srcap = oub.ap().rearrange("p (c t) -> p c t", t=TBM)
                        em.step("gpsimd",
                                lambda e, o=_dst, i=_srcap: e.dma_start(
                                    o, i, accum_op=ALU.add),
                                reads=["m_ou"], writes=["d:sp_acc"],
                                dma=True)
            em.flush()

        with ExitStack() as _st:
            x1m = _st.enter_context(nc.sbuf_tensor("f_x1", [128, 8 * TBC], F32))
            accm = _st.enter_context(nc.sbuf_tensor("f_ac", [128, 8 * TBC], F32))
            xom = _st.enter_context(nc.sbuf_tensor("f_xo", [128, 8 * TBC], F32))
            spacc_r = dt["sp_acc"].ap().rearrange("(c p) t -> p c t", p=128)
            for tb in range(NTBC):
                t0 = tb * TBC
                dma("sync", x1m.ap().rearrange("p (c t) -> p c t", t=TBC),
                    spx1_r[:, :, t0:t0 + TBC], reads=["d:sp_x1"],
                    writes=["f_x1"])
                dma("sync", accm.ap().rearrange("p (c t) -> p c t", t=TBC),
                    spacc_r[:, :, t0:t0 + TBC], reads=["d:sp_acc"],
                    writes=["f_ac"])
                for c in range(8):
                    em.step("vector", lambda e, c=c: e.tensor_tensor(
                        xom[:, c * TBC:(c + 1) * TBC],
                        x1m[:, c * TBC:(c + 1) * TBC],
                        accm[:, c * TBC:(c + 1) * TBC], ALU.add),
                        reads=["f_x1", "f_ac"], writes=["f_xo"])
                if t0 < SHALF:
                    dstv, dtok, tloc = o_r, "d:o", t0
                else:
                    dstv, dtok, tloc = o2_r, "d:o2", t0 - SHALF
                dma("sync", dstv[:, :, tloc:tloc + TBC],
                    xom.ap().rearrange("p (c t) -> p c t", t=TBC),
                    reads=["f_xo"], writes=[dtok])
            em.flush()

    for name in taps:
        dma("sync", tap_t[name].ap(), dt[name].ap(),
            reads=["d:" + name], writes=["tap:" + name])
    em.finish()
    return nc


def _top2(em, lgtok, m1c, m2c, mskc, e1c, cmb):
    for sub in range(2):
        sl = lgtok[:, sub * 8:(sub + 1) * 8]
        em.step("vector", lambda e, sl=sl, sub=sub: e.reduce_max(
            m1c[:, sub:sub + 1], sl, AX.X), reads=["m_lgt"], writes=["m_m1"], small=True)
        em.step("vector", lambda e, sub=sub: e.tensor_scalar(
            m1c[:, sub:sub + 1], m1c[:, sub:sub + 1], -1.0, None, ALU.mult),
            reads=["m_m1"], writes=["m_m1"], small=True)
        em.step("scalar", lambda e, sl=sl, sub=sub: e.activation(
            e1c[:, sub * 8:(sub + 1) * 8], sl, AF.Exp,
            bias=m1c[:, sub:sub + 1]), reads=["m_lgt", "m_m1"],
            writes=["m_e1"], small=True)
        em.step("vector", lambda e, sl=sl, sub=sub: e.tensor_scalar(
            mskc[:, :], sl, m1c[:, sub:sub + 1], 0.0, ALU.add, ALU.is_equal),
            reads=["m_lgt", "m_m1"], writes=["m_msk"], small=True)
        em.step("vector", lambda e, sl=sl: e.scalar_tensor_tensor(
            mskc[:, :], mskc[:, :], -1e30, sl, ALU.mult, ALU.add),
            reads=["m_msk", "m_lgt"], writes=["m_msk"], small=True)
        em.step("vector", lambda e, sub=sub: e.reduce_max(
            m2c[:, sub:sub + 1], mskc[:, :], AX.X), reads=["m_msk"],
            writes=["m_m2"], small=True)
        em.step("vector", lambda e, sl=sl, sub=sub: e.tensor_scalar(
            mskc[:, :], sl, m2c[:, sub:sub + 1], None, ALU.is_ge),
            reads=["m_lgt", "m_m2"], writes=["m_msk"], small=True)
        em.step("scalar", lambda e, sub=sub: e.activation(
            m2c[:, sub:sub + 1], m2c[:, sub:sub + 1], AF.Exp,
            bias=m1c[:, sub:sub + 1]), reads=["m_m2", "m_m1"],
            writes=["m_m2"], small=True)
        em.step("vector", lambda e, sub=sub: e.tensor_scalar(
            m2c[:, sub:sub + 1], m2c[:, sub:sub + 1], 1.0, None, ALU.add),
            reads=["m_m2"], writes=["m_m2"], small=True)
        em.step("vector", lambda e, sub=sub: e.reciprocal(
            m2c[:, sub:sub + 1], m2c[:, sub:sub + 1]), reads=["m_m2"],
            writes=["m_m2"], small=True)
        em.step("vector", lambda e, sub=sub: e.tensor_tensor(
            cmb[:, sub * 8:(sub + 1) * 8], e1c[:, sub * 8:(sub + 1) * 8],
            mskc[:, :], ALU.mult), reads=["m_e1", "m_msk"],
            writes=["m_cmb"], small=True)
        em.step("vector", lambda e, sub=sub: e.tensor_scalar(
            cmb[:, sub * 8:(sub + 1) * 8], cmb[:, sub * 8:(sub + 1) * 8],
            m2c[:, sub:sub + 1], None, ALU.mult), reads=["m_cmb", "m_m2"],
            writes=["m_cmb"], small=True)


_PROGRAM_CACHE = {}


def kernel(**inputs):
    if "prog" not in _PROGRAM_CACHE:
        _PROGRAM_CACHE["prog"] = build_program()
    nc = _PROGRAM_CACHE["prog"]
    in_maps = _host_prepare(inputs)
    res = run_bass_kernel_spmd(nc, in_maps, core_ids=list(range(NCORES)))
    out = np.zeros((B, S, D), np.float32)
    for core in range(NCORES):
        b = core // 2
        half = core % 2
        name = "o" if half == 0 else "o2"
        shard = np.asarray(res.results[core][name])
        out[b, half * SHALF:(half + 1) * SHALF, :] = shard.T
    return out


# version 32
# speedup vs baseline: 1.0339x; 1.0148x over previous
import os
from contextlib import ExitStack
import numpy as np

import concourse.bass as bass
import concourse.mybir as mybir
from concourse.bass_utils import run_bass_kernel_spmd

F32 = mybir.dt.float32
BF16 = mybir.dt.bfloat16
AF = mybir.ActivationFunctionType
ALU = mybir.AluOpType
AX = mybir.AxisListType

B, S, D = 4, 4096, 1024
INNER = 2048
KCONV = 4
NHEAD = 4
DH = 512
L = 4
EXP, FF = 8, 2048
NCORES = 8
SHALF = S // 2

TBA = 256
NTBA = S // TBA
TBC = 512
NTBC = S // TBC
TBM = 256

NEG_HALF_LOG_DH = -0.5 * float(np.log(DH))

L_EFF = L
MOE_ON = True
LAT_ON = True
DBG_B = False
POOL_STATS = True

ENGS = ("sync", "gpsimd", "scalar", "vector", "tensor")
QDEPTH = 6
SAME_ENGINE_SYNC = True
PREFETCH_PRO = True
MASK_POOL = True


class _Step:
    __slots__ = ("idx", "eng", "fn", "dma", "deps", "sem", "inc", "cum",
                 "small")

    def __init__(self, idx, eng, fn, dma):
        self.idx, self.eng, self.fn, self.dma = idx, eng, fn, dma
        self.small = False
        self.deps = set()
        self.sem = None
        self.inc = 0
        self.cum = 0


class Em:

    def __init__(self, nc):
        self.nc = nc
        self.sems = {}
        self.cnt = {}
        self.steps = []
        self.fstart = 0
        self.lastw = {}
        self.readers = {}
        self.hw = {e: {} for e in ENGS}
        self.qidx = {}
        self.last_event = {}
        self.nflush = 0

    def _sem(self, name):
        if name not in self.sems:
            h = self.nc.ctx.enter_context(
                self.nc.semaphore(name.replace(":", "_")))
            self.sems[name] = h
            self.cnt[name] = 0
        return self.sems[name]

    def step(self, eng, fn, reads=(), writes=(), dma=False, small=False):
        i = len(self.steps)
        s = _Step(i, eng, fn, dma)
        s.small = small
        for t in reads:
            w = self.lastw.get(t)
            if w is not None:
                s.deps.add(w)
        for t in writes:
            w = self.lastw.get(t)
            if w is not None:
                s.deps.add(w)
            for r in self.readers.get(t, ()):
                s.deps.add(r)
        for t in reads:
            self.readers.setdefault(t, []).append(i)
        for t in writes:
            self.lastw[t] = i
            self.readers[t] = []
        if dma:
            q = self.qidx.get(eng, 0)
            self.qidx[eng] = q + 1
            name = f"q:{eng}:{q % QDEPTH}"
            self._sem(name)
            s.sem = name
            s.inc = 16
            self.cnt[name] += 16
            s.cum = self.cnt[name]
        self.steps.append(s)
        return i

    def flush(self):
        new = self.steps[self.fstart:]
        if not new:
            return
        need = set()
        for s in new:
            for d in s.deps:
                if d < self.fstart:
                    continue
                dd = self.steps[d]
                if dd.dma:
                    continue
                if dd.eng == s.eng and not s.dma and (
                        s.eng == "tensor"
                        or not (SAME_ENGINE_SYNC and (s.small or dd.small))):
                    continue
                need.add(d)
        lastof = {}
        for s in new:
            lastof[s.eng] = s.idx
            if s.dma:
                lastof["q:" + s.eng] = s.idx
        for e, i in lastof.items():
            if not e.startswith("q:") and not self.steps[i].dma:
                need.add(i)
        for d in sorted(need):
            dd = self.steps[d]
            if dd.sem is None:
                name = f"c:{dd.eng}"
                self._sem(name)
                dd.sem = name
                dd.inc = 1
                self.cnt[name] += 1
                dd.cum = self.cnt[name]
        bar = {k: v for k, v in self.last_event.items()
               if k != "q:gpsimd"}
        engs_here = [e for e in ENGS if any(s.eng == e for s in new)]
        with self.nc.Block(f"fl{self.nflush}") as block:
            for ename in engs_here:
                def section(eng, ename=ename):
                    first = True
                    for s in new:
                        if s.eng != ename:
                            continue
                        waits = {}
                        if first:
                            for (sn, cv) in bar.values():
                                if sn is not None:
                                    waits[sn] = max(waits.get(sn, 0), cv)
                            first = False
                        if s.dma:
                            pdepth = s.cum - 16 * QDEPTH
                            if pdepth > 0:
                                waits[s.sem] = max(waits.get(s.sem, 0), pdepth)
                        for d in s.deps:
                            dd = self.steps[d]
                            if dd.sem is not None:
                                waits[dd.sem] = max(waits.get(dd.sem, 0), dd.cum)
                            elif d < self.fstart:
                                ev = self.last_event.get(dd.eng)
                                if ev is not None and ev[0] is not None:
                                    waits[ev[0]] = max(waits.get(ev[0], 0), ev[1])
                        hww = self.hw[ename]
                        for sn, v in sorted(waits.items()):
                            if v > hww.get(sn, 0):
                                eng.wait_ge(self.sems[sn], v)
                                hww[sn] = v
                        inst = s.fn(eng)
                        if s.sem is not None:
                            if inst is None:
                                raise RuntimeError("inc step returned no inst")
                            inst.then_inc(self.sems[s.sem], s.inc)
                getattr(block, ename)(section)
        for e, i in lastof.items():
            s = self.steps[i]
            self.last_event[e] = (s.sem, s.cum)
        self.fstart = len(self.steps)
        self.nflush += 1

    def finish(self):
        self.flush()
        bar = dict(self.last_event)
        with self.nc.Block("fin") as block:
            def section(eng):
                for (sn, cv) in sorted(bar.values()):
                    if sn is not None:
                        eng.wait_ge(self.sems[sn], cv)
            block.sync(section)



def _host_prepare(inp):
    f32 = np.float32
    q_w = np.asarray(inp["L0_attn_q_w"], f32)
    k_w = np.asarray(inp["L0_attn_k_w"], f32)
    v_w = np.asarray(inp["L0_attn_v_w"], f32)

    def bd_prod(w):
        t = w.reshape(L, 16, 32, 4, 4)
        out = np.zeros((L, 16, 32, 4, 32, 4), f32)
        for n in range(32):
            out[:, :, n, :, n, :] = t[:, :, n].transpose(0, 1, 3, 2)
        return out.reshape(L, 16, 128, 128)

    def bd_T(w):
        t = w.reshape(L, 16, 32, 4, 4)
        out = np.zeros((L, 16, 32, 4, 32, 4), f32)
        for n in range(32):
            out[:, :, n, :, n, :] = t[:, :, n]
        return out.reshape(L, 16, 128, 128)

    bd = np.stack([bd_prod(q_w), bd_prod(k_w), bd_prod(v_w)], axis=1)
    bd = np.ascontiguousarray(bd.reshape(L, 48, 128, 128))
    bdvT = np.ascontiguousarray(bd_T(v_w))

    ig_w = np.asarray(inp["L0_attn_ig_w"], f32)
    fg_w = np.asarray(inp["L0_attn_fg_w"], f32)
    g8 = np.concatenate([ig_w, fg_w], axis=1)
    gT = np.ascontiguousarray(g8.transpose(0, 2, 1))
    g_qk = np.ascontiguousarray(gT[:, :4096].reshape(L, 32, 128, 8))
    g_v = np.ascontiguousarray(gT[:, 4096:].reshape(L, 16, 128, 8))
    gbias = np.ascontiguousarray(
        np.concatenate([np.asarray(inp["L0_attn_ig_b"], f32),
                        np.asarray(inp["L0_attn_fg_b"], f32)], axis=1).T)

    def cols(v, nchunk):
        a = np.asarray(v, f32).reshape(-1, nchunk, 128)
        return np.ascontiguousarray(a.transpose(2, 0, 1).reshape(128, -1))

    convw = np.asarray(inp["L0_attn_conv_w"], f32)
    convw_c = np.ascontiguousarray(
        convw.reshape(L, 16, 128, KCONV).transpose(2, 0, 1, 3)
        .reshape(128, L * 16 * KCONV))
    convb_c = cols(inp["L0_attn_conv_b"], 16)
    lnw_c = cols(inp["L0_attn_ln_w"], 8)
    skip_c = cols(inp["L0_attn_skip"], 16)
    onorm_c = cols(inp["L0_attn_onorm_w"], 16)
    postw_c = cols(inp["L0_attn_post_w"], 8)
    latb_c = cols(inp["L0_attn_lat_b"], 8)
    b1_c = np.ascontiguousarray(
        np.asarray(inp["L0_ffn_b1"], f32).reshape(EXP, 16, 128)
        .transpose(2, 0, 1).reshape(128, EXP * 16))
    b2_e = np.ascontiguousarray(np.asarray(inp["L0_ffn_b2"], f32))

    up_wT = np.ascontiguousarray(
        np.asarray(inp["L0_attn_up_w"], f32).transpose(0, 2, 1))
    down_wT = np.ascontiguousarray(
        np.asarray(inp["L0_attn_down_w"], f32).transpose(0, 2, 1))
    latT = np.ascontiguousarray(np.asarray(inp["L0_attn_lat_w"], f32).T)
    rT = np.ascontiguousarray(np.asarray(inp["L0_ffn_router_w"], f32).T)
    w1T = np.ascontiguousarray(
        np.asarray(inp["L0_ffn_w1"], f32).transpose(0, 2, 1))
    w2T = np.ascontiguousarray(
        np.asarray(inp["L0_ffn_w2"], f32).transpose(0, 2, 1))

    ident = np.eye(128, dtype=f32)
    masks = np.zeros((4, 128, 512), f32)
    p = np.arange(128)[:, None]
    fi = np.arange(512)[None, :]
    for o in range(4):
        masks[o] = (p <= fi - 128 * o).astype(f32)
    masks = np.ascontiguousarray(masks.transpose(1, 0, 2).reshape(128, 4 * 512))

    shared = dict(
        bd=bd, bdvT=bdvT, g_qk=g_qk, g_v=g_v, gbias=gbias,
        convw=convw_c, convb=convb_c, lnw=lnw_c, skipc=skip_c,
        onormc=onorm_c, postw=postw_c, latb=latb_c, b1c=b1_c, b2e=b2_e,
        up_wT=up_wT, down_wT=down_wT, latT=latT, rT=rT, w1T=w1T, w2T=w2T,
        ident=ident, masks=masks,
    )
    tokens = np.asarray(inp["L0_attn_tokens_batch"], f32)
    in_maps = []
    for core in range(NCORES):
        b = core // 2
        m = dict(shared)
        m["xT"] = np.ascontiguousarray(tokens[b].T)
        in_maps.append(m)
    return in_maps



def build_program(taps=()):
    nc = bass.Bass("TRN2", target_bir_lowering=False)
    em = Em(nc)
    dt = {}

    def din(name, shape):
        dt[name] = nc.dram_tensor(name, list(shape), F32, kind="ExternalInput")

    din("xT", (D, S))
    din("bd", (L, 48, 128, 128))
    din("bdvT", (L, 16, 128, 128))
    din("g_qk", (L, 32, 128, 8))
    din("g_v", (L, 16, 128, 8))
    din("gbias", (8, L))
    din("convw", (128, L * 16 * KCONV))
    din("convb", (128, L * 16))
    din("lnw", (128, L * 8))
    din("skipc", (128, L * 16))
    din("onormc", (128, L * 16))
    din("postw", (128, 8))
    din("latb", (128, 8))
    din("b1c", (128, EXP * 16))
    din("b2e", (EXP, D))
    din("up_wT", (L, D, 2 * INNER))
    din("down_wT", (L, INNER, D))
    din("latT", (D, D))
    din("rT", (D, EXP))
    din("w1T", (EXP, D, FF))
    din("w2T", (EXP, FF, D))
    din("ident", (128, 128))
    din("masks", (128, 4 * 512))

    o_out = nc.dram_tensor("o", [D, SHALF], F32, kind="ExternalOutput")
    o2_out = nc.dram_tensor("o2", [D, SHALF], F32, kind="ExternalOutput")

    def scr(name, shape, dtype):
        dt[name] = nc.dram_tensor(name, list(shape), dtype, kind="Internal")

    scr("sp_x", (D, S), F32)
    scr("sp_x1", (D, S), F32)
    scr("sp_xn", (D, S), BF16)
    scr("sp_z", (INNER, S), BF16)
    scr("sp_xc", (INNER, S), BF16)
    scr("sp_q", (INNER, S), BF16)
    scr("sp_k", (INNER, S), BF16)
    scr("sp_v", (S, INNER), BF16)
    scr("sp_hnT", (INNER, S), BF16)
    scr("sp_g8", (8, S), F32)
    scr("sp_rows", (12, S), F32)
    scr("sp_dbg", (128, 4096), F32)
    scr("sp_dbg2", (128, 8192), F32)
    scr("sp_cmb", (EXP, S), F32)
    scr("sp_acc", (D, S), F32)

    tap_t = {}
    for name in taps:
        h = dt[name]
        tap_t[name] = nc.dram_tensor(
            "tap_" + name, list(h.shape), h.dtype, kind="ExternalOutput")

    ctx = nc.ctx
    identb = ctx.enter_context(nc.sbuf_tensor("identb_s", [128, 128], BF16))
    identf = ctx.enter_context(nc.sbuf_tensor("identf_s", [128, 128], F32))
    masks_sb = ctx.enter_context(nc.sbuf_tensor("masks_s", [128, 4 * 512], BF16))
    ones_r = ctx.enter_context(nc.sbuf_tensor("ones_r", [1, 128], F32))
    ones_c = ctx.enter_context(nc.sbuf_tensor("ones_c", [128, 1], BF16))
    constv = ctx.enter_context(nc.sbuf_tensor("constv", [128, 4], F32))
    lnw_sb = ctx.enter_context(nc.sbuf_tensor("lnw_s", [128, L * 8], F32))
    convw_sb = ctx.enter_context(nc.sbuf_tensor("convw_s", [128, L * 64], F32))
    convb_sb = ctx.enter_context(nc.sbuf_tensor("convb_s", [128, L * 16], F32))
    skip_sb = ctx.enter_context(nc.sbuf_tensor("skipc_s", [128, L * 16], F32))
    onorm_sb = ctx.enter_context(nc.sbuf_tensor("onormc_s", [128, L * 16], F32))
    postw_sb = ctx.enter_context(nc.sbuf_tensor("postw_s", [128, 8], F32))
    latb_sb = ctx.enter_context(nc.sbuf_tensor("latb_s", [128, 8], F32))
    gbias_sb = ctx.enter_context(nc.sbuf_tensor("gbias_s", [8, L], F32))
    b1_sb = ctx.enter_context(nc.sbuf_tensor("b1c_s", [128, EXP * 16], F32))
    b2_sb = ctx.enter_context(nc.sbuf_tensor("b2e_s", [EXP, D], BF16))

    def dma(eng, out_ap, in_ap, reads=(), writes=()):
        return em.step(eng, lambda e, o=out_ap, i=in_ap: e.dma_start(o, i),
                       reads=reads, writes=writes, dma=True)

    dma("gpsimd", identb[:, :], dt["ident"].ap(), writes=["identb"])
    dma("sync", identf[:, :], dt["ident"].ap(), writes=["identf"])
    dma("gpsimd", masks_sb[:, :], dt["masks"].ap(), writes=["masks"])
    dma("sync", lnw_sb[:, :], dt["lnw"].ap(), writes=["lnw"])
    dma("sync", convw_sb[:, :], dt["convw"].ap(), writes=["convw"])
    dma("sync", convb_sb[:, :], dt["convb"].ap(), writes=["convb"])
    dma("sync", skip_sb[:, :], dt["skipc"].ap(), writes=["skipc"])
    dma("sync", onorm_sb[:, :], dt["onormc"].ap(), writes=["onormc"])
    dma("sync", postw_sb[:, :], dt["postw"].ap(), writes=["postw"])
    dma("sync", latb_sb[:, :], dt["latb"].ap(), writes=["latb"])
    dma("sync", gbias_sb[:, :], dt["gbias"].ap(), writes=["gbias"])
    dma("sync", b1_sb[:, :], dt["b1c"].ap(), writes=["b1c"])
    dma("gpsimd", b2_sb[:, :], dt["b2e"].ap(), writes=["b2e"])
    em.step("vector", lambda e: e.memset(ones_r[:, :], 1.0), writes=["ones_r"])
    em.step("vector", lambda e: e.memset(ones_c[:, :], 1.0), writes=["ones_c"])
    em.step("vector", lambda e: e.memset(constv[:, 0:1], 1e-5), writes=["constv"])
    em.step("vector", lambda e: e.memset(constv[:, 1:2], 1.0), writes=["constv"])
    em.step("vector", lambda e: e.memset(constv[:, 2:3], NEG_HALF_LOG_DH),
            writes=["constv"])

    def ln_block_q(pref, xtok, xblk, nch, w, out, otok, scale_col_fn, lb):
        pieces = []
        for c in range(nch):
            pieces.append(lambda c=c: _ln_cast(
                pref, xtok, xblk, nch, w, c, lb))
        pieces.append(lambda: _ln_stats_a(pref, w, lb))
        pieces.append(None)
        pieces.append(lambda: _ln_stats_b(pref, w, lb))
        for c in range(nch):
            pieces.append(lambda c=c: _ln_norm(
                pref, xtok, xblk, nch, w, out, otok, scale_col_fn, c, lb))
        return pieces

    def ln_block(pref, xtok, xblk, nch, w, out, otok, scale_col_fn, lb):
        for p in ln_block_q(pref, xtok, xblk, nch, w, out, otok,
                            scale_col_fn, lb):
            if p is not None:
                p()

    def _ln_cast(pref, xtok, xblk, nch, w, c, lb):
        xb16, sqb = lb["xb16"], lb["sqb"]
        pLN = lb["pLN"]
        ptok = pref + "pLN"
        pool_stats = lb.get("pool_stats", False)
        if True:
            cb = c % 2
            ceng = "gpsimd" if pool_stats else "vector"
            em.step(ceng, lambda e, c=c, cb=cb: e.tensor_copy(
                xb16[:, cb * w:(cb + 1) * w], xblk[:, c * w:(c + 1) * w]),
                reads=[xtok], writes=[pref + "xb16%d" % cb])
            if pool_stats:
                em.step("gpsimd", lambda e, c=c, cb=cb: e.tensor_tensor(
                    sqb[:, cb * w:(cb + 1) * w], xblk[:, c * w:(c + 1) * w],
                    xblk[:, c * w:(c + 1) * w], ALU.mult),
                    reads=[xtok], writes=[pref + "sqb%d" % cb])
            else:
                em.step("scalar", lambda e, c=c, cb=cb: e.activation(
                    sqb[:, cb * w:(cb + 1) * w],
                    xblk[:, c * w:(c + 1) * w], AF.Square),
                    reads=[xtok], writes=[pref + "sqb%d" % cb])

            def f_s(e, c=c, cb=cb):
                i = e.matmul(pLN[0:1, 0:w], ones_c[:, :],
                             xb16[:, cb * w:(cb + 1) * w],
                             start=(c == 0), stop=(c == nch - 1))
                i = e.matmul(pLN[0:1, w:2 * w], ones_c[:, :],
                             sqb[:, cb * w:(cb + 1) * w],
                             start=(c == 0), stop=(c == nch - 1))
                return i
            em.step("tensor", f_s,
                    reads=[pref + "xb16%d" % cb, pref + "sqb%d" % cb,
                           "ones_c"],
                    writes=[ptok])

    def _ln_stats_a(pref, w, lb):
        pLN = lb["pLN"]
        rowmu, rowrs, rowt = lb["rowmu"], lb["rowrs"], lb["rowt"]
        ptok = pref + "pLN"
        em.step("vector", lambda e: e.tensor_scalar(
            rowmu[:, :], pLN[0:1, 0:w], 1.0 / D, None, ALU.mult),
            reads=[ptok], writes=[pref + "rowmu"], small=True)
        em.step("vector", lambda e: e.tensor_scalar(
            rowt[:, :], pLN[0:1, w:2 * w], 1.0 / D, None, ALU.mult),
            reads=[ptok], writes=[pref + "rowt"], small=True)
        em.step("vector", lambda e: e.tensor_tensor(
            rowrs[:, :], rowmu[:, :], rowmu[:, :], ALU.mult),
            reads=[pref + "rowmu"], writes=[pref + "rowrs"], small=True)
        em.step("vector", lambda e: e.tensor_tensor(
            rowt[:, :], rowt[:, :], rowrs[:, :], ALU.subtract),
            reads=[pref + "rowt", pref + "rowrs"], writes=[pref + "rowt"], small=True)
        em.step("scalar", lambda e: e.activation(
            rowrs[:, :], rowt[:, :], AF.Sqrt, bias=constv[0:1, 0:1]),
            reads=[pref + "rowt", "constv"], writes=[pref + "rowrs"], small=True)
        em.step("vector", lambda e: e.reciprocal(rowrs[:, :], rowrs[:, :]),
                reads=[pref + "rowrs"], writes=[pref + "rowrs"], small=True)

    def _ln_stats_b(pref, w, lb):
        pB = lb["pB"]
        rowmu, rowrs = lb["rowmu"], lb["rowrs"]
        muB, rstdB = lb["muB"], lb["rstdB"]
        pbtok = lb.get("pbtok", pref + "pB")
        em.step("tensor", lambda e: e.matmul(
            pB[:, 0:w], ones_r[0:1, :], rowmu[:, :], start=True, stop=True),
            reads=[pref + "rowmu", "ones_r"], writes=[pbtok], small=True)
        em.step("scalar", lambda e: e.activation(
            muB[:, :], pB[:, 0:w], AF.Copy), reads=[pbtok],
            writes=[pref + "muB"])
        em.step("tensor", lambda e: e.matmul(
            pB[:, 0:w], ones_r[0:1, :], rowrs[:, :], start=True, stop=True),
            reads=[pref + "rowrs", "ones_r", pref + "muB"], writes=[pbtok], small=True)
        em.step("scalar", lambda e: e.activation(
            rstdB[:, :], pB[:, 0:w], AF.Copy), reads=[pbtok],
            writes=[pref + "rstdB"])

    def _ln_norm(pref, xtok, xblk, nch, w, out, otok, scale_col_fn, c, lb):
        rstdB, muB, t1 = lb["rstdB"], lb["muB"], lb["t1"]
        em.step("vector", lambda e, c=c: e.tensor_tensor(
            t1[:, :], xblk[:, c * w:(c + 1) * w], muB[:, :],
            ALU.subtract), reads=[xtok, pref + "muB"],
            writes=[pref + "t1"])
        sc = scale_col_fn(c) if scale_col_fn else None
        if sc is not None:
            em.step("vector", lambda e, c=c, sc=sc: e.scalar_tensor_tensor(
                out[:, c * w:(c + 1) * w], t1[:, :], sc, rstdB[:, :],
                ALU.mult, ALU.mult),
                reads=[pref + "t1", pref + "rstdB"], writes=[otok])
        else:
            em.step("vector", lambda e, c=c: e.tensor_tensor(
                out[:, c * w:(c + 1) * w], t1[:, :], rstdB[:, :],
                ALU.mult), reads=[pref + "t1", pref + "rstdB"],
                writes=[otok])

    spx_r = dt["sp_x"].ap().rearrange("(c p) t -> p c t", p=128)
    spx1_r = dt["sp_x1"].ap().rearrange("(c p) t -> p c t", p=128)
    spxn_r = dt["sp_xn"].ap().rearrange("(c p) t -> p c t", p=128)
    spz_r = dt["sp_z"].ap().rearrange("(c p) t -> p c t", p=128)
    spxc_r = dt["sp_xc"].ap().rearrange("(c p) t -> p c t", p=128)
    spq_r = dt["sp_q"].ap().rearrange("(c p) t -> p c t", p=128)
    spk_r = dt["sp_k"].ap().rearrange("(c p) t -> p c t", p=128)
    spv_r = dt["sp_v"].ap().rearrange("(s p) c -> p s c", p=128)
    sphnT_r = dt["sp_hnT"].ap().rearrange("(c p) t -> p c t", p=128)
    xT_r = dt["xT"].ap().rearrange("(c p) t -> p c t", p=128)
    o_r = o_out.ap().rearrange("(c p) t -> p c t", p=128)
    o2_r = o2_out.ap().rearrange("(c p) t -> p c t", p=128)

    with ExitStack() as _st:
        xblk = _st.enter_context(nc.sbuf_tensor("pp_x", [128, 2 * 8 * TBC], F32))
        oblk = _st.enter_context(nc.sbuf_tensor("pp_o", [128, 2 * 8 * TBC], F32))
        xb16 = _st.enter_context(nc.sbuf_tensor("pp_b16", [128, 2 * TBC], BF16))
        sqb = _st.enter_context(nc.sbuf_tensor("pp_sq", [128, 2 * TBC], BF16))
        rowmu = _st.enter_context(nc.sbuf_tensor("pp_rmu", [1, TBC], F32))
        rowrs = _st.enter_context(nc.sbuf_tensor("pp_rrs", [1, TBC], F32))
        rowt = _st.enter_context(nc.sbuf_tensor("pp_rt", [1, TBC], F32))
        muB = _st.enter_context(nc.sbuf_tensor("pp_muB", [128, TBC], F32))
        rstdB = _st.enter_context(nc.sbuf_tensor("pp_rsB", [128, TBC], F32))
        t1 = _st.enter_context(nc.sbuf_tensor("pp_t1", [128, TBC], F32))
        pLN = _st.enter_context(nc.psum_tensor("pp_pLN", [1, 2 * TBC], F32))
        pB = _st.enter_context(nc.psum_tensor("pp_pB", [128, TBC], F32))
        lb = dict(xb16=xb16, sqb=sqb, pLN=pLN, rowmu=rowmu, rowrs=rowrs,
                  rowt=rowt, pB=pB, muB=muB, rstdB=rstdB, t1=t1,
                  pool_stats=POOL_STATS)

        def pp_load(tb):
            par = tb % 2
            t0 = tb * TBC
            dma("scalar",
                xblk.ap().rearrange("p (b c t) -> p b c t", b=2, t=TBC)
                [:, par], xT_r[:, :, t0:t0 + TBC],
                writes=["pp_x%d" % par])
        pp_load(0)
        for tb in range(NTBC):
            t0 = tb * TBC
            par = tb % 2
            if tb + 1 < NTBC:
                pp_load(tb + 1)
            xv = xblk[:, par * 8 * TBC:(par + 1) * 8 * TBC]
            ov = oblk[:, par * 8 * TBC:(par + 1) * 8 * TBC]
            ln_block("pp", "pp_x%d" % par, xv, 8, TBC, ov,
                     "pp_o%d" % par, None, lb)
            dma("sync", spx_r[:, :, t0:t0 + TBC],
                ov.rearrange("p (c t) -> p c t", t=TBC),
                reads=["pp_o%d" % par], writes=["d:sp_x"])
        em.flush()

    for l in range(L_EFF):
        with ExitStack() as _st:
            uw1 = _st.enter_context(nc.sbuf_tensor(f"a_uw1{l}", [128, 8 * INNER], BF16))
            uw2 = _st.enter_context(nc.sbuf_tensor(f"a_uw2{l}", [128, 8 * INNER], BF16))
            bd_sb = _st.enter_context(nc.sbuf_tensor(f"a_bd{l}", [128, 48 * 128], BF16))
            gqk_sb = _st.enter_context(nc.sbuf_tensor(f"a_gqk{l}", [128, 32 * 8], BF16))
            bdvT_sb = _st.enter_context(nc.sbuf_tensor(f"a_bdvT{l}", [128, 16 * 128], BF16))
            gvr_sb = _st.enter_context(nc.sbuf_tensor(f"a_gvr{l}", [128, 16 * 8], BF16))
            gtv_sb = _st.enter_context(nc.sbuf_tensor(f"a_gtv{l}", [128, 16 * 8], BF16))
            xblk = _st.enter_context(nc.sbuf_tensor(f"a_x{l}", [128, 2 * 8 * TBA], F32))
            xnb = _st.enter_context(nc.sbuf_tensor(f"a_xnb{l}", [128, 2 * 8 * TBA], BF16))
            xb16 = _st.enter_context(nc.sbuf_tensor(f"a_b16{l}", [128, 2 * TBA], BF16))
            sqb = _st.enter_context(nc.sbuf_tensor(f"a_sq{l}", [128, 2 * TBA], BF16))
            rowmu = _st.enter_context(nc.sbuf_tensor(f"a_rmu{l}", [1, TBA], F32))
            rowrs = _st.enter_context(nc.sbuf_tensor(f"a_rrs{l}", [1, TBA], F32))
            rowt = _st.enter_context(nc.sbuf_tensor(f"a_rt{l}", [1, TBA], F32))
            muB = _st.enter_context(nc.sbuf_tensor(f"a_muB{l}", [128, TBA], F32))
            rstdB = _st.enter_context(nc.sbuf_tensor(f"a_rsB{l}", [128, TBA], F32))
            t1 = _st.enter_context(nc.sbuf_tensor(f"a_t1{l}", [128, TBA], F32))
            xmb = _st.enter_context(nc.sbuf_tensor(f"a_xm{l}", [128, 16 * (TBA + 3)], BF16))
            xcb = _st.enter_context(nc.sbuf_tensor(f"a_xc{l}", [128, 16 * TBA], BF16))
            cv1 = _st.enter_context(nc.sbuf_tensor(f"a_cv1{l}", [128, 2 * TBA], F32))
            cv2 = _st.enter_context(nc.sbuf_tensor(f"a_cv2{l}", [128, 2 * TBA], F32))
            qb = _st.enter_context(nc.sbuf_tensor(f"a_q{l}", [128, 16 * TBA], BF16))
            kb = _st.enter_context(nc.sbuf_tensor(f"a_k{l}", [128, 16 * TBA], BF16))
            vb = _st.enter_context(nc.sbuf_tensor(f"a_v{l}", [128, 2 * INNER], BF16))
            gates_a = _st.enter_context(nc.sbuf_tensor(f"a_g{l}", [8, TBA], F32))
            zb = _st.enter_context(nc.sbuf_tensor(f"a_zb{l}", [128, 8 * TBC], BF16))
            xn2 = _st.enter_context(nc.sbuf_tensor(f"a_xn2{l}", [128, 2 * 8 * TBC], BF16))
            pLN = _st.enter_context(nc.psum_tensor(f"a_pLN{l}", [1, 2 * TBA], F32))
            pB = _st.enter_context(nc.psum_tensor(f"a_pB{l}", [128, TBA], F32))
            pU0 = _st.enter_context(nc.psum_tensor(f"a_pU0{l}", [128, 2 * TBA], F32))
            pU1 = _st.enter_context(nc.psum_tensor(f"a_pU1{l}", [128, 2 * TBA], F32))
            pU2 = _st.enter_context(nc.psum_tensor(f"a_pU2{l}", [128, 2 * TBA], F32))
            pUs = [pU0, pU1, pU2]
            pUts = [f"a_pU0", f"a_pU1", f"a_pU2"]
            pVa = _st.enter_context(nc.psum_tensor(f"a_pVa{l}", [128, 512], F32))
            pVb = _st.enter_context(nc.psum_tensor(f"a_pVb{l}", [128, 512], F32))
            pV2 = _st.enter_context(nc.psum_tensor(f"a_pV2{l}", [128, 256], F32))
            lb = dict(xb16=xb16, sqb=sqb, pLN=pLN, rowmu=rowmu, rowrs=rowrs,
                      rowt=rowt, pB=pB, muB=muB, rstdB=rstdB, t1=t1,
                      pool_stats=POOL_STATS)
            dma("gpsimd", bdvT_sb[:, :].rearrange("p (j m) -> p j m", m=128),
                dt["bdvT"].ap()[l].rearrange("j p m -> p j m"),
                writes=["bdvT"])
            dma("gpsimd", gvr_sb[:, :].rearrange("p (j m) -> p j m", m=8),
                dt["g_v"].ap()[l].rearrange("j p m -> p j m"), writes=["gvr"])
            dma("gpsimd", gqk_sb[:, :].rearrange("p (j m) -> p j m", m=8),
                dt["g_qk"].ap()[l].rearrange("j p m -> p j m"),
                writes=["gqk"])
            for cq in range(4):
                dma("gpsimd",
                    uw1[:, :].rearrange("p (k m) -> p k m", m=INNER)
                    [:, :, cq * 512:(cq + 1) * 512],
                    dt["up_wT"].ap()[l].rearrange("(k p) m -> p k m", p=128)
                    [:, :, cq * 512:(cq + 1) * 512],
                    writes=["uw1c%d" % cq])
                if cq == 0:
                    dma("gpsimd",
                        bd_sb[:, :].rearrange("p (j m) -> p j m", m=128),
                        dt["bd"].ap()[l].rearrange("j p m -> p j m"),
                        writes=["bd"])
            dma("gpsimd", uw2[:, :].rearrange("p (k m) -> p k m", m=INNER),
                dt["up_wT"].ap()[l].rearrange("(k p) m -> p k m", p=128)
                [:, :, INNER:2 * INNER], writes=["uw2"])
            for c in range(16):
                pX = pVa if c % 2 == 0 else pVb
                pxt = "a_pVa" if c % 2 == 0 else "a_pVb"
                em.step("tensor", lambda e, c=c, pX=pX: e.matmul(
                    pX[:, 0:8], bdvT_sb[:, c * 128:(c + 1) * 128],
                    gvr_sb[:, c * 8:(c + 1) * 8], start=True, stop=True),
                    reads=["bdvT", "gvr", "gtv"], writes=[pxt])
                em.step("scalar", lambda e, c=c, pX=pX: e.activation(
                    gtv_sb[:, c * 8:(c + 1) * 8], pX[:, 0:8], AF.Copy),
                    reads=[pxt], writes=["gtv"])

            TW = TBA + 3

            def a_ln_q(tb):
                par = tb % 2
                t0 = tb * TBA
                xv = xblk[:, par * 8 * TBA:(par + 1) * 8 * TBA]
                xnv = xnb[:, par * 8 * TBA:(par + 1) * 8 * TBA]

                def p_load(par=par, t0=t0, xv=xv):
                    dma("sync", xv.rearrange("p (c t) -> p c t", t=TBA),
                        spx_r[:, :, t0:t0 + TBA], reads=["d:sp_x"],
                        writes=["a_x%d" % par])

                def p_store(par=par, t0=t0, xnv=xnv):
                    dma("sync", spxn_r[:, :, t0:t0 + TBA],
                        xnv.rearrange("p (c t) -> p c t", t=TBA),
                        reads=["a_xn%d" % par], writes=["d:sp_xn"])
                return ([p_load] +
                        ln_block_q("a", "a_x%d" % par, xv, 8, TBA, xnv,
                                   "a_xn%d" % par,
                                   (lambda c: lnw_sb[:,
                                              l * 8 + c:l * 8 + c + 1]),
                                   lb) +
                        [p_store])

            def a_ln(tb):
                for p in a_ln_q(tb):
                    if p is not None:
                        p()
            a_ln(0)
            for tb in range(NTBA):
                t0 = tb * TBA
                par = tb % 2
                xnv = xnb[:, par * 8 * TBA:(par + 1) * 8 * TBA]
                lnq = a_ln_q(tb + 1) if tb + 1 < NTBA else []
                for c in range(16):
                    if tb == 0:
                        em.step("vector", lambda e, c=c: e.memset(
                            xmb[:, c * TW:c * TW + 3], 0.0),
                            reads=["a_xm%d" % c], writes=["a_xm%d" % c])
                    else:
                        em.step("vector", lambda e, c=c: e.tensor_copy(
                            xmb[:, c * TW:c * TW + 3],
                            xmb[:, c * TW + TBA:c * TW + TBA + 3]),
                            reads=["a_xm%d" % c], writes=["a_xm%d" % c])
                def qkv_group(mq):
                    pV = pVa if mq % 2 == 0 else pVb
                    pvt = "a_pVa" if mq % 2 == 0 else "a_pVb"
                    vp = mq % 2

                    def f_qkm(e, m=mq, pV=pV):
                        e.matmul(pV[:, 0:TBA],
                                 bd_sb[:, m * 128:(m + 1) * 128],
                                 xcb[:, m * TBA:(m + 1) * TBA],
                                 start=True, stop=True)
                        i = e.matmul(pV[:, TBA:2 * TBA],
                                     bd_sb[:, (16 + m) * 128:
                                           (17 + m) * 128],
                                     xcb[:, m * TBA:(m + 1) * TBA],
                                     start=True, stop=True)
                        return i
                    em.step("tensor", f_qkm, reads=["bd", "a_xc%d" % mq],
                            writes=[pvt])
                    em.step("scalar", lambda e, m=mq, pV=pV: e.activation(
                        qb[:, m * TBA:(m + 1) * TBA], pV[:, 0:TBA],
                        AF.Copy), reads=[pvt], writes=["a_q"])
                    em.step("scalar", lambda e, m=mq, pV=pV: e.activation(
                        kb[:, m * TBA:(m + 1) * TBA], pV[:, TBA:2 * TBA],
                        AF.Copy), reads=[pvt], writes=["a_k"])

                    def f_vm(e, m=mq):
                        i = None
                        for sub in range(2):
                            i = e.matmul(
                                pV2[:, sub * 128:(sub + 1) * 128],
                                xmb[:, m * TW + 3 + sub * 128:
                                    m * TW + 3 + sub * 128 + 128],
                                bd_sb[:, (32 + m) * 128:(33 + m) * 128],
                                start=True, stop=True)
                        return i
                    em.step("tensor", f_vm, reads=["a_xm%d" % mq, "bd"],
                            writes=["a_pV2"])
                    em.step("scalar", lambda e, m=mq: e.activation(
                        vb.ap().rearrange("p (i n) -> p i n", n=INNER)
                        [:, :, m * 128:(m + 1) * 128],
                        pV2[:, :].rearrange("p (i s) -> p i s", s=128),
                        AF.Copy), reads=["a_pV2"], writes=["a_v"])

                for m in range(16):
                    pU = pUs[m % 3]
                    ptok = pUts[m % 3]

                    def f_up(e, m=m, pU=pU, xnv=xnv):
                        i = None
                        for k in range(8):
                            i = e.matmul(pU[:, 0:TBA],
                                         uw1[:, k * INNER + m * 128:
                                             k * INNER + m * 128 + 128],
                                         xnv[:, k * TBA:(k + 1) * TBA],
                                         start=(k == 0), stop=(k == 7))
                        return i
                    em.step("tensor", f_up,
                            reads=["uw1c%d" % (m // 4), "a_xn%d" % par],
                            writes=[ptok])
                    em.step("scalar", lambda e, m=m, pU=pU: e.activation(
                        xmb[:, m * TW + 3:m * TW + 3 + TBA], pU[:, 0:TBA],
                        AF.Copy), reads=[ptok],
                        writes=["a_xm%d" % m, "a_xm"])
                    if m >= 2:
                        qkv_group(m - 2)
                    base = m * TW
                    wl0 = (l * 16 + m) * KCONV
                    cp = m % 2
                    co = cp * TBA
                    em.step("vector", lambda e, base=base, wl0=wl0, co=co:
                            e.tensor_scalar(
                                cv1[:, co:co + TBA], xmb[:, base:base + TBA],
                                convw_sb[:, wl0:wl0 + 1], None, ALU.mult),
                            reads=["a_xm%d" % m, "convw"],
                            writes=["a_cv1%d" % cp])
                    em.step("vector", lambda e, base=base, wl0=wl0, co=co:
                            e.scalar_tensor_tensor(
                                cv2[:, co:co + TBA],
                                xmb[:, base + 1:base + 1 + TBA],
                                convw_sb[:, wl0 + 1:wl0 + 2],
                                cv1[:, co:co + TBA], ALU.mult, ALU.add),
                            reads=["a_xm%d" % m, "convw", "a_cv1%d" % cp],
                            writes=["a_cv2%d" % cp])
                    em.step("vector", lambda e, base=base, wl0=wl0, co=co:
                            e.scalar_tensor_tensor(
                                cv1[:, co:co + TBA],
                                xmb[:, base + 2:base + 2 + TBA],
                                convw_sb[:, wl0 + 2:wl0 + 3],
                                cv2[:, co:co + TBA], ALU.mult, ALU.add),
                            reads=["a_xm%d" % m, "convw", "a_cv2%d" % cp],
                            writes=["a_cv1%d" % cp])
                    em.step("vector", lambda e, base=base, wl0=wl0, co=co:
                            e.scalar_tensor_tensor(
                                cv2[:, co:co + TBA],
                                xmb[:, base + 3:base + 3 + TBA],
                                convw_sb[:, wl0 + 3:wl0 + 4],
                                cv1[:, co:co + TBA], ALU.mult, ALU.add),
                            reads=["a_xm%d" % m, "convw", "a_cv1%d" % cp],
                            writes=["a_cv2%d" % cp])
                    em.step("scalar", lambda e, m=m, co=co: e.activation(
                        xcb[:, m * TBA:(m + 1) * TBA], cv2[:, co:co + TBA],
                        AF.Silu,
                        bias=convb_sb[:, l * 16 + m:l * 16 + m + 1]),
                        reads=["a_cv2%d" % cp, "convb"],
                        writes=["a_xc%d" % m, "a_xc"])
                    if lnq and m < 14:
                        p = lnq.pop(0)
                        if p is not None:
                            p()
                for p in lnq:
                    if p is not None:
                        p()
                lnq = []
                qkv_group(14)
                qkv_group(15)

                def f_g(e):
                    i = None
                    for c in range(16):
                        i = e.matmul(pB[0:8, 0:TBA],
                                     gqk_sb[:, c * 8:(c + 1) * 8],
                                     qb[:, c * TBA:(c + 1) * TBA],
                                     start=(c == 0), stop=False)
                    for c in range(16):
                        i = e.matmul(pB[0:8, 0:TBA],
                                     gqk_sb[:, (16 + c) * 8:(17 + c) * 8],
                                     kb[:, c * TBA:(c + 1) * TBA],
                                     start=False, stop=False)
                    for c in range(16):
                        i = e.matmul(pB[0:8, 0:TBA],
                                     gtv_sb[:, c * 8:(c + 1) * 8],
                                     xmb[:, c * TW + 3:c * TW + 3 + TBA],
                                     start=False, stop=(c == 15))
                    return i
                em.step("tensor", f_g,
                        reads=["a_q", "a_k", "a_xm", "gqk", "gtv"] +
                              ["a_xm%d" % c for c in range(16)],
                        writes=["a_pB"])
                em.step("vector", lambda e: e.tensor_scalar(
                    gates_a[:, :], pB[0:8, 0:TBA],
                    gbias_sb[0:8, l:l + 1], None,
                    ALU.add), reads=["a_pB", "gbias"], writes=["a_gates"], small=True)
                dma("sync", dt["sp_g8"].ap()[:, t0:t0 + TBA], gates_a[:, :],
                    reads=["a_gates"], writes=["d:sp_g8"])
                dma("scalar", spxc_r[:, :, t0:t0 + TBA],
                    xcb.ap().rearrange("p (c t) -> p c t", t=TBA),
                    reads=["a_xc"], writes=["d:sp_xc"])
                dma("scalar", spq_r[:, :, t0:t0 + TBA],
                    qb.ap().rearrange("p (c t) -> p c t", t=TBA),
                    reads=["a_q"], writes=["d:sp_q"])
                dma("scalar", spk_r[:, :, t0:t0 + TBA],
                    kb.ap().rearrange("p (c t) -> p c t", t=TBA),
                    reads=["a_k"], writes=["d:sp_k"])
                dma("scalar", spv_r[:, t0 // 128:t0 // 128 + 2, :],
                    vb.ap().rearrange("p (s c) -> p s c", c=INNER),
                    reads=["a_v"], writes=["d:sp_v"])
            def a2_load(tb):
                par2 = tb % 2
                t0 = tb * TBC
                dma("scalar",
                    xn2[:, par2 * 8 * TBC:(par2 + 1) * 8 * TBC]
                    .rearrange("p (c t) -> p c t", t=TBC),
                    spxn_r[:, :, t0:t0 + TBC], reads=["d:sp_xn"],
                    writes=["a_xn2%d" % par2])
            a2_load(0)
            for tb in range(NTBC):
                t0 = tb * TBC
                par2 = tb % 2
                if tb + 1 < NTBC:
                    a2_load(tb + 1)
                for m in range(16):
                    pU = pUs[m % 3]
                    ptok = pUts[m % 3]

                    def f_up2(e, m=m, pU=pU, par2=par2):
                        i = None
                        for k in range(8):
                            i = e.matmul(pU[:, 0:TBC],
                                         uw2[:, k * INNER + m * 128:
                                             k * INNER + m * 128 + 128],
                                         xn2[:, par2 * 8 * TBC + k * TBC:
                                             par2 * 8 * TBC + (k + 1) * TBC],
                                         start=(k == 0), stop=(k == 7))
                        return i
                    em.step("tensor", f_up2,
                            reads=["uw2", "a_xn2%d" % par2],
                            writes=[ptok])
                    em.step("scalar", lambda e, m=m, pU=pU: e.activation(
                        zb[:, (m % 8) * TBC:(m % 8 + 1) * TBC], pU[:, 0:TBC],
                        AF.Copy), reads=[ptok], writes=["a_zb"])
                    if m % 8 == 7:
                        g0 = (m // 8) * 8
                        dma("sync", spz_r[:, g0:g0 + 8, t0:t0 + TBC],
                            zb.ap().rearrange("p (c t) -> p c t", t=TBC),
                            reads=["a_zb"], writes=["d:sp_z"])
            em.flush()

        dwst = ExitStack()
        dwA = dwst.enter_context(
            nc.sbuf_tensor(f"c_dwA{l}", [128, 8 * D], BF16))
        kvst = ExitStack()
        k_sb = kvst.enter_context(
            nc.sbuf_tensor(f"b_k{l}", [128, 4 * S], BF16))
        v_sb = kvst.enter_context(
            nc.sbuf_tensor(f"b_v{l}", [128, 32 * DH], BF16))

        with ExitStack() as _st:
            ig4 = _st.enter_context(nc.sbuf_tensor(f"s_ig{l}", [4, S], F32))
            fg4 = _st.enter_context(nc.sbuf_tensor(f"s_fg{l}", [4, S], F32))
            st1 = _st.enter_context(nc.sbuf_tensor(f"s_t1{l}", [4, S], F32))
            st2 = _st.enter_context(nc.sbuf_tensor(f"s_t2{l}", [4, S], F32))
            szero = _st.enter_context(nc.sbuf_tensor(f"s_zero{l}", [4, S], F32))
            for ch in range(4):
                dma("gpsimd",
                    k_sb.ap().rearrange("p (c t) -> p c t", t=S)
                    [:, :, ch * 1024:(ch + 1) * 1024],
                    spk_r[:, 0:4, ch * 1024:(ch + 1) * 1024],
                    reads=["d:sp_k"], writes=["b_kc%d" % ch])
                dma("gpsimd",
                    v_sb.ap().rearrange("p (s c) -> p s c", c=DH)
                    [:, ch * 8:(ch + 1) * 8, :],
                    spv_r[:, ch * 8:(ch + 1) * 8, 0:DH],
                    reads=["d:sp_v"], writes=["b_vc%d" % ch])
            dma("gpsimd", dwA.ap().rearrange("p (k m) -> p k m", m=D),
                dt["down_wT"].ap()[l].rearrange("(k p) m -> p k m", p=128)
                [:, 0:8, :], writes=["dwA"])
            dma("sync", ig4[:, :], dt["sp_g8"].ap()[0:4, :],
                reads=["d:sp_g8"], writes=["s_ig"])
            dma("sync", fg4[:, :], dt["sp_g8"].ap()[4:8, :],
                reads=["d:sp_g8"], writes=["s_fg"])
            em.step("vector", lambda e: e.memset(szero[:, :], 0.0),
                    writes=["s_zero"])
            em.step("scalar", lambda e: e.activation(
                st1[:, :], fg4[:, :], AF.Abs),
                reads=["s_fg"], writes=["s_t1"], small=True)
            em.step("scalar", lambda e: e.activation(
                st1[:, :], st1[:, :], AF.Exp, scale=-1.0),
                reads=["s_t1"], writes=["s_t1"], small=True)
            em.step("scalar", lambda e: e.activation(
                st1[:, :], st1[:, :], AF.Ln, bias=constv[0:4, 1:2]),
                reads=["s_t1", "constv"], writes=["s_t1"], small=True)
            em.step("vector", lambda e: e.tensor_scalar(
                st2[:, :], fg4[:, :], 0.0, None, ALU.min),
                reads=["s_fg"], writes=["s_t2"], small=True)
            em.step("vector", lambda e: e.tensor_tensor(
                st2[:, :], st2[:, :], st1[:, :], ALU.subtract),
                reads=["s_t2", "s_t1"], writes=["s_t2"], small=True)
            em.step("vector", lambda e: e.tensor_tensor_scan(
                st1[:, :], st2[:, :], szero[:, :], 0.0, ALU.add, ALU.add),
                reads=["s_t2", "s_zero"], writes=["s_t1"], small=True)
            em.step("vector", lambda e: e.tensor_tensor(
                fg4[:, :], ig4[:, :], st1[:, :], ALU.subtract),
                reads=["s_ig", "s_t1"], writes=["s_fg"], small=True)
            dma("sync", dt["sp_rows"].ap()[0:4, :], fg4[:, :],
                reads=["s_fg"], writes=["d:sp_rows"])
            em.step("vector", lambda e: e.tensor_tensor_scan(
                st2[:, :], fg4[:, :], szero[:, :], -3.0e38, ALU.max, ALU.add),
                reads=["s_fg", "s_zero"], writes=["s_t2"], small=True)
            dma("sync", dt["sp_rows"].ap()[8:12, :], st2[:, :],
                reads=["s_t2"], writes=["d:sp_rows"])
            em.step("vector", lambda e: e.tensor_tensor(
                st1[:, :], st1[:, :], st2[:, :], ALU.add),
                reads=["s_t1", "s_t2"], writes=["s_t1"], small=True)
            em.step("scalar", lambda e: e.activation(
                st1[:, :], st1[:, :], AF.Exp, scale=-1.0),
                reads=["s_t1"], writes=["s_t1"], small=True)
            dma("sync", dt["sp_rows"].ap()[4:8, :], st1[:, :],
                reads=["s_t1"], writes=["d:sp_rows"])
            em.flush()

        with ExitStack() as _st:
            q_sb = _st.enter_context(nc.sbuf_tensor(f"b_q{l}", [128, 2 * 4 * 512], BF16))
            mrow = _st.enter_context(nc.sbuf_tensor(f"b_mr{l}", [1, 2 * S], F32))
            MB = _st.enter_context(nc.sbuf_tensor(f"b_MB{l}", [128, 2 * 512], F32))
            g32 = _st.enter_context(nc.sbuf_tensor(f"b_g32{l}", [32, 2 * 128], F32))
            gcol = _st.enter_context(nc.sbuf_tensor(f"b_gcol{l}", [128, 2 * 32], F32))
            gcol2 = _st.enter_context(nc.sbuf_tensor(f"b_gcol2{l}", [128, 2 * 32], F32))
            ecol = _st.enter_context(nc.sbuf_tensor(f"b_ecol{l}", [128, 2 * 32], F32))
            epsb = _st.enter_context(nc.sbuf_tensor(f"b_eps{l}", [128, 3 * 512], F32))
            Eb = _st.enter_context(nc.sbuf_tensor(f"b_E{l}", [128, 3 * 512], BF16))
            ctb = _st.enter_context(nc.sbuf_tensor(f"b_ct{l}", [128, 3 * 512], BF16))
            nb = _st.enter_context(nc.sbuf_tensor(f"b_nb{l}", [128, 4], F32))
            ninv = _st.enter_context(nc.sbuf_tensor(f"b_ninv{l}", [128, 4], F32))
            stat = _st.enter_context(nc.sbuf_tensor(f"b_stat{l}", [128, 24], F32))
            hf = _st.enter_context(nc.sbuf_tensor(f"b_hf{l}", [128, 4 * 512], F32))
            hsq = _st.enter_context(nc.sbuf_tensor(f"b_hsq{l}", [128, 512], F32))
            hn = _st.enter_context(nc.sbuf_tensor(f"b_hn{l}", [128, 4 * 512], F32))
            hT = _st.enter_context(nc.sbuf_tensor(f"b_hT{l}", [128, 2048], BF16))
            pC0 = _st.enter_context(nc.psum_tensor(f"b_pC0{l}", [128, 512], F32))
            pC1 = _st.enter_context(nc.psum_tensor(f"b_pC1{l}", [128, 512], F32))
            pH0 = _st.enter_context(nc.psum_tensor(f"b_pH0{l}", [128, 512], F32))
            pH1 = _st.enter_context(nc.psum_tensor(f"b_pH1{l}", [128, 512], F32))
            pH2 = _st.enter_context(nc.psum_tensor(f"b_pH2{l}", [128, 512], F32))
            pH3 = _st.enter_context(nc.psum_tensor(f"b_pH3{l}", [128, 512], F32))
            pP = _st.enter_context(nc.psum_tensor(f"b_pP{l}", [128, 512], F32))
            pT = _st.enter_context(nc.psum_tensor(f"b_pT{l}", [128, 512], F32))
            pH = [pH0, pH1, pH2, pH3]
            pHt = ["b_pH0", "b_pH1", "b_pH2", "b_pH3"]
            pC = [pC0, pC1]
            pCt = ["b_pC0", "b_pC1"]
            def head_prologue(h):
                hp = h % 2
                dma("sync", mrow[0:1, hp * S:(hp + 1) * S],
                    dt["sp_rows"].ap()[8 + h:9 + h, :],
                    reads=["d:sp_rows"], writes=["b_mr%d" % hp])
                em.step("vector", lambda e, hp=hp: e.tensor_scalar(
                    mrow[0:1, hp * S:(hp + 1) * S],
                    mrow[0:1, hp * S:(hp + 1) * S], -1.0, None,
                    ALU.mult),
                    reads=["b_mr%d" % hp], writes=["b_mr%d" % hp],
                    small=True)
                for (src0, dst, dtok) in ((0, gcol, "b_gcol"),
                                          (4, ecol, "b_ecol")):
                    dma("sync", g32[:, hp * 128:(hp + 1) * 128],
                        dt["sp_rows"].ap()[src0 + h:src0 + h + 1, :]
                        .rearrange("r (j f) -> (r j) f", f=128),
                        reads=["d:sp_rows"], writes=["b_g32%d" % hp])
                    em.step("tensor", lambda e, hp=hp: e.transpose(
                        pT[:, 0:32], g32[:, hp * 128:(hp + 1) * 128],
                        identf[0:32, 0:32]),
                        reads=["b_g32%d" % hp, "identf"], writes=["b_pT"])
                    em.step("scalar", lambda e, dst=dst, hp=hp, dtok=dtok:
                            e.activation(
                                dst[:, hp * 32:(hp + 1) * 32],
                                pT[:, 0:32], AF.Copy),
                            reads=["b_pT"], writes=[dtok + str(hp)])
                em.step("vector", lambda e, hp=hp: e.tensor_scalar(
                    gcol2[:, hp * 32:(hp + 1) * 32],
                    gcol[:, hp * 32:(hp + 1) * 32], NEG_HALF_LOG_DH, None,
                    ALU.add),
                    reads=["b_gcol%d" % hp], writes=["b_gcol2%d" % hp],
                    small=True)

            def emit_mb(I, h):
                hp = h % 2
                mp = I % 2
                em.step("tensor", lambda e, I=I, hp=hp: e.matmul(
                    pT[:, :], ones_r[0:1, :],
                    mrow[0:1, hp * S + I * 512:hp * S + (I + 1) * 512],
                    start=True, stop=True),
                    reads=["b_mr%d" % hp, "ones_r", "b_MBc%d" % mp],
                    writes=["b_pT"], small=True)
                em.step("scalar", lambda e, mp=mp: e.activation(
                    MB[:, mp * 512:(mp + 1) * 512], pT[:, :], AF.Copy),
                    reads=["b_pT"], writes=["b_MBc%d" % mp])

            head_prologue(0)
            emit_mb(0, 0)
            for h in range(NHEAD):
                hp = h % 2
                if h > 0 and not PREFETCH_PRO:
                    head_prologue(h)
                if h > 0:
                    for ch in range(4):
                        dma("scalar",
                            k_sb.ap().rearrange("p (c t) -> p c t", t=S)
                            [:, :, ch * 1024:(ch + 1) * 1024],
                            spk_r[:, 4 * h:4 * h + 4,
                                  ch * 1024:(ch + 1) * 1024],
                            reads=["d:sp_k"], writes=["b_kc%d" % ch])
                        dma("scalar",
                            v_sb.ap().rearrange("p (s c) -> p s c", c=DH)
                            [:, ch * 8:(ch + 1) * 8, :],
                            spv_r[:, ch * 8:(ch + 1) * 8,
                                  DH * h:DH * h + DH],
                            reads=["d:sp_v"], writes=["b_vc%d" % ch])
                pending_fin = []
                pending_fa = []
                pending_fb = []

                def qload(I, h=h):
                    qi = I % 2
                    dma("sync",
                        q_sb.ap().rearrange("p (qi c t) -> p qi c t",
                                            qi=2, t=512)[:, qi],
                        spq_r[:, 4 * h:4 * h + 4, I * 512:I * 512 + 512],
                        reads=["d:sp_q"], writes=["b_q" + str(qi)])
                qload(0)
                for I in range(8):
                    s0 = I * 512
                    qi = I % 2
                    qv = q_sb.ap().rearrange("p (qi c t) -> p qi c t",
                                             qi=2, t=512)[:, qi]
                    if I + 1 < 8:
                        qload(I + 1)
                    nJ = 4 * I + 4

                    def emit_qk(J):
                        pc, pct = pC[J % 2], pCt[J % 2]

                        def f_qk(e, J=J, pc=pc, qv=qv):
                            i = None
                            for d in range(4):
                                i = e.matmul(pc[:, :],
                                             k_sb[:, d * S + J * 128:
                                                  d * S + J * 128 + 128],
                                             qv[:, d, :],
                                             start=(d == 0), stop=(d == 3))
                            return i
                        em.step("tensor", f_qk,
                                reads=["b_kc%d" % (J // 8),
                                       "b_q" + str(qi)],
                                writes=[pct])

                    def emit_e(J, I=None):
                        ei = J % 3
                        o = J - 4 * I
                        s0 = I * 512
                        if o < 0:
                            em.step("scalar",
                                    lambda e, ei=ei, J=J, mp=I % 2, hp=hp:
                                    e.activation(
                                        Eb[:, ei * 512:(ei + 1) * 512],
                                        MB[:, mp * 512:(mp + 1) * 512],
                                        AF.Exp,
                                        bias=gcol2[:, hp * 32 + J:
                                                   hp * 32 + J + 1]),
                                    reads=["b_MBc%d" % (I % 2),
                                           "b_gcol2%d" % hp],
                                    writes=["b_E" + str(ei)], small=True)
                        else:
                            em.step("vector",
                                    lambda e, ei=ei, J=J, mp=I % 2, hp=hp:
                                    e.tensor_scalar(
                                        epsb[:, ei * 512:(ei + 1) * 512],
                                        MB[:, mp * 512:(mp + 1) * 512],
                                        gcol[:, hp * 32 + J:hp * 32 + J + 1],
                                        0.0, ALU.add,
                                        ALU.min),
                                    reads=["b_MBc%d" % (I % 2),
                                           "b_gcol%d" % hp],
                                    writes=["b_eps" + str(ei)], small=True)
                            em.step("scalar", lambda e, ei=ei: e.activation(
                                Eb[:, ei * 512:(ei + 1) * 512],
                                epsb[:, ei * 512:(ei + 1) * 512], AF.Exp,
                                bias=constv[:, 2:3]),
                                reads=["b_eps" + str(ei), "constv"],
                                writes=["b_E" + str(ei)])
                            em.step("gpsimd" if MASK_POOL else "vector",
                                    lambda e, ei=ei, o=o:
                                    e.tensor_tensor(
                                        Eb[:, ei * 512:(ei + 1) * 512],
                                        Eb[:, ei * 512:(ei + 1) * 512],
                                        masks_sb[:, o * 512:(o + 1) * 512],
                                        ALU.mult),
                                    reads=["b_E" + str(ei), "masks"],
                                    writes=["b_E" + str(ei)])

                    def emit_ct(J):
                        ei = J % 3
                        pc, pct = pC[J % 2], pCt[J % 2]
                        em.step("vector", lambda e, ei=ei, pc=pc:
                                e.tensor_tensor(
                                    ctb[:, ei * 512:(ei + 1) * 512],
                                    pc[:, :], Eb[:, ei * 512:(ei + 1) * 512],
                                    ALU.mult),
                                reads=[pct, "b_E" + str(ei)],
                                writes=["b_ct" + str(ei)])

                    def emit_av(J, nJ):
                        ei = J % 3

                        def f_av(e, J=J, ei=ei, nJ=nJ):
                            first = (J == 0)
                            i = None
                            for sub in range(4):
                                i = e.matmul(pH[sub][:, :],
                                             ctb[:, ei * 512 + sub * 128:
                                                 ei * 512 + sub * 128 + 128],
                                             v_sb[:, J * DH:(J + 1) * DH],
                                             start=first,
                                             stop=(J == nJ - 1))
                                i = e.matmul(pP[:, sub:sub + 1],
                                             ctb[:, ei * 512 + sub * 128:
                                                 ei * 512 + sub * 128 + 128],
                                             ones_c[:, :],
                                             start=(first and sub == 0),
                                             stop=(J == nJ - 1))
                            return i
                        em.step("tensor", f_av,
                                reads=["b_ct" + str(ei), "b_vc%d" % (J // 8),
                                       "ones_c"],
                                writes=pHt + ["b_pP"])

                    for J in range(nJ):
                        emit_qk(J)
                        if J == 1:
                            if I + 1 < 8:
                                emit_mb(I + 1, h)
                            elif h + 1 < NHEAD:
                                emit_mb(0, h + 1)
                        if J == 2 and pending_fa:
                            for fn in pending_fa:
                                fn()
                            pending_fa = []
                        if J == 4 and pending_fb:
                            for fn in pending_fb:
                                fn()
                            pending_fb = []
                        if J == 7 and pending_fin:
                            for fn in pending_fin:
                                fn()
                            pending_fin = []
                        emit_e(J, I=I)
                        emit_ct(J)
                        if J >= 2:
                            emit_av(J - 2, nJ)
                    if pending_fa:
                        for fn in pending_fa:
                            fn()
                        pending_fa = []
                    if pending_fb:
                        for fn in pending_fb:
                            fn()
                        pending_fb = []
                    if pending_fin:
                        for fn in pending_fin:
                            fn()
                        pending_fin = []
                    emit_av(nJ - 2, nJ)
                    emit_av(nJ - 1, nJ)

                    def fin_a(I=I):
                        em.step("scalar", lambda e: e.activation(
                            nb[:, :], pP[:, 0:4], AF.Abs),
                            reads=["b_pP"], writes=["b_nb"], small=True)
                        em.step("vector", lambda e, I=I, hp=hp:
                            e.tensor_tensor(
                            nb[:, :], nb[:, :],
                            ecol[:, hp * 32 + 4 * I:hp * 32 + 4 * I + 4],
                            ALU.max), reads=["b_nb", "b_ecol%d" % hp],
                            writes=["b_nb"], small=True)
                        em.step("vector", lambda e: e.tensor_scalar(
                            nb[:, :], nb[:, :], 1e-6, None, ALU.add),
                            reads=["b_nb"], writes=["b_nb"], small=True)
                        em.step("vector", lambda e: e.reciprocal(
                            ninv[:, :], nb[:, :]), reads=["b_nb"],
                            writes=["b_ninv"], small=True)
                        for sub in range(4):
                            em.step("scalar", lambda e, sub=sub: e.activation(
                                hf[:, sub * 512:(sub + 1) * 512],
                                pH[sub][:, :], AF.Copy,
                                scale=ninv[:, sub:sub + 1],
                                accum_out=stat[:, sub:sub + 1]),
                                reads=[pHt[sub], "b_ninv"],
                                writes=["b_hf%d" % sub, "b_stat"],
                                small=True)
                    pending_fa.append(fin_a)

                    def fin_b():
                        for sub in range(4):
                            em.step("scalar", lambda e, sub=sub:
                                    e.activation(
                                        hsq[:, :],
                                        hf[:, sub * 512:(sub + 1) * 512],
                                        AF.Square,
                                        accum_out=stat[:, 4 + sub:5 + sub]),
                                    reads=["b_hf%d" % sub],
                                    writes=["b_hsq", "b_stat"], small=True)
                        em.step("vector", lambda e: e.tensor_scalar(
                            stat[:, 8:12], stat[:, 0:4], 1.0 / DH, None,
                            ALU.mult), reads=["b_stat"], writes=["b_stat"],
                            small=True)
                        em.step("vector", lambda e: e.tensor_tensor(
                            stat[:, 12:16], stat[:, 8:12], stat[:, 8:12],
                            ALU.mult), reads=["b_stat"], writes=["b_stat"],
                            small=True)
                        em.step("vector", lambda e: e.scalar_tensor_tensor(
                            stat[:, 16:20], stat[:, 4:8], 1.0 / DH,
                            stat[:, 12:16], ALU.mult, ALU.subtract),
                            reads=["b_stat"], writes=["b_stat"], small=True)
                        em.step("scalar", lambda e: e.activation(
                            stat[:, 16:20], stat[:, 16:20], AF.Sqrt,
                            bias=constv[:, 0:1]),
                            reads=["b_stat", "constv"], writes=["b_stat"],
                            small=True)
                        em.step("vector", lambda e: e.reciprocal(
                            stat[:, 20:24], stat[:, 16:20]),
                            reads=["b_stat"], writes=["b_stat"], small=True)
                        for sub in range(4):
                            em.step("vector", lambda e, sub=sub:
                                    e.tensor_scalar(
                                        hn[:, sub * 512:(sub + 1) * 512],
                                        hf[:, sub * 512:(sub + 1) * 512],
                                        stat[:, 8 + sub:9 + sub],
                                        stat[:, 20 + sub:21 + sub],
                                        ALU.subtract, ALU.mult),
                                    reads=["b_hf%d" % sub, "b_stat"],
                                    writes=["b_hn" + str(sub)], small=True)
                    pending_fb.append(fin_b)

                    for sub in range(4):
                        def fin_tr(sub=sub, I=I, h=h, s0=s0):
                            def f_tr(e, sub=sub):
                                i = None
                                for cdh in range(4):
                                    i = e.transpose(
                                        pT[:, cdh * 128:(cdh + 1) * 128],
                                        hn[:, sub * 512 + cdh * 128:
                                           sub * 512 + (cdh + 1) * 128],
                                        identf[:, :])
                                return i
                            em.step("tensor", f_tr,
                                    reads=["b_hn" + str(sub), "identf"],
                                    writes=["b_pT"])
                            em.step("vector", lambda e, sub=sub:
                                    e.tensor_copy(
                                        hT.ap().rearrange(
                                            "p (c i s) -> p c i s",
                                            c=4, i=4, s=128)[:, :, sub, :],
                                        pT[:, 0:512].rearrange(
                                            "p (c s) -> p c s", s=128)),
                                    reads=["b_pT"], writes=["b_hT"])
                        pending_fin.append(fin_tr)

                    def fin_dma(h=h, s0=s0):
                        dma("sync",
                            sphnT_r[:, 4 * h:4 * h + 4, s0:s0 + 512],
                            hT.ap().rearrange("p (c t) -> p c t", t=512),
                            reads=["b_hT"], writes=["d:sp_hnT"])
                    pending_fin.append(fin_dma)
                    if I == 5 and h + 1 < NHEAD and PREFETCH_PRO:
                        head_prologue(h + 1)
                if pending_fa:
                    for fn in pending_fa:
                        fn()
                    pending_fa = []
                if pending_fb:
                    for fn in pending_fb:
                        fn()
                    pending_fb = []
                if pending_fin:
                    for fn in pending_fin:
                        fn()
                    pending_fin = []
            em.flush()

        kvst.close()

        TBX = 256
        NTBX = S // TBX
        with ExitStack() as _st:
            dwB = _st.enter_context(nc.sbuf_tensor(f"c_dwB{l}", [128, 8 * D], BF16))
            hTb = _st.enter_context(nc.sbuf_tensor(f"c_hT{l}", [128, 2 * 16 * TBX], BF16))
            xcc = _st.enter_context(nc.sbuf_tensor(f"c_xc{l}", [128, 2 * 16 * TBX], BF16))
            zc = _st.enter_context(nc.sbuf_tensor(f"c_z{l}", [128, 2 * 16 * TBX], BF16))
            xbc = _st.enter_context(nc.sbuf_tensor(f"c_x{l}", [128, 2 * 8 * TBX], F32))
            cs1 = _st.enter_context(nc.sbuf_tensor(f"c_s1{l}", [128, 2 * TBX], F32))
            cs2 = _st.enter_context(nc.sbuf_tensor(f"c_s2{l}", [128, 2 * TBX], F32))
            cs3 = _st.enter_context(nc.sbuf_tensor(f"c_s3{l}", [128, 2 * TBX], F32))
            outc = _st.enter_context(nc.sbuf_tensor(f"c_out{l}", [128, 2 * 16 * TBX], BF16))
            xoc = _st.enter_context(nc.sbuf_tensor(f"c_xo{l}", [128, 2 * 8 * TBX], F32))
            pDA = _st.enter_context(nc.psum_tensor(f"c_pDA{l}", [128, 8 * TBX], F32))
            pDB = _st.enter_context(nc.psum_tensor(f"c_pDB{l}", [128, 8 * TBX], F32))
            dma("gpsimd", dwB.ap().rearrange("p (k m) -> p k m", m=D),
                dt["down_wT"].ap()[l].rearrange("(k p) m -> p k m", p=128)
                [:, 8:16, :], writes=["dwB"])

            def c_loads(tb):
                t0 = tb * TBX
                par = tb % 2
                o16 = par * 16 * TBX
                o8 = par * 8 * TBX
                dma("scalar",
                    hTb[:, o16:o16 + 16 * TBX]
                    .rearrange("p (c t) -> p c t", t=TBX),
                    sphnT_r[:, :, t0:t0 + TBX], reads=["d:sp_hnT"],
                    writes=["c_hT%d" % par])
                dma("scalar",
                    xcc[:, o16:o16 + 16 * TBX]
                    .rearrange("p (c t) -> p c t", t=TBX),
                    spxc_r[:, :, t0:t0 + TBX], reads=["d:sp_xc"],
                    writes=["c_xc%d" % par])
                dma("scalar",
                    zc[:, o16:o16 + 16 * TBX]
                    .rearrange("p (c t) -> p c t", t=TBX),
                    spz_r[:, :, t0:t0 + TBX], reads=["d:sp_z"],
                    writes=["c_z%d" % par])
                dma("sync",
                    xbc[:, o8:o8 + 8 * TBX]
                    .rearrange("p (c t) -> p c t", t=TBX),
                    spx_r[:, :, t0:t0 + TBX], reads=["d:sp_x"],
                    writes=["c_x%d" % par])
            c_loads(0)
            c_loads(1)
            for tb in range(NTBX):
                t0 = tb * TBX
                par = tb % 2
                o16 = par * 16 * TBX
                o8 = par * 8 * TBX
                pD = pDA if par == 0 else pDB
                pdtok = "c_pDA" if par == 0 else "c_pDB"
                for c in range(16):
                    cp = c % 2
                    co = cp * TBX
                    em.step("vector", lambda e, c=c, o16=o16, par=par, co=co:
                            e.tensor_scalar(
                                cs1[:, co:co + TBX],
                                hTb[:, o16 + c * TBX:o16 + (c + 1) * TBX],
                                onorm_sb[:, l * 16 + c:l * 16 + c + 1],
                                None, ALU.mult),
                            reads=["c_hT%d" % par, "onormc"],
                            writes=["c_s1%d" % cp])
                    em.step("vector", lambda e, c=c, o16=o16, par=par, co=co:
                            e.scalar_tensor_tensor(
                                cs2[:, co:co + TBX],
                                xcc[:, o16 + c * TBX:o16 + (c + 1) * TBX],
                                skip_sb[:, l * 16 + c:l * 16 + c + 1],
                                cs1[:, co:co + TBX], ALU.mult, ALU.add),
                            reads=["c_xc%d" % par, "skipc", "c_s1%d" % cp],
                            writes=["c_s2%d" % cp])
                    em.step("scalar", lambda e, c=c, o16=o16, par=par, co=co:
                            e.activation(
                                cs3[:, co:co + TBX],
                                zc[:, o16 + c * TBX:o16 + (c + 1) * TBX],
                                AF.Silu),
                            reads=["c_z%d" % par], writes=["c_s3%d" % cp])
                    em.step("gpsimd", lambda e, c=c, o16=o16, par=par, co=co:
                            e.tensor_tensor(
                                outc[:, o16 + c * TBX:o16 + (c + 1) * TBX],
                                cs2[:, co:co + TBX], cs3[:, co:co + TBX],
                                ALU.mult),
                            reads=["c_s2%d" % cp, "c_s3%d" % cp],
                            writes=["c_o%d_%d" % (par, c), "c_out%d" % par])

                    dwc = dwA if c < 8 else dwB
                    dwtok = "dwA" if c < 8 else "dwB"

                    def f_dnk(e, c=c, pD=pD, o16=o16, dwc=dwc):
                        i = None
                        for m in range(8):
                            i = e.matmul(pD[:, m * TBX:(m + 1) * TBX],
                                         dwc[:, (c % 8) * D + m * 128:
                                             (c % 8) * D + m * 128 + 128],
                                         outc[:, o16 + c * TBX:
                                              o16 + (c + 1) * TBX],
                                         start=(c == 0 and m % 2 == 0),
                                         stop=(c == 15))
                        return i
                    em.step("tensor", f_dnk,
                            reads=[dwtok, "c_o%d_%d" % (par, c)],
                            writes=[pdtok])
                for m in range(8):
                    em.step("vector", lambda e, m=m, pD=pD, o8=o8, par=par:
                            e.tensor_tensor(
                                xoc[:, o8 + m * TBX:o8 + (m + 1) * TBX],
                                pD[:, m * TBX:(m + 1) * TBX],
                                xbc[:, o8 + m * TBX:o8 + (m + 1) * TBX],
                                ALU.add),
                            reads=[pdtok, "c_x%d" % par],
                            writes=["c_xo%d" % par])
                dma("sync", spx_r[:, :, t0:t0 + TBX],
                    xoc[:, o8:o8 + 8 * TBX]
                    .rearrange("p (c t) -> p c t", t=TBX),
                    reads=["c_xo%d" % par], writes=["d:sp_x"])
                if tb + 2 < NTBX:
                    c_loads(tb + 2)
            em.flush()
        dwst.close()

    if LAT_ON:
        TBL = 256
        NTBL = S // TBL
        with ExitStack() as _st:
            latw = _st.enter_context(nc.sbuf_tensor("l_lat", [128, 8 * D], BF16))
            xbl = _st.enter_context(nc.sbuf_tensor("l_x", [128, 2 * 8 * TBL], F32))
            x0l = _st.enter_context(nc.sbuf_tensor("l_x0", [128, 2 * 8 * TBL], F32))
            xnl = _st.enter_context(nc.sbuf_tensor("l_xn", [128, 2 * 8 * TBL], BF16))
            xb16l = _st.enter_context(nc.sbuf_tensor("l_b16", [128, 2 * TBL], BF16))
            sqbl = _st.enter_context(nc.sbuf_tensor("l_sq", [128, 2 * TBL], BF16))
            rowmul = _st.enter_context(nc.sbuf_tensor("l_rmu", [1, TBL], F32))
            rowrsl = _st.enter_context(nc.sbuf_tensor("l_rrs", [1, TBL], F32))
            rowtl = _st.enter_context(nc.sbuf_tensor("l_rt", [1, TBL], F32))
            muBl = _st.enter_context(nc.sbuf_tensor("l_muB", [128, TBL], F32))
            rstdBl = _st.enter_context(nc.sbuf_tensor("l_rsB", [128, TBL], F32))
            t1l = _st.enter_context(nc.sbuf_tensor("l_t1", [128, TBL], F32))
            xol = _st.enter_context(nc.sbuf_tensor("l_xo", [128, 2 * 8 * TBL], F32))
            pLNl = _st.enter_context(nc.psum_tensor("l_pLN", [1, 2 * TBL], F32))
            pBl = _st.enter_context(nc.psum_tensor("l_pB", [128, TBL], F32))
            pLa = _st.enter_context(nc.psum_tensor("l_pLa", [128, 8 * TBL], F32))
            lb = dict(xb16=xb16l, sqb=sqbl, pLN=pLNl, rowmu=rowmul,
                      rowrs=rowrsl, rowt=rowtl, pB=pBl, muB=muBl,
                      rstdB=rstdBl, t1=t1l, pool_stats=POOL_STATS)
            dma("gpsimd", latw.ap().rearrange("p (k m) -> p k m", m=D),
                dt["latT"].ap().rearrange("(k p) m -> p k m", p=128),
                writes=["latw"])

            def l_loads(tb):
                par = tb % 2
                t0 = tb * TBL
                dma("sync",
                    xbl[:, par * 8 * TBL:(par + 1) * 8 * TBL]
                    .rearrange("p (c t) -> p c t", t=TBL),
                    spx_r[:, :, t0:t0 + TBL], reads=["d:sp_x"],
                    writes=["l_x%d" % par])
                dma("scalar",
                    x0l[:, par * 8 * TBL:(par + 1) * 8 * TBL]
                    .rearrange("p (c t) -> p c t", t=TBL),
                    xT_r[:, :, t0:t0 + TBL], writes=["l_x0%d" % par])

            def l_pieces(tb):
                par = tb % 2
                xv = xbl[:, par * 8 * TBL:(par + 1) * 8 * TBL]
                xnv = xnl[:, par * 8 * TBL:(par + 1) * 8 * TBL]
                return ln_block_q("l", "l_x%d" % par, xv, 8, TBL, xnv,
                                  "l_xn%d" % par,
                                  (lambda c: postw_sb[:, c:c + 1]), lb)

            l_loads(0)
            l_loads(1)
            pcs = l_pieces(0)
            for p in pcs[:11]:
                if p is not None:
                    p()
            norms = pcs[11:]
            for tb in range(NTBL):
                par = tb % 2
                t0 = tb * TBL
                nxt = l_pieces(tb + 1) if tb + 1 < NTBL else []
                nxt_c, nxt_n = nxt[:11], nxt[11:]
                for k in range(8):
                    norms[k]()

                    def f_latk(e, k=k, par=par):
                        i = None
                        for m in range(8):
                            i = e.matmul(pLa[:, m * TBL:(m + 1) * TBL],
                                         latw[:, k * D + m * 128:
                                              k * D + m * 128 + 128],
                                         xnl[:, par * 8 * TBL + k * TBL:
                                             par * 8 * TBL + (k + 1) * TBL],
                                         start=(k == 0 and m % 2 == 0),
                                         stop=(k == 7))
                        return i
                    em.step("tensor", f_latk,
                            reads=["latw", "l_xn%d" % par], writes=["l_pLa"])
                    if k >= 2:
                        for p in nxt_c[:2]:
                            if p is not None:
                                p()
                        nxt_c = nxt_c[2:]
                for p in nxt_c:
                    if p is not None:
                        p()
                for m in range(8):
                    em.step("vector", lambda e, m=m, par=par:
                            e.scalar_tensor_tensor(
                                xol[:, par * 8 * TBL + m * TBL:
                                    par * 8 * TBL + (m + 1) * TBL],
                                pLa[:, m * TBL:(m + 1) * TBL],
                                latb_sb[:, m:m + 1],
                                x0l[:, par * 8 * TBL + m * TBL:
                                    par * 8 * TBL + (m + 1) * TBL],
                                ALU.add, ALU.add),
                            reads=["l_pLa", "l_x0%d" % par, "latb"],
                            writes=["l_xo%d" % par])
                dma("sync", spx1_r[:, :, t0:t0 + TBL],
                    xol[:, par * 8 * TBL:(par + 1) * 8 * TBL]
                    .rearrange("p (c t) -> p c t", t=TBL),
                    reads=["l_xo%d" % par], writes=["d:sp_x1"])
                if tb + 2 < NTBL:
                    l_loads(tb + 2)
                norms = nxt_n
            em.flush()

    if MOE_ON:
        with ExitStack() as _st:
            x1m = _st.enter_context(nc.sbuf_tensor("m_x1", [128, 2 * 8 * TBM], F32))
            xnm = _st.enter_context(nc.sbuf_tensor("m_xn", [128, 8 * TBM], BF16))
            xnf = _st.enter_context(nc.sbuf_tensor("m_xnf", [128, 8 * TBM], F32))
            rt_sb = _st.enter_context(nc.sbuf_tensor("m_rt", [128, 8 * 8], F32))
            xb16m = _st.enter_context(nc.sbuf_tensor("m_b16", [128, 2 * TBM], BF16))
            sqbm = _st.enter_context(nc.sbuf_tensor("m_sq", [128, 2 * TBM], BF16))
            rowmum = _st.enter_context(nc.sbuf_tensor("m_rmu", [1, TBM], F32))
            rowrsm = _st.enter_context(nc.sbuf_tensor("m_rrs", [1, TBM], F32))
            rowtm = _st.enter_context(nc.sbuf_tensor("m_rt2", [1, TBM], F32))
            muBm = _st.enter_context(nc.sbuf_tensor("m_muB", [128, TBM], F32))
            rstdBm = _st.enter_context(nc.sbuf_tensor("m_rsB", [128, TBM], F32))
            t1m = _st.enter_context(nc.sbuf_tensor("m_t1", [128, TBM], F32))
            lgT = _st.enter_context(nc.sbuf_tensor("m_lg", [8, TBM], F32))
            lgtok = _st.enter_context(nc.sbuf_tensor("m_lgt", [128, 2 * 8], F32))
            m1c = _st.enter_context(nc.sbuf_tensor("m_m1", [128, 2], F32))
            m2c = _st.enter_context(nc.sbuf_tensor("m_m2", [128, 2], F32))
            mskc = _st.enter_context(nc.sbuf_tensor("m_msk", [128, 8], F32))
            e1c = _st.enter_context(nc.sbuf_tensor("m_e1", [128, 2 * 8], F32))
            cmb = _st.enter_context(nc.sbuf_tensor("m_cmb", [128, 2 * 8], F32))
            cTf = _st.enter_context(nc.sbuf_tensor("m_cT", [8, TBM], F32))
            pM = _st.enter_context(nc.psum_tensor("m_pM", [128, 512], F32))
            pLNm = _st.enter_context(nc.psum_tensor("m_pLN", [1, 2 * TBM], F32))
            lb = dict(xb16=xb16m, sqb=sqbm, pLN=pLNm, rowmu=rowmum,
                      rowrs=rowrsm, rowt=rowtm, pB=pM, muB=muBm,
                      rstdB=rstdBm, t1=t1m, pbtok="m_pM",
                      pool_stats=POOL_STATS)
            dma("sync", rt_sb.ap().rearrange("p (k m) -> p k m", m=8),
                dt["rT"].ap().rearrange("(k p) m -> p k m", p=128),
                writes=["rt"])

            def r_load(tb):
                par = tb % 2
                t0 = tb * TBM
                dma("scalar",
                    x1m[:, par * 8 * TBM:(par + 1) * 8 * TBM]
                    .rearrange("p (c t) -> p c t", t=TBM),
                    spx1_r[:, :, t0:t0 + TBM], reads=["d:sp_x1"],
                    writes=["m_x1%d" % par])

            def r_pieces(tb):
                par = tb % 2
                xv = x1m[:, par * 8 * TBM:(par + 1) * 8 * TBM]
                return ln_block_q("m", "m_x1%d" % par, xv, 8, TBM, xnf,
                                  "m_xnf", None, lb)

            r_load(0)
            r_load(1)
            pcs = r_pieces(0)
            for p in pcs[:11]:
                if p is not None:
                    p()
            norms = pcs[11:]
            for tb in range(S // TBM):
                t0 = tb * TBM
                nxt = r_pieces(tb + 1) if tb + 1 < S // TBM else []
                nxt_c, nxt_n = nxt[:11], nxt[11:]
                for k in range(8):
                    norms[k]()

                    def f_rk(e, k=k):
                        return e.matmul(pM[0:8, 0:TBM],
                                        rt_sb[:, k * 8:(k + 1) * 8],
                                        xnf[:, k * TBM:(k + 1) * TBM],
                                        start=(k == 0), stop=(k == 7))
                    em.step("tensor", f_rk, reads=["rt", "m_xnf", "m_lg"],
                            writes=["m_pM"])
                    em.step("gpsimd", lambda e, k=k: e.tensor_copy(
                        xnm[:, k * TBM:(k + 1) * TBM],
                        xnf[:, k * TBM:(k + 1) * TBM]),
                        reads=["m_xnf"], writes=["m_xn"])
                dma("sync", spxn_r[:, :, t0:t0 + TBM],
                    xnm.ap().rearrange("p (c t) -> p c t", t=TBM),
                    reads=["m_xn"], writes=["d:sp_xn"])
                if tb + 2 < S // TBM:
                    r_load(tb + 2)
                em.step("scalar", lambda e: e.activation(
                    lgT[:, :], pM[0:8, 0:TBM], AF.Copy), reads=["m_pM"],
                    writes=["m_lg"])
                for sub in range(2):
                    em.step("tensor", lambda e, sub=sub: e.transpose(
                        pM[:, 256:264], lgT[:, sub * 128:(sub + 1) * 128],
                        identf[0:8, 0:8]), reads=["m_lg", "identf"],
                        writes=["m_pM"])
                    em.step("scalar", lambda e, sub=sub: e.activation(
                        lgtok[:, sub * 8:(sub + 1) * 8], pM[:, 256:264],
                        AF.Copy), reads=["m_pM"], writes=["m_lgt"], small=True)
                _top2(em, lgtok, m1c, m2c, mskc, e1c, cmb)
                for sub in range(2):
                    em.step("tensor", lambda e, sub=sub: e.transpose(
                        pM[0:8, 256:384], cmb[:, sub * 8:(sub + 1) * 8],
                        identf[:, :]), reads=["m_cmb", "identf"],
                        writes=["m_pM"], small=True)
                    em.step("scalar", lambda e, sub=sub: e.activation(
                        cTf[0:8, sub * 128:(sub + 1) * 128],
                        pM[0:8, 256:384], AF.Copy), reads=["m_pM"],
                        writes=["m_cT"])
                dma("sync", dt["sp_cmb"].ap()[:, t0:t0 + TBM], cTf[0:8, :],
                    reads=["m_cT"], writes=["d:sp_cmb"])
                for p in nxt_c:
                    if p is not None:
                        p()
                norms = nxt_n
                if DBG_B and tb == 0:
                    dma("sync", dt["sp_dbg2"].ap()[:, 0:16], lgtok[:, :],
                        reads=["m_lgt"], writes=["d:sp_dbg2"])
                    dma("sync", dt["sp_dbg2"].ap()[:, 16:32], cmb[:, :],
                        reads=["m_cmb"], writes=["d:sp_dbg2"])
            em.flush()

        with ExitStack() as _st:
            w1a = _st.enter_context(nc.sbuf_tensor("m_w1a", [128, 8 * FF], BF16))
            w1b = _st.enter_context(nc.sbuf_tensor("m_w1b", [128, 8 * FF], BF16))
            w2a = _st.enter_context(nc.sbuf_tensor("m_w2a", [128, 16 * D], BF16))
            w2b = _st.enter_context(nc.sbuf_tensor("m_w2b", [128, 16 * D], BF16))
            xnm = _st.enter_context(nc.sbuf_tensor("m_xn2", [128, 2 * 8 * TBM], BF16))
            cTb = _st.enter_context(nc.sbuf_tensor("m_cTb", [8, 2 * TBM], BF16))
            crow = _st.enter_context(nc.sbuf_tensor("m_crow", [1, 2 * TBM], F32))
            cB = _st.enter_context(nc.sbuf_tensor("m_cB", [128, 2 * TBM], BF16))
            hmb = _st.enter_context(nc.sbuf_tensor("m_hm", [128, 16 * TBM], BF16))
            hgb = _st.enter_context(nc.sbuf_tensor("m_hg", [128, 2 * TBM], BF16))
            oub = _st.enter_context(nc.sbuf_tensor("m_ou", [128, 8 * TBM], F32))
            pW0 = _st.enter_context(nc.psum_tensor("m_pW0", [128, TBM], F32))
            pW1 = _st.enter_context(nc.psum_tensor("m_pW1", [128, TBM], F32))
            pO0 = _st.enter_context(nc.psum_tensor("m_pO0", [128, 512], F32))
            pO1 = _st.enter_context(nc.psum_tensor("m_pO1", [128, 512], F32))
            pO2 = _st.enter_context(nc.psum_tensor("m_pO2", [128, 512], F32))
            pO3 = _st.enter_context(nc.psum_tensor("m_pO3", [128, 512], F32))
            pM2 = _st.enter_context(nc.psum_tensor("m_pM2", [128, TBM], F32))
            pO = [pO0, pO1, pO2, pO3]
            pOt = ["m_pO0", "m_pO1", "m_pO2", "m_pO3"]
            w1s, w2s = [w1a, w1b], [w2a, w2b]
            w1toks, w2toks = ["m_w1a", "m_w1b"], ["m_w2a", "m_w2b"]
            dma("gpsimd", w1a.ap().rearrange("p (k m) -> p k m", m=FF),
                dt["w1T"].ap()[0].rearrange("(k p) m -> p k m", p=128),
                writes=["m_w1a"])
            dma("gpsimd", w2a.ap().rearrange("p (k m) -> p k m", m=D),
                dt["w2T"].ap()[0].rearrange("(k p) m -> p k m", p=128),
                writes=["m_w2a"])
            for ee in range(EXP):
                wi = ee % 2
                w1c, w2c = w1s[wi], w2s[wi]
                w1tok, w2tok = w1toks[wi], w2toks[wi]
                if ee + 1 < EXP:
                    dma("gpsimd",
                        w1s[(ee + 1) % 2].ap()
                        .rearrange("p (k m) -> p k m", m=FF),
                        dt["w1T"].ap()[ee + 1]
                        .rearrange("(k p) m -> p k m", p=128),
                        writes=[w1toks[(ee + 1) % 2]])
                    dma("gpsimd",
                        w2s[(ee + 1) % 2].ap()
                        .rearrange("p (k m) -> p k m", m=D),
                        dt["w2T"].ap()[ee + 1]
                        .rearrange("(k p) m -> p k m", p=128),
                        writes=[w2toks[(ee + 1) % 2]])
                def m_loads(tb, ee=ee):
                    t0 = tb * TBM
                    par = tb % 2
                    dma("scalar",
                        xnm[:, par * 8 * TBM:(par + 1) * 8 * TBM]
                        .rearrange("p (c t) -> p c t", t=TBM),
                        spxn_r[:, :, t0:t0 + TBM], reads=["d:sp_xn"],
                        writes=["m_xn%d" % par])
                    dma("scalar", crow[:, par * TBM:(par + 1) * TBM],
                        dt["sp_cmb"].ap()[ee:ee + 1, t0:t0 + TBM],
                        reads=["d:sp_cmb"], writes=["m_crow%d" % par])
                    if ee == 0:
                        dma("gpsimd", cTb[:, par * TBM:(par + 1) * TBM],
                            dt["sp_cmb"].ap()[:, t0:t0 + TBM],
                            reads=["d:sp_cmb"], writes=["m_cTb%d" % par])
                m_loads(0)
                for tb in range(S // TBM):
                    t0 = tb * TBM
                    par = tb % 2
                    xnv = xnm[:, par * 8 * TBM:(par + 1) * 8 * TBM]
                    if tb + 1 < S // TBM:
                        m_loads(tb + 1)
                    em.step("tensor", lambda e, par=par: e.matmul(
                        pM2[:, :], ones_r[0:1, :],
                        crow[0:1, par * TBM:(par + 1) * TBM],
                        start=True, stop=True),
                        reads=["m_crow%d" % par, "ones_r", "m_cB%d" % par],
                        writes=["m_pM2"])
                    em.step("scalar", lambda e, par=par: e.activation(
                        cB[:, par * TBM:(par + 1) * TBM], pM2[:, :],
                        AF.Copy), reads=["m_pM2"],
                        writes=["m_cB%d" % par])
                    if ee == 0:
                        def f_b2(e, par=par):
                            i = None
                            for hc in range(8):
                                i = e.matmul(
                                    pO[hc // 2][:, (hc % 2) * TBM:
                                                (hc % 2) * TBM + TBM],
                                    b2_sb[:, hc * 128:(hc + 1) * 128],
                                    cTb[:, par * TBM:(par + 1) * TBM],
                                    start=(hc % 2 == 0),
                                    stop=False)
                            return i
                        em.step("tensor", f_b2,
                                reads=["b2e", "m_cTb%d" % par],
                                writes=pOt)
                    for mc in range(16):
                        pW = pW0 if mc % 2 == 0 else pW1
                        pwt = "m_pW0" if mc % 2 == 0 else "m_pW1"

                        def f_w1(e, mc=mc, pW=pW, w1c=w1c, xnv=xnv):
                            i = None
                            for k in range(8):
                                i = e.matmul(pW[:, :],
                                             w1c[:, k * FF + mc * 128:
                                                 k * FF + mc * 128 + 128],
                                             xnv[:, k * TBM:(k + 1) * TBM],
                                             start=(k == 0),
                                             stop=(k == 7))
                            return i
                        em.step("tensor", f_w1, reads=[w1tok, "m_xn%d" % par],
                                writes=[pwt])
                        hp = mc % 2
                        em.step("scalar", lambda e, mc=mc, pW=pW, ee=ee,
                                hp=hp: e.activation(
                                    hgb[:, hp * TBM:(hp + 1) * TBM],
                                    pW[:, :], AF.Gelu,
                                    bias=b1_sb[:, ee * 16 + mc:
                                               ee * 16 + mc + 1]),
                                reads=[pwt, "b1c"], writes=["m_hg%d" % hp])
                        em.step("vector", lambda e, mc=mc, par=par,
                                hp=hp: e.tensor_tensor(
                                    hmb[:, mc * TBM:(mc + 1) * TBM],
                                    hgb[:, hp * TBM:(hp + 1) * TBM],
                                    cB[:, par * TBM:(par + 1) * TBM],
                                    ALU.mult),
                                reads=["m_hg%d" % hp, "m_cB%d" % par],
                                writes=["m_hm"])
                    for hc in range(8):
                        def f_w2(e, hc=hc, ee=ee, w2c=w2c):
                            i = None
                            for k in range(16):
                                i = e.matmul(
                                    pO[hc // 2][:, (hc % 2) * TBM:
                                                (hc % 2) * TBM + TBM],
                                    w2c[:, k * D + hc * 128:
                                        k * D + hc * 128 + 128],
                                    hmb[:, k * TBM:(k + 1) * TBM],
                                    start=(ee != 0 and k == 0
                                           and hc % 2 == 0),
                                    stop=(k == 15))
                            return i
                        em.step("tensor", f_w2, reads=[w2tok, "m_hm"],
                                writes=[pOt[hc // 2]])
                        em.step("scalar", lambda e, hc=hc: e.activation(
                            oub[:, hc * TBM:(hc + 1) * TBM],
                            pO[hc // 2][:, (hc % 2) * TBM:
                                        (hc % 2) * TBM + TBM], AF.Copy),
                            reads=[pOt[hc // 2]], writes=["m_ou"])
                    if ee == 0:
                        dma("gpsimd",
                            dt["sp_acc"].ap()
                            .rearrange("(c p) t -> p c t", p=128)
                            [:, :, t0:t0 + TBM],
                            oub.ap().rearrange("p (c t) -> p c t", t=TBM),
                            reads=["m_ou"], writes=["d:sp_acc"])
                    else:
                        _dst = (dt["sp_acc"].ap()
                                .rearrange("(c p) t -> p c t", p=128)
                                [:, :, t0:t0 + TBM])
                        _srcap = oub.ap().rearrange("p (c t) -> p c t", t=TBM)
                        em.step("gpsimd",
                                lambda e, o=_dst, i=_srcap: e.dma_start(
                                    o, i, accum_op=ALU.add),
                                reads=["m_ou"], writes=["d:sp_acc"],
                                dma=True)
            em.flush()

        with ExitStack() as _st:
            x1m = _st.enter_context(nc.sbuf_tensor("f_x1", [128, 8 * TBC], F32))
            accm = _st.enter_context(nc.sbuf_tensor("f_ac", [128, 8 * TBC], F32))
            xom = _st.enter_context(nc.sbuf_tensor("f_xo", [128, 8 * TBC], F32))
            spacc_r = dt["sp_acc"].ap().rearrange("(c p) t -> p c t", p=128)
            for tb in range(NTBC):
                t0 = tb * TBC
                dma("sync", x1m.ap().rearrange("p (c t) -> p c t", t=TBC),
                    spx1_r[:, :, t0:t0 + TBC], reads=["d:sp_x1"],
                    writes=["f_x1"])
                dma("sync", accm.ap().rearrange("p (c t) -> p c t", t=TBC),
                    spacc_r[:, :, t0:t0 + TBC], reads=["d:sp_acc"],
                    writes=["f_ac"])
                for c in range(8):
                    em.step("vector", lambda e, c=c: e.tensor_tensor(
                        xom[:, c * TBC:(c + 1) * TBC],
                        x1m[:, c * TBC:(c + 1) * TBC],
                        accm[:, c * TBC:(c + 1) * TBC], ALU.add),
                        reads=["f_x1", "f_ac"], writes=["f_xo"])
                if t0 < SHALF:
                    dstv, dtok, tloc = o_r, "d:o", t0
                else:
                    dstv, dtok, tloc = o2_r, "d:o2", t0 - SHALF
                dma("sync", dstv[:, :, tloc:tloc + TBC],
                    xom.ap().rearrange("p (c t) -> p c t", t=TBC),
                    reads=["f_xo"], writes=[dtok])
            em.flush()

    for name in taps:
        dma("sync", tap_t[name].ap(), dt[name].ap(),
            reads=["d:" + name], writes=["tap:" + name])
    em.finish()
    return nc


def _top2(em, lgtok, m1c, m2c, mskc, e1c, cmb):
    for sub in range(2):
        sl = lgtok[:, sub * 8:(sub + 1) * 8]
        em.step("vector", lambda e, sl=sl, sub=sub: e.reduce_max(
            m1c[:, sub:sub + 1], sl, AX.X), reads=["m_lgt"], writes=["m_m1"], small=True)
        em.step("vector", lambda e, sub=sub: e.tensor_scalar(
            m1c[:, sub:sub + 1], m1c[:, sub:sub + 1], -1.0, None, ALU.mult),
            reads=["m_m1"], writes=["m_m1"], small=True)
        em.step("scalar", lambda e, sl=sl, sub=sub: e.activation(
            e1c[:, sub * 8:(sub + 1) * 8], sl, AF.Exp,
            bias=m1c[:, sub:sub + 1]), reads=["m_lgt", "m_m1"],
            writes=["m_e1"], small=True)
        em.step("vector", lambda e, sl=sl, sub=sub: e.tensor_scalar(
            mskc[:, :], sl, m1c[:, sub:sub + 1], 0.0, ALU.add, ALU.is_equal),
            reads=["m_lgt", "m_m1"], writes=["m_msk"], small=True)
        em.step("vector", lambda e, sl=sl: e.scalar_tensor_tensor(
            mskc[:, :], mskc[:, :], -1e30, sl, ALU.mult, ALU.add),
            reads=["m_msk", "m_lgt"], writes=["m_msk"], small=True)
        em.step("vector", lambda e, sub=sub: e.reduce_max(
            m2c[:, sub:sub + 1], mskc[:, :], AX.X), reads=["m_msk"],
            writes=["m_m2"], small=True)
        em.step("vector", lambda e, sl=sl, sub=sub: e.tensor_scalar(
            mskc[:, :], sl, m2c[:, sub:sub + 1], None, ALU.is_ge),
            reads=["m_lgt", "m_m2"], writes=["m_msk"], small=True)
        em.step("scalar", lambda e, sub=sub: e.activation(
            m2c[:, sub:sub + 1], m2c[:, sub:sub + 1], AF.Exp,
            bias=m1c[:, sub:sub + 1]), reads=["m_m2", "m_m1"],
            writes=["m_m2"], small=True)
        em.step("vector", lambda e, sub=sub: e.tensor_scalar(
            m2c[:, sub:sub + 1], m2c[:, sub:sub + 1], 1.0, None, ALU.add),
            reads=["m_m2"], writes=["m_m2"], small=True)
        em.step("vector", lambda e, sub=sub: e.reciprocal(
            m2c[:, sub:sub + 1], m2c[:, sub:sub + 1]), reads=["m_m2"],
            writes=["m_m2"], small=True)
        em.step("vector", lambda e, sub=sub: e.tensor_tensor(
            cmb[:, sub * 8:(sub + 1) * 8], e1c[:, sub * 8:(sub + 1) * 8],
            mskc[:, :], ALU.mult), reads=["m_e1", "m_msk"],
            writes=["m_cmb"], small=True)
        em.step("vector", lambda e, sub=sub: e.tensor_scalar(
            cmb[:, sub * 8:(sub + 1) * 8], cmb[:, sub * 8:(sub + 1) * 8],
            m2c[:, sub:sub + 1], None, ALU.mult), reads=["m_cmb", "m_m2"],
            writes=["m_cmb"], small=True)


_PROGRAM_CACHE = {}


def kernel(**inputs):
    if "prog" not in _PROGRAM_CACHE:
        _PROGRAM_CACHE["prog"] = build_program()
    nc = _PROGRAM_CACHE["prog"]
    in_maps = _host_prepare(inputs)
    res = run_bass_kernel_spmd(nc, in_maps, core_ids=list(range(NCORES)))
    out = np.zeros((B, S, D), np.float32)
    for core in range(NCORES):
        b = core // 2
        half = core % 2
        name = "o" if half == 0 else "o2"
        shard = np.asarray(res.results[core][name])
        out[b, half * SHALF:(half + 1) * SHALF, :] = shard.T
    return out
